# version 05
# baseline (speedup 1.0000x reference)
import numpy as np
import sys

sys.path.insert(0, "/opt/trn_rl_repo")

import concourse.bass as bass
import concourse.mybir as mybir
from concourse.bass import IndirectOffsetOnAxis
from concourse.bass_utils import run_bass_kernel_spmd
from contextlib import ExitStack
import ml_dtypes

F32 = mybir.dt.float32
BF16 = mybir.dt.bfloat16
I32 = mybir.dt.int32
U32 = mybir.dt.uint32
OP = mybir.AluOpType
AF = mybir.ActivationFunctionType
AX = mybir.AxisListType

DEBUG = []

D = 2048
DIN = 4096
H = 64
HP = 64
NST = 128
KC = 4
CONV = DIN + 2 * NST
NPROJ = CONV + 64
PROJ = 8512
Q = 128
T = 2048
NCH = T // Q
TG = 512
NG = T // TG
E = 32
LAT = 1024
INTER = 1024
SH = 4096
EPS = 1e-5
MASKNEG = -30000.0


def _f32(x):
    return np.ascontiguousarray(np.asarray(x), dtype=np.float32)



class Prog:

    ENGS = ("sync", "vector", "scalar", "gpsimd", "tensor")

    def __init__(self, nc):
        self.nc = nc
        self.steps = {e: [] for e in self.ENGS}
        self.sems = {}
        self.cnt = {}
        for e in self.ENGS:
            self.sems[("c", e)] = nc.alloc_semaphore(name="Sc_" + e)
            self.sems[("q", e)] = nc.alloc_semaphore(name="Sq_" + e)
            self.cnt[("c", e)] = 0
            self.cnt[("q", e)] = 0
        self.writer = {}
        self.readers = {}
        self.water = {e: {} for e in self.ENGS}

    def _need_wait(self, eng, semid, cnt, src_eng):
        if cnt <= 0:
            return None
        if semid[0] == "c" and src_eng == eng and eng == "tensor":
            return None
        if self.water[eng].get(semid, 0) >= cnt:
            return None
        return (semid, cnt)

    def do(self, eng, fn, reads=(), writes=(), dma=False, tag=None):
        if dma:
            writes = list(writes) + ["__q_" + eng]
        excl = [k for k in reads if k.startswith("P$")]
        if excl:
            reads = [k for k in reads if not k.startswith("P$")]
            writes = list(writes) + excl
        waits = {}
        def add_wait(semid, cnt, src_eng):
            r = self._need_wait(eng, semid, cnt, src_eng)
            if r:
                waits[r[0]] = max(waits.get(r[0], 0), r[1])
        for k in reads:
            w = self.writer.get(k)
            if w:
                add_wait(w[0], w[1], w[2])
        for k in writes:
            w = self.writer.get(k)
            if w:
                add_wait(w[0], w[1], w[2])
            for semid, (rcnt, reng) in self.readers.get(k, {}).items():
                add_wait(semid, rcnt, reng)
        semid = ("q", eng) if dma else ("c", eng)
        inc = 16 if dma else 1
        self.cnt[semid] += inc
        after = self.cnt[semid]
        for k in writes:
            self.writer[k] = (semid, after, eng)
            self.readers[k] = {}
        for k in reads:
            self.readers.setdefault(k, {})[semid] = (after, eng)
        for sid, v in waits.items():
            self.water[eng][sid] = max(self.water[eng].get(sid, 0), v)
        sems = self.sems
        mysem = sems[semid]
        wl = list(waits.items())

        def wrapped(engine):
            for sid, v in wl:
                engine.wait_ge(sems[sid], v)
            r = fn(engine)
            r.then_inc(mysem, inc)

        self.steps[eng].append(wrapped)

    def raw(self, eng, fn):
        self.steps[eng].append(fn)

    def barrier(self):
        snap = dict(self.cnt)
        for e in self.ENGS:
            sems = self.sems
            for semid, v in snap.items():
                if v > self.water[e].get(semid, 0):
                    def wrapped(engine, semid=semid, v=v):
                        engine.wait_ge(sems[semid], v)
                    self.steps[e].append(wrapped)
                    self.water[e][semid] = v

    def finalize(self, block):
        for ename in self.ENGS:
            dec = getattr(block, ename)
            steps = self.steps[ename]

            def make(steps):
                def body(eng):
                    for fn in steps:
                        fn(eng)
                return body

            dec(make(steps))


def cdiv(a, b):
    return (a + b - 1) // b



def host_prepare(inputs):
    g = {}
    w_in = _f32(inputs["L0_attn_in_proj_w"])
    perm = np.concatenate([np.arange(DIN, DIN + CONV),
                           np.arange(DIN + CONV, PROJ),
                           np.arange(0, DIN)])
    g["w_inT"] = np.ascontiguousarray(w_in[perm].T)
    g["conv_w"] = _f32(inputs["L0_attn_conv_w"])
    g["conv_b"] = _f32(inputs["L0_attn_conv_b"])[:, None]
    g["dt_bias"] = _f32(inputs["L0_attn_dt_bias"])[:, None]
    g["A_log"] = _f32(inputs["L0_attn_A_log"])[:, None]
    g["norm_w"] = np.ascontiguousarray(
        _f32(inputs["L0_attn_norm_w"]).reshape(32, 128).T)
    g["w_outT"] = np.ascontiguousarray(_f32(inputs["L0_attn_out_proj_w"]).T)
    g["gate_wT"] = np.ascontiguousarray(_f32(inputs["L0_ffn_gate_w"]).T)
    g["gate_bias"] = _f32(inputs["L0_ffn_gate_bias"])[None, :]
    g["fc1T"] = np.ascontiguousarray(_f32(inputs["L0_ffn_fc1_w"]).T)
    g["fc2T"] = np.ascontiguousarray(_f32(inputs["L0_ffn_fc2_w"]).T)
    g["w1T"] = np.ascontiguousarray(_f32(inputs["L0_ffn_w1"]).transpose(0, 2, 1))
    g["w2T"] = np.ascontiguousarray(_f32(inputs["L0_ffn_w2"]).transpose(0, 2, 1))
    g["w_suT"] = np.ascontiguousarray(_f32(inputs["L0_ffn_shared_up_w"]).T)
    g["w_sdT"] = np.ascontiguousarray(_f32(inputs["L0_ffn_shared_down_w"]).T)

    g["eye_bf"] = np.eye(128, dtype=ml_dtypes.bfloat16)
    g["eye_f32"] = np.eye(128, dtype=np.float32)
    g["ones1_f32"] = np.ones((1, 128), dtype=np.float32)
    g["ones3_bf"] = np.ones((3, Q), dtype=ml_dtypes.bfloat16)
    em = np.zeros((128, Q), dtype=np.float32)
    for s in range(1, Q):
        em[s - 1, s] = MASKNEG
    g["emask"] = em.astype(ml_dtypes.bfloat16)
    u = (np.arange(128)[:, None] >= np.arange(Q)[None, :]).astype(np.float32)
    g["urep"] = np.tile(u, (1, 4)).astype(ml_dtypes.bfloat16)
    i64 = np.zeros((64, H * Q), dtype=np.float32)
    for h in range(H):
        i64[h, h * Q:(h + 1) * Q] = 1.0
    g["i64"] = i64.astype(ml_dtypes.bfloat16)
    g["i64d"] = np.concatenate([i64, i64], 0).astype(ml_dtypes.bfloat16)
    ih = np.zeros((64, 32 * Q), dtype=np.float32)
    for hh in range(32):
        ih[hh, hh * Q:(hh + 1) * Q] = 1.0
        ih[32 + hh, hh * Q:(hh + 1) * Q] = 1.0
    g["i64h"] = np.concatenate([ih, ih], 0).astype(ml_dtypes.bfloat16)
    Dfull = np.repeat(_f32(inputs["L0_attn_D"]), HP)
    g["Dcols"] = np.ascontiguousarray(Dfull.reshape(32, 128).T)
    g["nonce"] = np.random.default_rng().standard_normal((1, 4)).astype(np.float32)
    return g


def build_core_inputs(inputs, g, core):
    x = np.asarray(inputs["L0_attn_hidden_states"])
    b, seg = core // 4, core % 4
    lo = seg * T
    xo = x[b, lo:lo + T]
    halo_o = x[b, lo - 3:lo] if seg > 0 else np.zeros((3, D), np.float32)
    if seg == 2:
        flo = 0
    elif seg == 3:
        flo = T
    else:
        flo = lo
    xf = x[b, flo:flo + T]
    halo_f = x[b, flo - 3:flo] if flo > 0 else np.zeros((3, D), np.float32)
    m = {}
    m["x_own"] = _f32(np.concatenate([halo_o, xo], 0))
    m["x_for"] = _f32(np.concatenate([halo_f, xf], 0))
    m["halo_mask"] = np.full((128, 1), 0.0 if seg == 0 else 1.0, np.float32)
    m["halo_mask_f"] = np.full((128, 1), 0.0 if flo == 0 else 1.0, np.float32)
    me = core % 2
    m["widx"] = (np.arange(128, dtype=np.uint32)[:, None] + 128 * me)
    m["ridx"] = (np.arange(128, dtype=np.uint32)[:, None] + 128 * (1 - me))
    rm = np.zeros((128, 4), np.float32)
    rm[:, 0] = 1.0 if seg >= 2 else 0.0
    if seg in (1, 2, 3):
        rm[:, seg] = 1.0
    m["rolemask"] = rm
    m.update(g)
    return m



def build_program(stop_after="all"):
    nc = bass.Bass("TRN2", target_bir_lowering=False)

    def din(name, shape, dt=F32):
        return nc.dram_tensor(name, list(shape), dt, kind="ExternalInput").ap()

    dbg_outs = {}

    def dbg(name, shape, dt=F32):
        if name in DEBUG:
            dbg_outs[name] = nc.dram_tensor(
                "dbg_" + name, list(shape), dt, kind="ExternalOutput").ap()
            return dbg_outs[name]
        return None

    x_own = din("x_own", [T + 3, D])
    x_for = din("x_for", [T + 3, D])
    halo_mask = din("halo_mask", [128, 1])
    halo_mask_f = din("halo_mask_f", [128, 1])
    widx = din("widx", [128, 1], U32)
    ridx = din("ridx", [128, 1], U32)
    w_inT = din("w_inT", [D, PROJ])
    conv_w = din("conv_w", [CONV, KC])
    conv_b = din("conv_b", [CONV, 1])
    dt_bias = din("dt_bias", [H, 1])
    A_log = din("A_log", [H, 1])
    norm_w = din("norm_w", [128, 32])
    w_outT = din("w_outT", [DIN, D])
    gate_wT = din("gate_wT", [D, E])
    gate_bias = din("gate_bias", [1, E])
    fc1T = din("fc1T", [D, LAT])
    fc2T = din("fc2T", [LAT, D])
    w1T = din("w1T", [E, LAT, INTER])
    w2T = din("w2T", [E, INTER, LAT])
    w_suT = din("w_suT", [D, SH])
    w_sdT = din("w_sdT", [SH, D])
    eye_bf = din("eye_bf", [128, 128], BF16)
    eye_f32 = din("eye_f32", [128, 128])
    ones1_f32 = din("ones1_f32", [1, 128])
    ones3_bf = din("ones3_bf", [3, Q], BF16)
    emask = din("emask", [128, Q], BF16)
    urep = din("urep", [128, 4 * Q], BF16)
    i64 = din("i64", [H, H * Q], BF16)
    i64h = din("i64h", [128, 32 * Q], BF16)
    i64d = din("i64d", [128, H * Q], BF16)
    Dcols = din("Dcols", [128, 32])
    nonce = din("nonce", [1, 4])
    rolemask = din("rolemask", [128, 4])

    y_out = nc.dram_tensor("y_out", [T, D], F32, kind="ExternalOutput").ap()

    xn_sp = nc.dram_tensor("xn_sp", [16, 128, T + 3], BF16, kind="Internal").ap()
    partial = nc.dram_tensor("partial", [T, D], F32, kind="Internal").ap()
    ymain_sp = nc.dram_tensor("ymain_sp", [T, DIN], BF16, kind="Internal").ap()
    sgate_sp = nc.dram_tensor("sgate_sp", [DIN, T], BF16, kind="Internal").ap()
    csrows = nc.dram_tensor("csrows", [3, H, T], BF16, kind="Internal").ap()
    exch = nc.dram_tensor("exch", [2 * 128, 2 * 4104], F32, kind="Internal",
                          addr_space="Shared").ap()
    exfl = nc.dram_tensor("exfl", [2 * 128, 4], F32, kind="Internal",
                          addr_space="Shared").ap()

    pr = Prog(nc)
    ES = ExitStack()

    def sb(name, shape, dt=F32):
        return ES.enter_context(nc.sbuf_tensor(name, list(shape), dt))

    def ps(name, shape, dt=F32):
        return ES.enter_context(nc.psum_tensor(name, list(shape), dt))

    with nc.Block() as block:
        cw_res = sb("cw_res", [128, 16, E])
        cwT = sb("cwT", [E, T], BF16)
        consts = {}
        consts["eye_bf"] = sb("c_eye_bf", [128, 128], BF16)
        consts["eye_f32"] = sb("c_eye_f32", [128, 128])
        consts["ones1_f32"] = sb("c_ones1", [1, 128])
        consts["ones1_bf"] = sb("c_ones1b", [1, 128], BF16)
        consts["gwT"] = sb("c_gwT", [128, 16, E])
        consts["gbt"] = sb("c_gbt", [128, E])
        consts["epscol"] = sb("c_eps", [128, 1])
        gb_row = sb("c_gbrow", [1, E])

        with ExitStack() as PH0:
            def ps0(name, shape, dt=F32):
                return PH0.enter_context(nc.psum_tensor(name, list(shape), dt))

            def sb0(name, shape, dt=F32):
                return PH0.enter_context(nc.sbuf_tensor(name, list(shape), dt))

            xtok = [sb0("xtok%d" % i, [128, D]) for i in range(2)]
            gmt = [sb0("gmt%d" % i, [128, 256]) for i in range(2)]
            xnt = [sb0("xnt%d" % i, [128, D]) for i in range(2)]
            xnT = [sb0("xnT%d" % i, [128, 16, 128]) for i in range(2)]
            xnb = [sb0("xnb%d" % i, [128, 16, 128], BF16) for i in range(2)]
            stat = sb0("stat", [128, 8])
            dummy = sb0("p0dummy", [128, D], BF16)
            scr = sb0("p0scr", [128, 64])
            pT = [ps0("P$pT%d" % i, [128, 128]) for i in range(4)]
            pL = ps0("pL", [128, E])
            pC = ps0("pC", [32, 128])
            pB = ps0("pB", [128, E])

            pr.do("vector", lambda e: e.memset(consts["epscol"][:], EPS), writes=["epscol"])
            pr.do("sync", lambda e: e.dma_start(out=consts["eye_bf"][:], in_=eye_bf[:]),
                  writes=["eye_bf"], dma=True)
            pr.do("sync", lambda e: e.dma_start(out=consts["eye_f32"][:], in_=eye_f32[:]),
                  writes=["eye_f32"], dma=True)
            pr.do("sync", lambda e: e.dma_start(out=consts["ones1_f32"][:], in_=ones1_f32[:]),
                  writes=["ones1"], dma=True)
            pr.do("gpsimd", lambda e: e.dma_start(out=consts["ones1_bf"][:], in_=ones1_f32[:]),
                  writes=["ones1bf"], dma=True)
            pr.do("sync", lambda e: e.dma_start(
                out=consts["gwT"][:], in_=gate_wT.rearrange("(j p) e -> p j e", p=128)),
                writes=["gwT"], dma=True)
            pr.do("sync", lambda e: e.dma_start(out=gb_row[:], in_=gate_bias[:]),
                  writes=["gbrow"], dma=True)
            pr.do("tensor", lambda e: e.matmul(out=pB[:], lhsT=consts["ones1_f32"][:],
                                               rhs=gb_row[:], start=True, stop=True),
                  reads=["ones1", "gbrow"], writes=["P$pB"])
            pr.do("vector", lambda e: e.tensor_copy(out=consts["gbt"][:], in_=pB[:]),
                  reads=["P$pB"], writes=["gbt"])

            def ln_tile(i, x_dram, col_off, n_rows, buf):
                xt, xn_t, xT = xtok[buf], xnt[buf], xnT[buf]
                kxt, kxn, kxT = "xtok%d" % buf, "xnt%d" % buf, "xnT%d" % buf
                pr.do("sync", lambda e: e.dma_start(
                    out=xt[0:n_rows, :], in_=x_dram[col_off:col_off + n_rows, :]),
                    writes=[kxt], dma=True)
                pr.do("scalar", lambda e: e.activation(
                    out=dummy[0:n_rows, :], in_=xt[0:n_rows, :], func=AF.Copy,
                    accum_out=stat[0:n_rows, 0:1]), reads=[kxt], writes=["stat", "p0dummy"])
                pr.do("scalar", lambda e: e.activation(
                    out=dummy[0:n_rows, :], in_=xt[0:n_rows, :], func=AF.Square,
                    accum_out=stat[0:n_rows, 1:2]), reads=[kxt], writes=["stat2", "p0dummy"])
                pr.do("vector", lambda e: e.tensor_scalar(
                    out=stat[0:n_rows, 2:3], in0=stat[0:n_rows, 0:1],
                    scalar1=1.0 / D, scalar2=None, op0=OP.mult, op1=OP.bypass),
                    reads=["stat"], writes=["mu"])
                pr.do("vector", lambda e: e.tensor_tensor(
                    out=stat[0:n_rows, 3:4], in0=stat[0:n_rows, 2:3],
                    in1=stat[0:n_rows, 2:3], op=OP.mult),
                    reads=["mu"], writes=["mu2"])
                pr.do("vector", lambda e: e.scalar_tensor_tensor(
                    out=stat[0:n_rows, 4:5], in0=stat[0:n_rows, 1:2],
                    scalar=1.0 / D, in1=stat[0:n_rows, 3:4],
                    op0=OP.mult, op1=OP.subtract),
                    reads=["stat2", "mu2"], writes=["var"])
                pr.do("scalar", lambda e: e.activation(
                    out=stat[0:n_rows, 5:6], in_=stat[0:n_rows, 4:5],
                    func=AF.Sqrt, bias=consts["epscol"][0:n_rows, :]),
                    reads=["var", "epscol"], writes=["sd"])
                pr.do("vector", lambda e: e.reciprocal(
                    out=stat[0:n_rows, 6:7], in_=stat[0:n_rows, 5:6]),
                    reads=["sd"], writes=["rstd"])
                pr.do("vector", lambda e: e.tensor_scalar(
                    out=xn_t[0:n_rows, :], in0=xt[0:n_rows, :],
                    scalar1=stat[0:n_rows, 2:3], scalar2=stat[0:n_rows, 6:7],
                    op0=OP.subtract, op1=OP.mult),
                    reads=[kxt, "mu", "rstd"], writes=[kxn])
                for jj in range(16):
                    pj = pT[jj % 4]
                    kpj = "P$pT%d" % (jj % 4)

                    def tp(e, jj=jj, pj=pj, xn_t=xn_t):
                        return e.transpose(out=pj[:, 0:n_rows],
                                           in_=xn_t[0:n_rows, jj * 128:(jj + 1) * 128],
                                           identity=consts["eye_f32"][0:n_rows, 0:n_rows])
                    pr.do("tensor", tp, reads=[kxn, "eye_f32"], writes=[kpj])
                    def ev_f(e, jj=jj, pj=pj, xT=xT):
                        return e.tensor_copy(out=xT[:, jj, 0:n_rows], in_=pj[:, 0:n_rows])
                    pr.do("vector", ev_f, reads=[kpj], writes=[kxT])
                    def ev_b(e, jj=jj, pj=pj, buf=buf):
                        return e.copy(out=xnb[buf][:, jj, 0:n_rows],
                                      in_=pj[:, 0:n_rows])
                    pr.do("scalar", ev_b, reads=[kpj], writes=["xnb%d" % buf])
                def spx(e, buf=buf, col_off=col_off, n_rows=n_rows):
                    return e.dma_start(
                        out=xn_sp[:, :, col_off:col_off + n_rows]
                        .rearrange("j p t -> p j t"),
                        in_=xnb[buf][:, :, 0:n_rows])
                pr.do("scalar", spx, reads=["xnb%d" % buf], dma=True)

            def routing_tile(i, buf):
                xT = xnT[buf]
                kxT = "xnT%d" % buf
                for k in range(16):
                    def mm(e, k=k, xT=xT):
                        return e.matmul(out=pL[:], lhsT=xT[:, k, :],
                                        rhs=consts["gwT"][:, k, :],
                                        start=(k == 0), stop=(k == 15))
                    pr.do("tensor", mm, reads=[kxT, "gwT"], writes=["P$pL"])
                sc = scr[:, 0:32]
                sbv = scr[:, 32:64]
                pr.do("scalar", lambda e: e.activation(out=sc, in_=pL[:], func=AF.Sigmoid),
                      reads=["P$pL"], writes=["sc"])
                pr.do("vector", lambda e: e.tensor_tensor(out=sbv, in0=sc, in1=consts["gbt"][:],
                                                          op=OP.add),
                      reads=["sc", "gbt"], writes=["sbv"])
                gm = gmt[buf]
                kgm = "gmt%d" % buf
                for ggi in range(4):
                    def m8(e, ggi=ggi, gm=gm):
                        return e.max(out=gm[:, ggi * 8:(ggi + 1) * 8],
                                     in_=sbv[:, ggi * 8:(ggi + 1) * 8])
                    pr.do("vector", m8, reads=["sbv"], writes=[kgm])
                pr.do("vector", lambda e: e.tensor_tensor(
                    out=gm[:, 32:36],
                    in0=gm[:, 0:32].rearrange("p (g k) -> p g k", g=4)[:, :, 0],
                    in1=gm[:, 0:32].rearrange("p (g k) -> p g k", g=4)[:, :, 1],
                    op=OP.add), reads=[kgm], writes=[kgm + "gs"])
                pr.do("vector", lambda e: e.memset(gm[:, 36:40], -1e30), writes=[kgm + "pad"])
                pr.do("vector", lambda e: e.max(out=gm[:, 40:48], in_=gm[:, 32:40]),
                      reads=[kgm + "gs", kgm + "pad"], writes=[kgm + "g8"])
                pr.do("vector", lambda e: e.tensor_scalar(
                    out=gm[:, 48:52], in0=gm[:, 32:36], scalar1=gm[:, 41:42],
                    scalar2=None, op0=OP.is_ge, op1=OP.bypass),
                    reads=[kgm + "gs", kgm + "g8"], writes=[kgm + "keep"])
                pr.do("vector", lambda e: e.tensor_scalar(
                    out=gm[:, 52:56], in0=gm[:, 48:52], scalar1=1e5, scalar2=-1e5,
                    op0=OP.mult, op1=OP.add), reads=[kgm + "keep"], writes=[kgm + "t"])
                pr.do("vector", lambda e: e.tensor_tensor(
                    out=gm[:, 64:96],
                    in0=gm[:, 52:56].unsqueeze(2).broadcast_to((128, 4, 8)),
                    in1=sbv.rearrange("p (g k) -> p g k", g=4), op=OP.add),
                    reads=[kgm + "t", "sbv"], writes=[kgm + "masked"])
                pr.do("vector", lambda e: e.max(out=gm[:, 96:104], in_=gm[:, 64:96]),
                      reads=[kgm + "masked"], writes=[kgm + "m8"])
                pr.do("vector", lambda e: e.tensor_scalar(
                    out=gm[:, 104:136], in0=gm[:, 64:96], scalar1=gm[:, 103:104],
                    scalar2=None, op0=OP.is_ge, op1=OP.bypass),
                    reads=[kgm + "masked", kgm + "m8"], writes=[kgm + "sel"])
                pr.do("vector", lambda e: e.scalar_tensor_tensor(
                    out=gm[:, 136:168], in0=gm[:, 104:136], scalar=1.0, in1=sc,
                    op0=OP.mult, op1=OP.mult, accum_out=gm[:, 168:169]),
                    reads=[kgm + "sel", "sc"], writes=[kgm + "ssel", kgm + "ssum"])
                pr.do("vector", lambda e: e.reciprocal(out=gm[:, 169:170], in_=gm[:, 168:169]),
                      reads=[kgm + "ssum"], writes=[kgm + "rec"])

                def cwf(e, i=i, gm=gm):
                    return e.tensor_scalar(
                        out=cw_res[:, i, :], in0=gm[:, 136:168], scalar1=gm[:, 169:170],
                        scalar2=2.5, op0=OP.mult, op1=OP.mult)
                pr.do("vector", cwf, reads=[kgm + "ssel", kgm + "rec"], writes=["cw%d" % i])
                def cwt(e, i=i):
                    return e.transpose(out=pC[:], in_=cw_res[:, i, :],
                                       identity=consts["eye_f32"][:])
                pr.do("tensor", cwt, reads=["cw%d" % i, "eye_f32"], writes=["P$pC"])

                def cwev(e, i=i):
                    return e.copy(out=cwT[:, i * 128:(i + 1) * 128], in_=pC[:])
                pr.do("scalar", cwev, reads=["P$pC"], writes=["cwT"])

            ln_tile(-1, x_own, 0, 3, 0)
            for i in range(16):
                ln_tile(i, x_own, 3 + i * 128, 128, i % 2)
                routing_tile(i, i % 2)

            od = dbg("xn_fm", [16, 128, T + 3], BF16)
            if od is not None:
                pr.do("sync", lambda e, od=od: e.dma_start(out=od[:], in_=xn_sp[:]),
                      dma=True)
            od = dbg("cw", [128, 16, E])
            if od is not None:
                pr.do("sync", lambda e, od=od: e.dma_start(out=od[:], in_=cw_res[:]),
                      reads=["cw%d" % i for i in range(16)], dma=True)
            od = dbg("cwT", [E, T], BF16)
            if od is not None:
                pr.do("sync", lambda e, od=od: e.dma_start(out=od[:], in_=cwT[:]),
                      reads=["cwT"], dma=True)
            pr.barrier()

        pr.barrier()

        if stop_after == "p0":
            ztile = sb("zz", [128, D])
            pr.do("vector", lambda e: e.memset(ztile[:], 0.0), writes=["zz"])
            for i in range(16):
                def st(e, i=i):
                    return e.dma_start(out=y_out[i * 128:(i + 1) * 128, :], in_=ztile[:])
                pr.do("scalar", st, reads=["zz"], dma=True)
            pr.barrier()
            pr.finalize(block)
            ES.close()
            return nc

        ctx = dict(
            nc=nc, pr=pr, ES=ES, sb=sb, ps=ps, consts=consts,
            xn_sp=xn_sp, cw_res=cw_res, cwT=cwT, dbg=dbg,
            stop_after=stop_after,
            x_own=x_own, x_for=x_for, halo_mask=halo_mask,
            halo_mask_f=halo_mask_f, widx=widx, ridx=ridx, w_inT=w_inT,
            conv_w=conv_w, conv_b=conv_b, dt_bias=dt_bias, A_log=A_log,
            norm_w=norm_w, w_outT=w_outT, fc1T=fc1T, fc2T=fc2T, w1T=w1T,
            w2T=w2T, w_suT=w_suT, w_sdT=w_sdT, ones3_bf=ones3_bf,
            emask=emask, urep=urep, i64=i64, i64h=i64h, i64d=i64d, Dcols=Dcols,
            nonce=nonce, rolemask=rolemask, y_out=y_out, partial=partial, ymain_sp=ymain_sp,
            sgate_sp=sgate_sp, csrows=csrows, exch=exch, exfl=exfl,
        )
        emit_phases(ctx)

        pr.barrier()
        pr.finalize(block)
    ES.close()
    return nc


def kernel(**inputs):
    g = host_prepare(inputs)
    maps = [build_core_inputs(inputs, g, c) for c in range(8)]
    nc = build_program()
    res = run_bass_kernel_spmd(nc, maps, core_ids=list(range(8)))
    x = np.asarray(inputs["L0_attn_hidden_states"])
    out = np.empty((2, 8192, D), np.float32)
    for c in range(8):
        b, seg = c // 4, c % 4
        out[b, seg * T:(seg + 1) * T] = np.asarray(res.results[c]["y_out"])
    return out



def emit_phases(ctx):
    nc = ctx["nc"]; pr = ctx["pr"]; sb = ctx["sb"]; ES = ctx["ES"]
    consts = ctx["consts"]; cwT = ctx["cwT"]
    dbg = ctx["dbg"]; stop_after = ctx["stop_after"]

    xlat_sp = nc.dram_tensor("xlat_sp", [8, 128, T], BF16, kind="Internal").ap()
    ctx["xlat_sp"] = xlat_sp
    w_in_bf = nc.dram_tensor("w_in_bf", [D, PROJ], BF16, kind="Internal").ap()
    ylat_sp = nc.dram_tensor("ylat_sp", [8, 128, T], F32, kind="Internal").ap()
    ctx["w_in_bf"] = w_in_bf
    ctx["ylat_sp"] = ylat_sp

    with ExitStack() as PH:
        def psl(name, shape, dt=F32):
            return PH.enter_context(nc.sbuf_tensor(name, list(shape), dt))
        wtmp = [psl("wc%d" % i, [128, 16, 512], BF16) for i in range(2)]
        for b in range(17):
            buf = wtmp[b % 2]; k = "wc%d" % (b % 2)
            lo = b * 512
            w = min(512, PROJ - lo)
            def ld(e, buf=buf, lo=lo, w=w):
                return e.dma_start(
                    out=buf[:, :, 0:w],
                    in_=ctx["w_inT"][:, lo:lo + w].rearrange("(j p) c -> p j c", p=128))
            pr.do("gpsimd", ld, writes=[k], dma=True)
            def st(e, buf=buf, lo=lo, w=w):
                return e.dma_start(
                    out=w_in_bf[:, lo:lo + w].rearrange("(j p) c -> p j c", p=128),
                    in_=buf[:, :, 0:w])
            pr.do("scalar", st, reads=[k], dma=True)
        pr.barrier()

    with ExitStack() as PH:
        def psl(name, shape, dt=F32):
            return PH.enter_context(nc.sbuf_tensor(name, list(shape), dt))
        def pp(name, shape, dt=F32):
            return PH.enter_context(nc.psum_tensor(name, list(shape), dt))

        x_lat = psl("x_lat", [128, 8, T], BF16)
        xslab = [psl("xsl%d" % i, [128, 16, TG], BF16) for i in range(2)]
        def load_xslab(tg, b):
            def ld(e, tg=tg, b=b):
                return e.dma_start(
                    out=xslab[b][:],
                    in_=ctx["xn_sp"][:, :, 3 + tg * TG:3 + (tg + 1) * TG]
                    .rearrange("j p t -> p j t"))
            pr.do("sync", ld, writes=["xslab%d" % b], dma=True)
        fc1w = psl("fc1w", [128, 16, LAT], BF16)
        for j4 in range(4):
            def ld(e, j4=j4):
                return e.dma_start(
                    out=fc1w[:, j4 * 4:(j4 + 1) * 4, :],
                    in_=ctx["fc1T"].rearrange("(j p) l -> p j l", p=128)[:, j4 * 4:(j4 + 1) * 4, :])
            pr.do("gpsimd", ld, writes=["fc1w"], dma=True)
        pfc = [pp("pfc%d" % i, [128, 512]) for i in range(2)]
        for tg in range(NG):
            b = tg % 2
            load_xslab(tg, b)
            for lt in range(8):
                pj = pfc[lt % 2]; kp = "P$pfc%d" % (lt % 2)
                for k in range(16):
                    def mm(e, k=k, lt=lt, tg=tg, pj=pj, b=b):
                        return e.matmul(out=pj[:], lhsT=fc1w[:, k, lt * 128:(lt + 1) * 128],
                                        rhs=xslab[b][:, k, :],
                                        start=(k == 0), stop=(k == 15))
                    pr.do("tensor", mm, reads=["fc1w", "xslab%d" % b], writes=[kp])
                def ev(e, lt=lt, tg=tg, pj=pj):
                    return e.copy(out=x_lat[:, lt, tg * TG:(tg + 1) * TG], in_=pj[:])
                pr.do("scalar", ev, reads=[kp], writes=["x_lat"])

        for lt in range(8):
            def spx(e, lt=lt):
                return e.dma_start(out=xlat_sp[lt], in_=x_lat[:, lt, :])
            pr.do("scalar", spx, reads=["x_lat"], dma=True)
        wsu = [psl("wsu%d" % i, [128, 16, 128], BF16) for i in range(2)]
        wsd = [psl("wsd%d" % i, [128, 512], BF16) for i in range(2)]
        s1g = psl("s1g", [128, 32, TG], BF16)
        rl32 = psl("rl32", [128, TG], BF16)
        xread = [psl("xrd%d" % i, [128, 512]) for i in range(2)]
        oslab = [psl("osl%d" % i, [128, 512]) for i in range(2)]
        pup = [pp("pup%d" % i, [128, 512]) for i in range(2)]
        pdn = [pp("pdn%d" % i, [128, 512]) for i in range(4)]
        for tg in range(NG):
            bx = tg % 2
            load_xslab(tg, bx)
            for sl in range(32):
                b = sl % 2
                def ldw(e, sl=sl, b=b):
                    return e.dma_start(
                        out=wsu[b][:],
                        in_=ctx["w_suT"][:, sl * 128:(sl + 1) * 128]
                        .rearrange("(j p) c -> p j c", p=128))
                pr.do("gpsimd", ldw, writes=["wsu%d" % b], dma=True)
                pj = pup[b]; kp = "P$pup%d" % b
                for k in range(16):
                    def mm(e, k=k, sl=sl, tg=tg, pj=pj, b=b, bx=bx):
                        return e.matmul(out=pj[:], lhsT=wsu[b][:, k, :],
                                        rhs=xslab[bx][:, k, :],
                                        start=(k == 0), stop=(k == 15))
                    pr.do("tensor", mm, reads=["wsu%d" % b, "xslab%d" % bx], writes=[kp])
                def rel(e, sl=sl, pj=pj):
                    return e.activation(out=rl32[:], in_=pj[:], func=AF.Relu)
                pr.do("scalar", rel, reads=[kp], writes=["rl32"])
                def sq(e, sl=sl):
                    return e.activation(out=s1g[:, sl, :], in_=rl32[:], func=AF.Square)
                pr.do("scalar", sq, reads=["rl32"], writes=["s1g"])
            for dgrp in range(4):
                for ksl in range(32):
                    b = ksl % 2
                    def ldw(e, ksl=ksl, dgrp=dgrp, b=b):
                        return e.dma_start(
                            out=wsd[b][:],
                            in_=ctx["w_sdT"][ksl * 128:(ksl + 1) * 128,
                                             dgrp * 512:(dgrp + 1) * 512])
                    pr.do("gpsimd", ldw, writes=["wsd%d" % b], dma=True)
                    for ts in range(4):
                        pj = pdn[ts]; kp = "P$pdn%d" % ts
                        def mm(e, ksl=ksl, ts=ts, pj=pj, b=b):
                            return e.matmul(out=pj[:],
                                            lhsT=s1g[:, ksl, ts * 128:(ts + 1) * 128],
                                            rhs=wsd[b][:],
                                            start=(ksl == 0), stop=(ksl == 31))
                        pr.do("tensor", mm, reads=["wsd%d" % b, "s1g"], writes=[kp])
                for ts in range(4):
                    b2 = ts % 2
                    row = tg * TG + ts * 128
                    def ldx(e, row=row, dgrp=dgrp, b2=b2):
                        return e.dma_start(out=xread[b2][:],
                                           in_=ctx["x_own"][3 + row:3 + row + 128,
                                                            dgrp * 512:(dgrp + 1) * 512])
                    pr.do("sync", ldx, writes=["xrd%d" % b2], dma=True)
                    pj = pdn[ts]; kp = "P$pdn%d" % ts
                    def ev(e, ts=ts, b2=b2, pj=pj):
                        return e.tensor_tensor(out=oslab[b2][:], in0=pj[:],
                                               in1=xread[b2][:], op=OP.add)
                    pr.do("vector", ev, reads=[kp, "xrd%d" % b2], writes=["osl%d" % b2])
                    def st(e, row=row, dgrp=dgrp, b2=b2):
                        return e.dma_start(
                            out=ctx["partial"][row:row + 128, dgrp * 512:(dgrp + 1) * 512],
                            in_=oslab[b2][:])
                    pr.do("scalar", st, reads=["osl%d" % b2], dma=True)
        pr.barrier()
    pr.barrier()
    if stop_after == "p1":
        _finish_zero_out(ctx)
        return
    emit_attn(ctx, None)


def _finish_zero_out(ctx):
    nc, pr, sb = ctx["nc"], ctx["pr"], ctx["sb"]
    ztile = sb("zz", [128, D])
    pr.do("vector", lambda e: e.memset(ztile[:], 0.0), writes=["zz"])
    for i in range(16):
        def st(e, i=i):
            return e.dma_start(out=ctx["y_out"][i * 128:(i + 1) * 128, :], in_=ztile[:])
        pr.do("scalar", st, reads=["zz"], dma=True)
    pr.barrier()


def emit_attn(ctx, x_lat):
    nc = ctx["nc"]; pr = ctx["pr"]; sb = ctx["sb"]
    consts = ctx["consts"]

    w_in_bf = ctx["w_in_bf"]

    Pinb = sb("Pinb", [128, DIN], BF16)
    ctx["Pinb"] = Pinb
    keep_es = ExitStack()
    ctx["keep_es"] = keep_es
    def sb(name, shape, dt=F32):
        return keep_es.enter_context(nc.sbuf_tensor(name, list(shape), dt))
    flogD = sb("flogD", [64, 1])
    ologD = sb("ologD", [64, 1])
    C_keep = sb("C_keep", [128, T], BF16)
    cs_keep = sb("cs_keep", [64, T])
    dpre_keep = sb("dpre_keep", [64, NCH])
    rstd_res = sb("rstd_res", [128, NCH])
    Ddiag = sb("Ddiag", [128, 32, 128], BF16)
    convw = sb("convw_s", [128, 34, KC])
    convb = sb("convb_s", [128, 34])
    dtb_c = sb("dtb_c", [64, 1])
    nAcol = sb("nAcol", [64, 1])
    ones64 = sb("ones64", [64, Q])
    onesb1 = sb("onesb1", [1, 128], BF16)
    onescol_bf = sb("onescol_bf", [128, 1], BF16)
    zcol = sb("zcol", [64, 1])

    pr.do("sync", lambda e: e.dma_start(
        out=convw[:, 0:34, :],
        in_=ctx["conv_w"].rearrange("(s p) k -> p s k", p=128)), writes=["convw"], dma=True)
    pr.do("sync", lambda e: e.dma_start(
        out=convb[:, 0:34],
        in_=ctx["conv_b"].rearrange("(s p) o -> p s o", p=128)[:, :, 0],
        allow_slow_non_contiguous=True),
        writes=["convb"], dma=True)
    pr.do("sync", lambda e: e.dma_start(out=dtb_c[:], in_=ctx["dt_bias"][:]),
          writes=["dtb"], dma=True)
    pr.do("vector", lambda e: e.memset(ones64[:], 1.0), writes=["ones64"])
    pr.do("vector", lambda e: e.memset(onesb1[:], 1.0), writes=["onesb1"])
    pr.do("vector", lambda e: e.memset(onescol_bf[:], 1.0), writes=["onescol_bf"])
    pr.do("vector", lambda e: e.memset(zcol[:], 0.0), writes=["zcol"])
    tmpa = sb("tmpa", [64, 1])
    pr.do("sync", lambda e: e.dma_start(out=tmpa[:], in_=ctx["A_log"][:]),
          writes=["tmpa"], dma=True)
    pr.do("scalar", lambda e: e.activation(out=nAcol[:], in_=tmpa[:], func=AF.Exp),
          reads=["tmpa"], writes=["nA"])
    pr.do("vector", lambda e: e.tensor_scalar(out=nAcol[:], in0=nAcol[:], scalar1=-1.0,
                                              scalar2=None, op0=OP.mult, op1=OP.bypass),
          reads=["nA"], writes=["nA"])
    Dc = sb("Dc_s", [128, 32])
    pr.do("sync", lambda e: e.dma_start(out=Dc[:], in_=ctx["Dcols"][:]),
          writes=["Dc"], dma=True)
    for j in range(32):
        def mkdiag(e, j=j):
            return e.tensor_scalar(
                out=Ddiag[:, j, :], in0=consts["eye_bf"][:], scalar1=Dc[:, j:j + 1],
                scalar2=None, op0=OP.mult, op1=OP.bypass)
        pr.do("vector", mkdiag, reads=["eye_bf", "Dc"], writes=["Ddiag"])

    att_es = ExitStack()
    P_state = att_es.enter_context(nc.sbuf_tensor("P_state", [128, DIN], F32))
    fO = att_es.enter_context(nc.sbuf_tensor("fO", [128, DIN], F32))
    ctx["att_es"] = att_es
    pr.do("vector", lambda e: e.memset(P_state[:], 0.0), writes=["P_state"])
    emit_front(ctx, own=False, P_state=P_state, fO=fO, flogD=flogD,
               ologD=ologD, C_keep=C_keep, cs_keep=cs_keep,
               dpre_keep=dpre_keep,
               statics=dict(convw=convw, convb=convb, dtb_c=dtb_c, nAcol=nAcol,
                            ones64=ones64, onesb1=onesb1, zcol=zcol,
                            Ddiag=Ddiag, w_in_bf=w_in_bf,
                            onescol_bf=onescol_bf))
    pr.do("vector", lambda e: e.tensor_copy(out=fO[:], in_=P_state[:]),
          reads=["P_state"], writes=["fO"])
    pr.do("vector", lambda e: e.memset(P_state[:], 0.0), writes=["P_state"])
    pr.barrier()
    if ctx["stop_after"] == "pfor":
        ctx["att_es"].close()
        ctx["keep_es"].close()
        _finish_zero_out(ctx)
        return
    emit_front(ctx, own=True, P_state=P_state, fO=fO, flogD=flogD,
               ologD=ologD, C_keep=C_keep, cs_keep=cs_keep,
               dpre_keep=dpre_keep,
               statics=dict(convw=convw, convb=convb, dtb_c=dtb_c, nAcol=nAcol,
                            ones64=ones64, onesb1=onesb1, zcol=zcol,
                            Ddiag=Ddiag, w_in_bf=w_in_bf,
                            onescol_bf=onescol_bf))
    pr.barrier()
    if ctx["stop_after"] == "pown":
        ctx["att_es"].close()
        ctx["keep_es"].close()
        _finish_zero_out(ctx)
        return
    emit_exchange_and_tail(ctx, x_lat, P_state, fO, flogD, ologD,
                           C_keep, cs_keep, dpre_keep, rstd_res,
                           onesb1, onescol_bf, zcol)


def convw_col(st, sl, kk):
    return st["convw"][:, sl, kk:kk + 1]


def emit_front(ctx, own, P_state, fO, flogD, ologD, C_keep, cs_keep,
               dpre_keep, statics):
    nc = ctx["nc"]; pr = ctx["pr"]; sb = ctx["sb"]
    consts = ctx["consts"]
    st = statics
    tagc = "o" if own else "f"

    with ExitStack() as PH:
        def psl(name, shape, dt=F32):
            return PH.enter_context(nc.sbuf_tensor(tagc + name, list(shape), dt))
        def pp(name, shape, dt=F32):
            return PH.enter_context(nc.psum_tensor(tagc + name, list(shape), dt))

        pA = pp("pA", [128, 512]); pB = pp("pB", [128, 512])
        pC = pp("pC", [128, 512]); pD = pp("pD", [128, 512])
        pE = pp("pE", [128, 512]); pF = pp("pF", [128, 512])
        ptp = [pp("ptp%d" % i, [128, 1024], BF16) for i in range(2)]
        KA, KB, KC_, KD, KE, KF = "P$pA", "P$pB", "P$pC", "P$pD", "P$pE", "P$pF"
        KT = ["P$ptp0", "P$ptp1"]

        winb = [psl("winb0", [128, 16, 128], BF16)]
        winb.append(winb[0])
        xc = [psl("xc0", [128, 515])]
        xc.append(xc[0])
        tails = psl("tails", [128, 34, 3])
        xfm = psl("xfm", [128, 32, TG], BF16)
        bc = psl("bc", [128, 2, TG], BF16)
        dtt = psl("dtt", [64, TG])
        sp1 = psl("sp1", [64, TG])
        sp2 = psl("sp2", [64, TG])
        dtTb = psl("dtTb", [64, TG], BF16)
        csT = cs_keep
        runT = psl("runT", [64, 1])
        cdc = psl("cdc", [64, 1])
        cdrow = psl("cdrow", [1, 64], BF16)
        cdrow4 = psl("cdrow4", [1, 512], BF16)
        xdt_tok = psl("xdtok", [128, DIN], BF16)
        dt_tok = psl("dttok", [128, 64], BF16)
        dec_tok = psl("dectok", [128, 64], BF16)
        decfm = psl("decfm", [64, Q], BF16)
        xdg = psl("xdg", [128, 512], BF16)
        B_tok = psl("B_tok", [128, 128], BF16)
        Pbg = psl("Pbg", [128, 512], BF16)
        t1s = psl("t1s", [128, 512])
        xnf = psl("xnf", [128, 16, 515 if not own else 259], BF16)
        Msl = None
        if not own:
            xtok = [psl("xtk0", [128, D])]
            xnt = [psl("xnt0", [128, D], BF16)]
            lstat = psl("lstat", [128, 8])
            ldum = psl("ldum", [128, D], BF16)
        if own:
            csHI = psl("csHI", [64, TG], BF16)
            csMID = psl("csMID", [64, TG], BF16)
            csLOp = psl("csLOp", [64, TG], BF16)
            ncsLO = psl("ncsLO", [64, TG], BF16)
            lhsB = psl("lhsB", [128, TG], BF16)
            nhi = psl("nhi", [64, TG], BF16)
            nmid = psl("nmid", [64, TG], BF16)
            ecsfm = psl("ecsfm", [64, TG], BF16)
            ecs_tok = psl("ecstok", [128, 64], BF16)
            rhsA = psl("rhsA", [3, 16, Q], BF16)
            Mslab = psl("Msl", [128, H * Q // 2], BF16)
            Gs = psl("Gs", [128, 128], BF16)
            yo = psl("yo", [128, 512], BF16)
            yos = psl("yos", [128, 512], BF16)
            ysl2 = psl("ysl2", [128, 512], BF16)
            sgsl = psl("sgsl", [128, 512], BF16)
            i64_s = psl("i64_s", [128, 32 * Q], BF16)
            urep_s = psl("urep_s", [128, 4 * Q], BF16)
            emask_s = psl("emask_s", [128, Q], BF16)
            o3 = psl("ones3_s", [3, Q], BF16)
            pr.do("sync", lambda e: e.dma_start(out=emask_s[:], in_=ctx["emask"][:]),
                  writes=["em"], dma=True)
            pr.do("sync", lambda e: e.dma_start(out=urep_s[:], in_=ctx["urep"][:]),
                  writes=["ur"], dma=True)
            pr.do("sync", lambda e: e.dma_start(out=i64_s[:], in_=ctx["i64h"][:]),
                  writes=["i64"], dma=True)
            pr.do("sync", lambda e: e.dma_start(out=o3[:], in_=ctx["ones3_bf"][:]),
                  writes=["o3"], dma=True)
            st = dict(st)
            st.update(i64=i64_s, urep=urep_s, emask=emask_s, ones3=o3)

        x_dram = ctx["x_own"] if own else ctx["x_for"]
        hmask = ctx["halo_mask"] if own else ctx["halo_mask_f"]
        hmask_s = psl("hms", [128, 1])
        pr.do("sync", lambda e: e.dma_start(out=hmask_s[:], in_=hmask[:]),
              writes=[tagc + "hm"], dma=True)
        pr.do("vector", lambda e: e.memset(runT[:], 0.0), writes=["runT"])

        def foreign_ln(tg):
            rows = [(tg * TG + i * 128, 128, 3 + i * 128, 0) for i in range(4)]
            if tg == 0:
                rows = [(0, 3, 0, 0)] + [(r + 3, n, c, b) for (r, n, c, b) in rows]
            else:
                rows = [(r + 3, n, c, b) for (r, n, c, b) in rows]
            for (row, n, colo, b) in rows:
                kxt, kxn = "fxtk%d" % b, "fxnt%d" % b
                def ld(e, row=row, n=n, b=b):
                    return e.dma_start(out=xtok[b][0:n, :], in_=x_dram[row:row + n, :])
                pr.do("sync", ld, writes=[kxt], dma=True)
                def st1(e, n=n, b=b):
                    return e.activation(out=ldum[0:n, :], in_=xtok[b][0:n, :],
                                        func=AF.Copy, accum_out=lstat[0:n, 0:1])
                pr.do("scalar", st1, reads=[kxt], writes=["fst1", "fldum"])
                def st2(e, n=n, b=b):
                    return e.activation(out=ldum[0:n, :], in_=xtok[b][0:n, :],
                                        func=AF.Square, accum_out=lstat[0:n, 1:2])
                pr.do("scalar", st2, reads=[kxt], writes=["fst2", "fldum"])
                def s3(e, n=n):
                    return e.tensor_scalar(out=lstat[0:n, 2:3], in0=lstat[0:n, 0:1],
                                           scalar1=1.0 / D, scalar2=None,
                                           op0=OP.mult, op1=OP.bypass)
                pr.do("vector", s3, reads=["fst1"], writes=["fmu"])
                def s4(e, n=n):
                    return e.tensor_tensor(out=lstat[0:n, 3:4], in0=lstat[0:n, 2:3],
                                           in1=lstat[0:n, 2:3], op=OP.mult)
                pr.do("vector", s4, reads=["fmu"], writes=["fmu2"])
                def s5(e, n=n):
                    return e.scalar_tensor_tensor(out=lstat[0:n, 4:5], in0=lstat[0:n, 1:2],
                                                  scalar=1.0 / D, in1=lstat[0:n, 3:4],
                                                  op0=OP.mult, op1=OP.subtract)
                pr.do("vector", s5, reads=["fst2", "fmu2"], writes=["fvar"])
                def s6(e, n=n):
                    return e.activation(out=lstat[0:n, 5:6], in_=lstat[0:n, 4:5],
                                        func=AF.Sqrt, bias=consts["epscol"][0:n, :])
                pr.do("scalar", s6, reads=["fvar", "epscol"], writes=["fsd"])
                def s7(e, n=n):
                    return e.reciprocal(out=lstat[0:n, 6:7], in_=lstat[0:n, 5:6])
                pr.do("vector", s7, reads=["fsd"], writes=["frstd"])
                def s8(e, n=n, b=b):
                    return e.tensor_scalar(out=xnt[b][0:n, :], in0=xtok[b][0:n, :],
                                           scalar1=lstat[0:n, 2:3], scalar2=lstat[0:n, 6:7],
                                           op0=OP.subtract, op1=OP.mult)
                pr.do("vector", s8, reads=[kxt, "fmu", "frstd"], writes=[kxn])
                for jj in range(16):
                    pt = ptp[jj % 2]; kp = KT[jj % 2]
                    sl8 = (jj // 2) % 4
                    def tp(e, jj=jj, n=n, b=b, pt=pt, sl8=sl8):
                        return e.transpose(out=pt[:, sl8 * 128:sl8 * 128 + n],
                                           in_=xnt[b][0:n, jj * 128:(jj + 1) * 128],
                                           identity=consts["eye_bf"][0:n, 0:n])
                    pr.do("tensor", tp, reads=[kxn, "eye_bf"], writes=[kp])
                    def ev(e, jj=jj, n=n, colo=colo, pt=pt, sl8=sl8):
                        return e.copy(out=xnf[:, jj, colo:colo + n],
                                      in_=pt[:, sl8 * 128:sl8 * 128 + n])
                    pr.do("scalar", ev, reads=[kp], writes=["xnf"])

        def own_slab(tg, half):
            lo = tg * TG + half * 256
            def ld(e, lo=lo):
                return e.dma_start(
                    out=xnf[:], in_=ctx["xn_sp"][:, :, lo:lo + 259]
                    .rearrange("j p t -> p j t"))
            pr.do("sync", ld, writes=["xnf"], dma=True)

        NSL = 67 if own else 35

        for tg in range(NG):
            import os as _os2
            if not own and _os2.environ.get("KK_NOLN", "0") != "1":
                foreign_ln(tg)
            HALVES = 2 if own else 1
            HW = 256 if own else TG
            for half in range(HALVES):
                if own:
                    own_slab(tg, half)
                if tg == 0 and half == 0 and _os2.environ.get("KK_NOHALO", "0") != "1":
                    for sl in range(34):
                        b = sl % 2
                        def ldw(e, sl=sl, b=b):
                            return e.dma_start(
                                out=winb[b][:],
                                in_=st["w_in_bf"][:, sl * 128:(sl + 1) * 128]
                                .rearrange("(j p) c -> p j c", p=128))
                        pr.do("gpsimd", ldw, writes=[tagc + "winb0"], dma=True)
                        for k in range(16):
                            def mm(e, k=k, sl=sl, b=b):
                                return e.matmul(out=pF[:, 0:3], lhsT=winb[b][:, k, :],
                                                rhs=xnf[:, k, 0:3],
                                                start=(k == 0), stop=(k == 15))
                            pr.do("tensor", mm, reads=[tagc + "winb0", "xnf"],
                                  writes=[KF])
                        def evh(e, sl=sl):
                            return e.tensor_scalar(out=tails[:, sl, :], in0=pF[:, 0:3],
                                                   scalar1=hmask_s[:], scalar2=None,
                                                   op0=OP.mult, op1=OP.bypass)
                        pr.do("vector", evh, reads=[KF, tagc + "hm"],
                              writes=["tails%d" % sl])
                for sl in range(NSL):
                    b = sl % 2
                    pj = pA if sl % 2 == 0 else pB
                    kp = KA if sl % 2 == 0 else KB
                    M_ = 64 if sl == 34 else 128
                    woff = (sl * 128 if sl < 34 else
                            (4352 if sl == 34 else 4416 + (sl - 35) * 128))
                    def ldw(e, woff=woff, M_=M_, b=b):
                        return e.dma_start(
                            out=winb[b][:, :, 0:M_],
                            in_=st["w_in_bf"][:, woff:woff + M_]
                            .rearrange("(j p) c -> p j c", p=128))
                    pr.do("gpsimd", ldw, writes=[tagc + "winb0"], dma=True)
                    for k in range(16):
                        def mm(e, k=k, sl=sl, b=b, pj=pj, M_=M_, half=half, HW=HW):
                            return e.matmul(out=pj[0:M_, 0:HW],
                                            lhsT=winb[b][:, k, 0:M_],
                                            rhs=xnf[:, k, 3:3 + HW],
                                            start=(k == 0), stop=(k == 15))
                        pr.do("tensor", mm, reads=[tagc + "winb0", "xnf"],
                              writes=[kp])
                    ho = half * 256 if own else 0
                    if _os2.environ.get("KK_NOCONV", "0") == "1":
                        continue
                    if sl < 34:
                        xb = xc[0]; kxc = tagc + "xc0"
                        def cp0(e, sl=sl, xb=xb):
                            return e.tensor_copy(out=xb[:, 0:3], in_=tails[:, sl, :])
                        pr.do("vector", cp0, reads=["tails%d" % sl], writes=[kxc])
                        def cp1(e, xb=xb, pj=pj, HW=HW):
                            return e.copy(out=xb[:, 3:3 + HW], in_=pj[:, 0:HW])
                        pr.do("scalar", cp1, reads=[kp], writes=[kxc + "m"])
                        def tl(e, sl=sl, xb=xb, HW=HW):
                            return e.tensor_copy(out=tails[:, sl, :],
                                                 in_=xb[:, HW:HW + 3])
                        pr.do("vector", tl, reads=[kxc + "m"], writes=["tails%d" % sl])
                        def c0(e, sl=sl, xb=xb, HW=HW):
                            return e.tensor_scalar(out=t1s[:, 0:HW],
                                                   in0=xb[:, 0:HW],
                                                   scalar1=convw_col(st, sl, 0),
                                                   scalar2=st["convb"][:, sl:sl + 1],
                                                   op0=OP.mult, op1=OP.add)
                        pr.do("vector", c0, reads=[kxc, kxc + "m", "convw", "convb"],
                              writes=["cacc"])
                        for kk in range(1, 4):
                            def ck(e, sl=sl, kk=kk, xb=xb, HW=HW):
                                return e.scalar_tensor_tensor(
                                    out=t1s[:, 0:HW], in0=xb[:, kk:kk + HW],
                                    scalar=convw_col(st, sl, kk), in1=t1s[:, 0:HW],
                                    op0=OP.mult, op1=OP.add)
                            pr.do("vector", ck, reads=[kxc, kxc + "m", "convw", "cacc"],
                                  writes=["cacc"])
                        if sl < 32:
                            def si(e, sl=sl, ho=ho, HW=HW):
                                return e.activation(out=xfm[:, sl, ho:ho + HW],
                                                    in_=t1s[:, 0:HW], func=AF.Silu)
                            pr.do("scalar", si, reads=["cacc"],
                                  writes=[tagc + "xfm%d" % sl])
                        else:
                            def si(e, sl=sl, ho=ho, HW=HW):
                                return e.activation(out=bc[:, sl - 32, ho:ho + HW],
                                                    in_=t1s[:, 0:HW], func=AF.Silu)
                            pr.do("scalar", si, reads=["cacc"],
                                  writes=[tagc + "bc%d" % (sl - 32)])
                    elif sl == 34:
                        def evd(e, pj=pj, ho=ho, HW=HW):
                            return e.tensor_scalar(out=sp1[:, ho:ho + HW],
                                                   in0=pj[0:64, 0:HW],
                                                   scalar1=st["dtb_c"][:], scalar2=None,
                                                   op0=OP.add, op1=OP.bypass)
                        pr.do("vector", evd, reads=[kp, "dtb"], writes=["sp1"])
                    else:
                        gsl = sl - 35
                        def sg(e, pj=pj, HW=HW):
                            return e.activation(out=sgsl[:, 0:HW], in_=pj[:, 0:HW],
                                                func=AF.Silu)
                        pr.do("scalar", sg, reads=[kp], writes=["sgsl"])
                        def sst(e, gsl=gsl, tg=tg, ho=ho, HW=HW):
                            return e.dma_start(
                                out=ctx["sgate_sp"][gsl * 128:(gsl + 1) * 128,
                                                    tg * TG + ho:tg * TG + ho + HW],
                                in_=sgsl[:, 0:HW])
                        pr.do("scalar", sst, reads=["sgsl"], dma=True)

            import os as _os
            if int(_os.environ.get("KK_FRONT_LVL", "9")) < 1:
                continue
            tsl = slice(tg * TG, (tg + 1) * TG)
            pr.do("scalar", lambda e: e.activation(
                out=sp2[:], in_=sp1[:], func=AF.Abs), reads=["sp1"], writes=["sp2"])
            pr.do("scalar", lambda e: e.activation(
                out=sp2[:], in_=sp2[:], func=AF.Exp, scale=-1.0),
                reads=["sp2"], writes=["sp2"])
            pr.do("scalar", lambda e: e.activation(
                out=sp2[:], in_=sp2[:], func=AF.Ln, bias=1.0),
                reads=["sp2"], writes=["sp2"])
            pr.do("vector", lambda e: e.tensor_scalar(
                out=dtt[:], in0=sp1[:], scalar1=0.0, scalar2=None,
                op0=OP.max, op1=OP.bypass), reads=["sp1"], writes=["dtt"])
            pr.do("vector", lambda e: e.tensor_tensor(
                out=dtt[:], in0=dtt[:], in1=sp2[:], op=OP.add),
                reads=["dtt", "sp2"], writes=["dtt"])
            pr.do("vector", lambda e: e.tensor_copy(
                out=dtTb[:], in_=dtt[:]), reads=["dtt"], writes=["dtTb"])
            pr.do("vector", lambda e: e.tensor_scalar(
                out=sp1[:], in0=dtt[:], scalar1=st["nAcol"][:],
                scalar2=None, op0=OP.mult, op1=OP.bypass),
                reads=["dtt", "nA"], writes=["sp1"])
            for c4 in range(4):
                c = tg * 4 + c4
                csl = slice(c * Q, (c + 1) * Q)
                def scn(e, csl=csl, c4=c4):
                    return e.tensor_tensor_scan(
                        out=csT[:, csl], data0=st["ones64"][:, 0:Q],
                        data1=sp1[:, c4 * Q:(c4 + 1) * Q], initial=0.0,
                        op0=OP.mult, op1=OP.add)
                pr.do("vector", scn, reads=["sp1", "ones64"], writes=["csT"])
            if own:
                pr.do("vector", lambda e, tsl=tsl: e.tensor_copy(
                    out=csHI[:], in_=csT[:, tsl]), reads=["csT"], writes=["csHI"])
                pr.do("vector", lambda e, tsl=tsl: e.tensor_tensor(
                    out=sp2[:], in0=csT[:, tsl], in1=csHI[:], op=OP.subtract),
                    reads=["csT", "csHI"], writes=["sp2"])
                pr.do("vector", lambda e: e.tensor_copy(
                    out=csMID[:], in_=sp2[:]), reads=["sp2"], writes=["csMID"])
                pr.do("vector", lambda e: e.tensor_tensor(
                    out=sp2[:], in0=sp2[:], in1=csMID[:], op=OP.subtract),
                    reads=["sp2", "csMID"], writes=["sp2"])
                pr.do("vector", lambda e: e.tensor_copy(
                    out=csLOp[:], in_=sp2[:]), reads=["sp2"], writes=["csLOp"])
                pr.do("vector", lambda e: e.tensor_scalar(
                    out=ncsLO[:], in0=sp2[:], scalar1=-1.0, scalar2=None,
                    op0=OP.mult, op1=OP.bypass), reads=["sp2"], writes=["ncsLO"])
                for r, buf, kk in ((0, csHI, "csHI"), (1, csMID, "csMID"),
                                   (2, csLOp, "csLOp")):
                    def bn(e, r=r, buf=buf, tsl=tsl):
                        return e.dma_start(out=ctx["csrows"][r, :, tsl], in_=buf[:])
                    pr.do("scalar", bn, reads=[kk], writes=["csrows"], dma=True)
                pr.do("vector", lambda e: e.tensor_scalar(
                    out=nhi[:], in0=csHI[:], scalar1=-1.0, scalar2=None,
                    op0=OP.mult, op1=OP.bypass), reads=["csHI"], writes=["nhi"])
                pr.do("vector", lambda e: e.tensor_scalar(
                    out=nmid[:], in0=csMID[:], scalar1=-1.0, scalar2=None,
                    op0=OP.mult, op1=OP.bypass), reads=["csMID"], writes=["nmid"])
                for (dsta, srca, kk) in ((0, (nhi, 0), "nhi"), (32, (nmid, 0), "nmid"),
                                         (64, (nhi, 32), "nhi"), (96, (nmid, 32), "nmid")):
                    def mv(e, dsta=dsta, srca=srca):
                        buf, s0 = srca
                        return e.dma_start(out=lhsB[dsta:dsta + 32, :],
                                           in_=buf[s0:s0 + 32, :])
                    pr.do("scalar", mv, reads=[kk], writes=["lhsB"], dma=True)
                pr.do("scalar", lambda e, tsl=tsl: e.activation(
                    out=ecsfm[:], in_=csT[:, tsl], func=AF.Exp),
                    reads=["csT"], writes=["ecsfm"])

            import os as _os
            lvl = int(_os.environ.get("KK_FRONT_LVL", "9"))
            if lvl >= 2:
                for c4 in range(4):
                    c = tg * 4 + c4
                    emit_chunk(ctx, own, c, c4, locals())

        dst_logD = ologD if own else flogD
        pr.do("vector", lambda e: e.tensor_copy(out=dst_logD[:], in_=runT[:]),
              reads=["runT"], writes=["ologD" if own else "flogD"])
        if own:
            od = ctx["dbg"]("dtT", [64, T], BF16)
            if od is not None:
                pr.do("sync", lambda e, od=od: e.dma_start(out=od[:], in_=dtTb[:]),
                      reads=["dtTb"], dma=True)
            od = ctx["dbg"]("csT", [64, T])
            if od is not None:
                pr.do("sync", lambda e, od=od: e.dma_start(out=od[:], in_=csT[:]),
                      reads=["csT"], dma=True)
        od = ctx["dbg"]("P_%s" % tagc, [128, DIN])
        if od is not None:
            pr.do("sync", lambda e, od=od: e.dma_start(out=od[:], in_=P_state[:]),
                  reads=["P_state"], dma=True)
        od = ctx["dbg"]("cs_%s" % tagc, [64, T])
        if od is not None:
            pr.do("sync", lambda e, od=od: e.dma_start(out=od[:], in_=csT[:]),
                  reads=["csT"], dma=True)
        od = ctx["dbg"]("bc_%s" % tagc, [128, 2, TG], BF16)
        if od is not None:
            pr.do("sync", lambda e, od=od: e.dma_start(out=od[:], in_=bc[:]),
                  reads=[tagc + "bc0", tagc + "bc1"], dma=True)
        od = ctx["dbg"]("xfm_%s" % tagc, [128, 32, TG], BF16)
        if od is not None:
            pr.do("sync", lambda e, od=od: e.dma_start(out=od[:], in_=xfm[:]),
                  reads=[tagc + "xfm0", tagc + "xfm31"], dma=True)
        od = ctx["dbg"]("xdt_%s" % tagc, [128, DIN], BF16)
        if od is not None:
            pr.do("sync", lambda e, od=od: e.dma_start(out=od[:], in_=xdt_tok[:]),
                  reads=["xdtok"], dma=True)
        pr.barrier()


def emit_chunk(ctx, own, c, c4, L):
    nc = ctx["nc"]; pr = ctx["pr"]; consts = ctx["consts"]
    st = L["st"]; tagc = L["tagc"]
    csT = L["csT"]; dtTb = L["dtTb"]
    xfm = L["xfm"]; bc = L["bc"]
    ptp = L["ptp"]; KT = L["KT"]
    pA, pB, pC, pD, pE, pF = L["pA"], L["pB"], L["pC"], L["pD"], L["pE"], L["pF"]
    KA, KB, KC_, KD, KE, KF = L["KA"], L["KB"], L["KC_"], L["KD"], L["KE"], L["KF"]
    xdt_tok = L["xdt_tok"]; dt_tok = L["dt_tok"]; dec_tok = L["dec_tok"]
    decfm = L["decfm"]; xdg = L["xdg"]; B_tok = L["B_tok"]; Pbg = L["Pbg"]
    t1s = L["t1s"]; runT = L["runT"]; cdc = L["cdc"]; cdrow = L["cdrow"]
    cdrow4 = L["cdrow4"]
    P_state = L["P_state"]; dpre_keep = L["dpre_keep"]
    kxd = "xdtok"
    csl = slice(c * Q, (c + 1) * Q)
    cend = c * Q + Q - 1

    if own:
        pr.do("vector", lambda e: e.tensor_copy(
            out=L["C_keep"][:, csl], in_=bc[:, 1, c4 * Q:(c4 + 1) * Q]),
            reads=[tagc + "bc1"], writes=["C_keep"])
        pr.do("vector", lambda e: e.tensor_copy(
            out=dpre_keep[:, c:c + 1], in_=runT[:]), reads=["runT"], writes=["dpre"])
    pr.do("vector", lambda e: e.tensor_tensor(
        out=runT[:], in0=runT[:], in1=csT[:, cend:cend + 1], op=OP.add),
        reads=["runT", "csT"], writes=["runT"])
    pr.do("scalar", lambda e: e.activation(out=cdc[:], in_=csT[:, cend:cend + 1],
                                           func=AF.Exp), reads=["csT"], writes=["cdc"])
    pr.do("vector", lambda e: e.tensor_copy(out=dec_tok[0:64, 0:1], in_=cdc[:]),
          reads=["cdc"], writes=["cdcb"])
    def cdt(e):
        return e.transpose(out=ptp[0][0:1, 0:64], in_=dec_tok[0:64, 0:1],
                           identity=consts["eye_bf"][0:64, 0:64])
    pr.do("tensor", cdt, reads=["cdcb", "eye_bf"], writes=[KT[0]])
    pr.do("vector", lambda e: e.tensor_copy(out=cdrow[:], in_=ptp[0][0:1, 0:64]),
          reads=[KT[0]], writes=["cdrow"])

    pr.do("scalar", lambda e: e.activation(
        out=decfm[:], in_=csT[:, csl], func=AF.Exp, scale=-1.0,
        bias=csT[:, cend:cend + 1]), reads=["csT"], writes=["decfm"])
    csl4 = slice(c4 * Q, (c4 + 1) * Q)
    def tpd(e):
        return e.transpose(out=ptp[0][0:128, 0:64], in_=dtTb[:, csl4],
                           identity=consts["eye_bf"][0:64, 0:64])
    pr.do("tensor", tpd, reads=["dtTb", "eye_bf"], writes=[KT[0]])
    pr.do("vector", lambda e: e.tensor_copy(out=dt_tok[:], in_=ptp[0][0:128, 0:64]),
          reads=[KT[0]], writes=["dt_tok"])
    def tpe(e):
        return e.transpose(out=ptp[0][0:128, 0:64], in_=decfm[:],
                           identity=consts["eye_bf"][0:64, 0:64])
    pr.do("tensor", tpe, reads=["decfm", "eye_bf"], writes=[KT[0]])
    pr.do("vector", lambda e: e.tensor_copy(out=dec_tok[:], in_=ptp[0][0:128, 0:64]),
          reads=[KT[0]], writes=["dec_tok"])
    if own:
        def tpc(e):
            return e.transpose(out=ptp[0][0:128, 0:64], in_=L["ecsfm"][:, c4 * Q:(c4 + 1) * Q],
                               identity=consts["eye_bf"][0:64, 0:64])
        pr.do("tensor", tpc, reads=["ecsfm", "eye_bf"], writes=[KT[0]])
        pr.do("vector", lambda e: e.tensor_copy(out=L["ecs_tok"][:], in_=ptp[0][0:128, 0:64]),
              reads=[KT[0]], writes=["ecs_tok"])
    def tpb(e):
        return e.transpose(out=ptp[0][0:128, 64:192], in_=bc[:, 0, c4 * Q:(c4 + 1) * Q],
                           identity=consts["eye_bf"][:])
    pr.do("tensor", tpb, reads=[tagc + "bc0", "eye_bf"], writes=[KT[0]])
    pr.do("vector", lambda e: e.tensor_copy(out=B_tok[:], in_=ptp[0][0:128, 64:192]),
          reads=[KT[0]], writes=["B_tok"])

    for half in range(4):
        kp = KT[1]
        for j8 in range(8):
            j = half * 8 + j8
            def tp(e, j=j, j8=j8):
                return e.transpose(out=ptp[1][:, j8 * 128:(j8 + 1) * 128],
                                   in_=xfm[:, j, c4 * Q:(c4 + 1) * Q],
                                   identity=consts["eye_bf"][:])
            pr.do("tensor", tp, reads=[tagc + "xfm%d" % j, "eye_bf"], writes=[kp])
        def ev(e, half=half):
            return e.tensor_tensor(
                out=xdt_tok[:, half * 1024:(half + 1) * 1024]
                .rearrange("p (h q) -> p h q", h=16),
                in0=ptp[1][:].rearrange("p (h q) -> p h q", h=16),
                in1=dt_tok[:, half * 16:(half + 1) * 16]
                .unsqueeze(2).broadcast_to((128, 16, 64)),
                op=OP.mult)
        pr.do("vector", ev, reads=[kp, "dt_tok"], writes=[kxd])

    if own:
        emit_chunk_own(ctx, c, c4, L)

    import os as _os
    if int(_os.environ.get("KK_FRONT_LVL", "9")) < 3:
        return
    for g in range(8):
        pS = pE if g % 2 == 0 else pF
        kS = KE if g % 2 == 0 else KF
        def xdd(e, g=g):
            return e.tensor_tensor(
                out=xdg[:].rearrange("p (h q) -> p h q", h=8),
                in0=xdt_tok[:, g * 512:(g + 1) * 512]
                .rearrange("p (h q) -> p h q", h=8),
                in1=dec_tok[:, g * 8:(g + 1) * 8].unsqueeze(2)
                .broadcast_to((128, 8, 64)),
                op=OP.mult)
        pr.do("vector", xdd, reads=[kxd, "dec_tok"], writes=["xdg"])
        def smm(e, g=g, pS=pS):
            return e.matmul(out=pS[:], lhsT=B_tok[:], rhs=xdg[:],
                            start=True, stop=True)
        pr.do("tensor", smm, reads=["B_tok", "xdg"], writes=[kS])
        pC2 = pF if g % 2 == 0 else pE
        kC2 = KF if g % 2 == 0 else KE
        def cdx(e, g=g):
            return e.tensor_copy(
                out=cdrow4[:].rearrange("o (h q) -> o h q", h=8),
                in_=cdrow[:, g * 8:(g + 1) * 8].unsqueeze(2)
                .broadcast_to((1, 8, 64)))
        pr.do("gpsimd", cdx, reads=["cdrow"], writes=["cdrow4"])
        def cbm(e, g=g, pC2=pC2):
            return e.matmul(out=pC2[:], lhsT=consts["ones1_bf"][:],
                            rhs=cdrow4[:], start=True, stop=True)
        pr.do("tensor", cbm, reads=["ones1bf", "cdrow4"], writes=[kC2])
        def pu1(e, g=g, pC2=pC2):
            return e.tensor_tensor(out=t1s[:], in0=P_state[:, g * 512:(g + 1) * 512],
                                   in1=pC2[:], op=OP.mult)
        pr.do("vector", pu1, reads=["P_state", kC2], writes=["t1s"])
        def pu2(e, g=g, pS=pS):
            return e.tensor_tensor(out=P_state[:, g * 512:(g + 1) * 512],
                                   in0=t1s[:], in1=pS[:], op=OP.add)
        pr.do("vector", pu2, reads=["t1s", kS], writes=["P_state"])




def emit_chunk_own(ctx, c, c4, L):
    nc = ctx["nc"]; pr = ctx["pr"]; consts = ctx["consts"]
    st = L["st"]; tagc = L["tagc"]
    csT = L["csT"]; xfm = L["xfm"]; bc = L["bc"]
    ptp = L["ptp"]; KT = L["KT"]
    pA, pB, pC, pD, pE, pF = L["pA"], L["pB"], L["pC"], L["pD"], L["pE"], L["pF"]
    KA, KB, KC_, KD, KE, KF = L["KA"], L["KB"], L["KC_"], L["KD"], L["KE"], L["KF"]
    xdt_tok = L["xdtok"] if False else L["xdt_tok"]
    Pbg = L["Pbg"]; P_state = L["P_state"]
    kxd = "xdtok"
    csl = slice(c * Q, (c + 1) * Q)
    Msl = L["Mslab"]; kM = "Msl"
    Gs = L["Gs"]
    def gmm(e):
        return e.matmul(out=pF[:, 0:128], lhsT=bc[:, 0, c4 * Q:(c4 + 1) * Q],
                        rhs=bc[:, 1, c4 * Q:(c4 + 1) * Q], start=True, stop=True)
    pr.do("tensor", gmm, reads=[tagc + "bc0", tagc + "bc1"], writes=[KF])
    pr.do("scalar", lambda e: e.copy(out=Gs[:], in_=pF[:, 0:128]),
          reads=[KF], writes=["Gs"])
    csl4 = slice(c4 * Q, (c4 + 1) * Q)
    for dg in range(16):
        if dg % 4 == 0:
            hh0 = (dg // 4) * 16
            def lda(e, hh0=hh0):
                return e.dma_start(out=L["rhsA"][:],
                                   in_=ctx["csrows"][:, hh0:hh0 + 16, csl])
            pr.do("sync", lda, reads=["csrows"], writes=["rhsA"], dma=True)
        pDx = pA if dg % 2 == 0 else pB
        kDx = KA if dg % 2 == 0 else KB
        gsl = slice(dg * 512, (dg + 1) * 512)
        gsl8 = slice((dg % 4) * 512, (dg % 4 + 1) * 512)
        gslH = slice((dg % 8) * 512, (dg % 8 + 1) * 512)
        def mA2(e, dg=dg, pDx=pDx, gsl8=gsl8):
            return e.matmul(out=pDx[:], lhsT=st["ones3"][:],
                            rhs=L["rhsA"][:].rearrange("r h t -> r (h t)")[:, gsl8],
                            start=True, stop=False)
        pr.do("tensor", mA2, reads=["rhsA", "o3"], writes=[kDx])
        hb = 0 if dg < 8 else 64
        lob = 0 if dg < 8 else 32
        def mB1(e, dg=dg, pDx=pDx, gslH=gslH, hb=hb):
            return e.matmul(out=pDx[:], lhsT=L["lhsB"][hb:hb + 64, csl4],
                            rhs=st["i64"][hb:hb + 64, gslH], start=False, stop=False)
        pr.do("tensor", mB1, reads=["lhsB", "i64"], writes=[kDx])
        def mC(e, dg=dg, pDx=pDx, gslH=gslH, lob=lob):
            return e.matmul(out=pDx[:], lhsT=L["ncsLO"][lob:lob + 32, csl4],
                            rhs=st["i64"][lob:lob + 32, gslH], start=False, stop=False)
        pr.do("tensor", mC, reads=["ncsLO", "i64"], writes=[kDx])
        def mD(e, dg=dg, pDx=pDx):
            return e.matmul(out=pDx[:], lhsT=st["emask"][:],
                            rhs=st["urep"][:], start=False, stop=True)
        pr.do("tensor", mD, reads=["em", "ur"], writes=[kDx])
        hof = (dg % 8) * 512
        def ex(e, pDx=pDx, hof=hof):
            return e.activation(out=Msl[:, hof:hof + 512], in_=pDx[:], func=AF.Exp)
        pr.do("scalar", ex, reads=[kDx], writes=[kM + "%d" % (dg % 8)])
        def mul(e, dg=dg, hof=hof):
            return e.tensor_tensor(
                out=Msl[:, hof:hof + 512].rearrange("p (h t) -> p h t", h=4),
                in0=Msl[:, hof:hof + 512].rearrange("p (h t) -> p h t", h=4),
                in1=Gs[:].unsqueeze(1).broadcast_to((128, 4, 128)),
                op=OP.mult)
        pr.do("vector", mul, reads=[kM + "%d" % (dg % 8), "Gs"],
              writes=[kM + "%d" % (dg % 8)])
        if dg % 8 == 7:
            hhalf = dg // 8
            for gy in range(4):
                g = hhalf * 4 + gy
                pY = pC if gy % 2 == 0 else pD
                kY = KC_ if gy % 2 == 0 else KD
                for jj in range(4):
                    j = g * 4 + jj
                    def md(e, j=j, jj=jj, pY=pY):
                        return e.matmul(out=pY[:, jj * 128:(jj + 1) * 128],
                                        lhsT=xfm[:, j, c4 * Q:(c4 + 1) * Q],
                                        rhs=st["Ddiag"][:, j, :],
                                        start=(jj == 0), stop=False)
                    pr.do("tensor", md, reads=[tagc + "xfm%d" % j, "Ddiag"],
                          writes=[kY])
                for hh in range(8):
                    h = g * 8 + hh
                    hloc = h - hhalf * 32
                    def mh(e, h=h, hh=hh, hloc=hloc, pY=pY):
                        return e.matmul(out=pY[:, hh * 64:(hh + 1) * 64],
                                        lhsT=Msl[:, hloc * Q:(hloc + 1) * Q],
                                        rhs=xdt_tok[:, h * 64:(h + 1) * 64],
                                        start=False, stop=(hh == 7))
                    pr.do("tensor", mh,
                          reads=[kM + "%d" % (hloc // 4), kxd], writes=[kY])
                def pbc2(e, g=g):
                    return e.tensor_copy(out=Pbg[:], in_=P_state[:, g * 512:(g + 1) * 512])
                pr.do("vector", pbc2, reads=["P_state"], writes=["Pbg"])
                def mo(e, g=g):
                    return e.matmul(out=pF[:], lhsT=L["C_keep"][:, csl],
                                    rhs=Pbg[:], start=True, stop=True)
                pr.do("tensor", mo, reads=["C_keep", "Pbg"], writes=[KF])
                pr.do("scalar", lambda e: e.copy(out=L["yo"][:], in_=pF[:]),
                      reads=[KF], writes=["yo"])
                def sc(e, g=g):
                    return e.tensor_tensor(
                        out=L["yos"][:].rearrange("p (h q) -> p h q", h=8),
                        in0=L["yo"][:].rearrange("p (h q) -> p h q", h=8),
                        in1=L["ecs_tok"][:, g * 8:(g + 1) * 8]
                        .unsqueeze(2).broadcast_to((128, 8, 64)), op=OP.mult)
                pr.do("gpsimd", sc, reads=["yo", "ecs_tok"], writes=["yos"])
                def fin(e, g=g, pY=pY):
                    return e.tensor_tensor(out=L["ysl2"][:],
                                           in0=pY[:], in1=L["yos"][:], op=OP.add)
                pr.do("vector", fin, reads=[kY, "yos"], writes=["ysl2"])
                def sp(e, g=g, c=c):
                    return e.dma_start(
                        out=ctx["ymain_sp"][c * Q:(c + 1) * Q,
                                            g * 512:(g + 1) * 512],
                        in_=L["ysl2"][:])
                pr.do("scalar", sp, reads=["ysl2"], dma=True)
def emit_exchange_and_tail(ctx, x_lat, P_state, fO, flogD, ologD,
                           C_keep, cs_keep, dpre_keep, rstd_res,
                           onesb1, onescol_bf, zcol):
    nc = ctx["nc"]; pr = ctx["pr"]; sb = ctx["sb"]; consts = ctx["consts"]

    Pinb = ctx["Pinb"]
    with ExitStack() as PH1:
        def psl(name, shape, dt=F32):
            return PH1.enter_context(nc.sbuf_tensor(name, list(shape), dt))
        def pp(name, shape, dt=F32):
            return PH1.enter_context(nc.psum_tensor(name, list(shape), dt))
        pA = pp("x1pA", [128, 512]); pB = pp("x1pB", [128, 512])
        KA, KB = "P$pA", "P$pB"

        sendb = psl("sendb", [128, 4104])
        recvb = psl("recvb", [128, 4104])
        P_in = psl("P_in", [128, DIN])
        Pmid = psl("Pmid", [128, DIN])
        lrow = psl("lrow", [1, 64])
        lrow4 = psl("lrow4", [1, DIN])
        elcol = psl("elcol", [64, 1])
        masks = psl("masks", [128, 4])
        nonce_s = psl("nonce_s", [1, 4])
        nonce128 = psl("nonce128", [128, 4])
        widx_s = psl("widx_s", [128, 1], U32)
        ridx_s = psl("ridx_s", [128, 1], U32)
        t2s = psl("t2s", [128, 512])

        pr.do("sync", lambda e: e.dma_start(out=masks[:], in_=ctx["rolemask"][:]),
              writes=["masks"], dma=True)
        pr.do("sync", lambda e: e.dma_start(out=nonce_s[:], in_=ctx["nonce"][:]),
              writes=["nonce_s"], dma=True)
        pr.do("sync", lambda e: e.dma_start(
            out=nonce128[:], in_=ctx["nonce"][:].broadcast_to((128, 4))),
            writes=["nonce128"], dma=True)
        pr.do("sync", lambda e: e.dma_start(out=widx_s[:], in_=ctx["widx"][:]),
              writes=["widx_s"], dma=True)
        pr.do("sync", lambda e: e.dma_start(out=ridx_s[:], in_=ctx["ridx"][:]),
              writes=["ridx_s"], dma=True)

        pr.do("vector", lambda e: e.memset(sendb[:], 0.0), writes=["sendb"])
        for g in range(8):
            gs = slice(g * 512, (g + 1) * 512)
            def s1(e, gs=gs):
                return e.tensor_tensor(out=sendb[:, gs], in0=fO[:, gs],
                                       in1=P_state[:, gs], op=OP.subtract)
            pr.do("vector", s1, reads=["fO", "P_state"], writes=["sendb"])
            def s2(e, gs=gs):
                return e.scalar_tensor_tensor(out=sendb[:, gs], in0=sendb[:, gs],
                                              scalar=masks[:, 0:1], in1=P_state[:, gs],
                                              op0=OP.mult, op1=OP.add)
            pr.do("vector", s2, reads=["sendb", "masks", "P_state"], writes=["sendb"])
        def sl1(e):
            return e.tensor_tensor(out=sendb[0:64, 4096:4097], in0=flogD[:],
                                   in1=ologD[:], op=OP.subtract)
        pr.do("vector", sl1, reads=["flogD", "ologD"], writes=["sendb"])
        def sl2(e):
            return e.scalar_tensor_tensor(out=sendb[0:64, 4096:4097],
                                          in0=sendb[0:64, 4096:4097],
                                          scalar=masks[0:64, 0:1], in1=ologD[:],
                                          op0=OP.mult, op1=OP.add)
        pr.do("vector", sl2, reads=["sendb", "masks", "ologD"], writes=["sendb"])

        import os as _os
        SIMEXCH = _os.environ.get("KK_SIMEXCH", "0") == "1"
        if SIMEXCH:
            pr.do("vector", lambda e: e.memset(recvb[:], 0.0), writes=["recvb"])

        def exch_round(r, src, dst, kdst):
            if SIMEXCH:
                pr.do("vector", lambda e, dst=dst: e.memset(dst[:], 0.0),
                      writes=[kdst])
                return
            def sc(e, r=r, src=src):
                return e.indirect_dma_start(
                    out=ctx["exch"],
                    out_offset=IndirectOffsetOnAxis(ap=widx_s[:], axis=0),
                    in_=src[:], in_offset=None, element_offset=r * 4104)
            pr.do("gpsimd", sc, reads=["sendb", "widx_s"], dma=True)
            def fl(e, r=r):
                return e.indirect_dma_start(
                    out=ctx["exfl"],
                    out_offset=IndirectOffsetOnAxis(ap=widx_s[:], axis=0),
                    in_=nonce128[:, 2 * r:2 * r + 2], in_offset=None,
                    element_offset=2 * r)
            pr.do("gpsimd", fl, reads=["nonce128", "widx_s"], dma=True)

            def poll(g, r=r):
                rf = g.alloc_register("rf_%d" % r)
                rf1 = g.alloc_register("rf1_%d" % r)
                rn = g.alloc_register("rn_%d" % r)
                cc = g.alloc_register("cc_%d" % r)
                c1 = g.alloc_register("c1_%d" % r)
                g.reg_load(rn, nonce_s[0:1, 2 * r:2 * r + 1].bitcast(I32))
                g.reg_mov(cc, 1)
                with g.While(cc):
                    g.nop(cycle_cnt=20000)
                    g.load(rf, ctx["exfl"][0:1, 2 * r:2 * r + 1].bitcast(I32))
                    g.load(rf1, ctx["exfl"][128:129, 2 * r:2 * r + 1].bitcast(I32))
                    g.reg_alu(cc, rf, rn, OP.bitwise_xor)
                    g.reg_alu(c1, rf1, rn, OP.bitwise_xor)
                    g.reg_alu(cc, cc, c1, OP.bitwise_or)
            import os as _os
            if _os.environ.get("KK_NOPOLL", "0") == "1":
                def poll(g):
                    g.nop(cycle_cnt=2000000)
            pr.raw("gpsimd", poll)
            pr.do("gpsimd", lambda e: e.nop(), writes=["polled%d" % r])
            def ga(e, r=r, dst=dst):
                return e.indirect_dma_start(
                    out=dst[:], out_offset=None, in_=ctx["exch"],
                    in_offset=IndirectOffsetOnAxis(ap=ridx_s[:], axis=0),
                    element_offset=r * 4104)
            pr.do("gpsimd", ga, reads=["polled%d" % r, "ridx_s"], writes=[kdst], dma=True)

        exch_round(0, sendb, recvb, "recvb")

        pr.do("scalar", lambda e: e.activation(out=elcol[:], in_=recvb[0:64, 4096:4097],
                                               func=AF.Exp), reads=["recvb"], writes=["elcol"])
        def elt(e):
            return e.transpose(out=pA[0:1, 0:64], in_=elcol[:],
                               identity=consts["eye_f32"][0:64, 0:64])
        pr.do("tensor", elt, reads=["elcol", "eye_f32"], writes=[KA])
        pr.do("vector", lambda e: e.tensor_copy(out=lrow[:], in_=pA[0:1, 0:64]),
              reads=[KA], writes=["lrow"])
        def eld(e):
            return e.tensor_copy(
                out=lrow4[:].rearrange("o (h q) -> o h q", h=64),
                in_=lrow[:].unsqueeze(2).broadcast_to((1, 64, 64)))
        pr.do("gpsimd", eld, reads=["lrow"], writes=["lrow4"])
        for g in range(8):
            gs = slice(g * 512, (g + 1) * 512)
            pX = pA if g % 2 == 0 else pB
            kX = KA if g % 2 == 0 else KB
            def bm(e, g=g, pX=pX):
                return e.matmul(out=pX[:], lhsT=consts["ones1_f32"][:],
                                rhs=lrow4[:, g * 512:(g + 1) * 512],
                                start=True, stop=True)
            pr.do("tensor", bm, reads=["ones1", "lrow4"], writes=[kX])
            def pm(e, gs=gs, pX=pX):
                return e.tensor_tensor(out=Pmid[:, gs], in0=fO[:, gs], in1=pX[:],
                                       op=OP.mult)
            pr.do("vector", pm, reads=["fO", kX], writes=["Pmid"])
            def pm2(e, gs=gs):
                return e.tensor_tensor(out=Pmid[:, gs], in0=Pmid[:, gs],
                                       in1=recvb[:, gs], op=OP.add)
            pr.do("vector", pm2, reads=["Pmid", "recvb"], writes=["Pmid"])

        pr.do("scalar", lambda e: e.activation(out=elcol[:], in_=ologD[:],
                                               func=AF.Exp), reads=["ologD"], writes=["elcol"])
        pr.do("tensor", elt, reads=["elcol", "eye_f32"], writes=[KA])
        pr.do("vector", lambda e: e.tensor_copy(out=lrow[:], in_=pA[0:1, 0:64]),
              reads=[KA], writes=["lrow"])
        pr.do("gpsimd", eld, reads=["lrow"], writes=["lrow4"])
        for g in range(8):
            gs = slice(g * 512, (g + 1) * 512)
            pX = pA if g % 2 == 0 else pB
            kX = KA if g % 2 == 0 else KB
            def bm(e, g=g, pX=pX):
                return e.matmul(out=pX[:], lhsT=consts["ones1_f32"][:],
                                rhs=lrow4[:, g * 512:(g + 1) * 512],
                                start=True, stop=True)
            pr.do("tensor", bm, reads=["ones1", "lrow4"], writes=[kX])
            def sb1(e, gs=gs, pX=pX):
                return e.tensor_tensor(out=sendb[:, gs], in0=Pmid[:, gs], in1=pX[:],
                                       op=OP.mult)
            pr.do("vector", sb1, reads=["Pmid", kX], writes=["sendb"])
            def sb2(e, gs=gs):
                return e.tensor_tensor(out=sendb[:, gs], in0=sendb[:, gs],
                                       in1=P_state[:, gs], op=OP.add)
            pr.do("vector", sb2, reads=["sendb", "P_state"], writes=["sendb"])

        exch_round(1, sendb, recvb, "recvb2")

        def ga1(e):
            return e.indirect_dma_start(
                out=Pmid[:] if False else recvb[:], out_offset=None, in_=ctx["exch"],
                in_offset=IndirectOffsetOnAxis(ap=ridx_s[:], axis=0),
                element_offset=0)
        recvb2 = psl("recvb2", [128, 4104])
        def ga2(e):
            return e.indirect_dma_start(
                out=recvb2[:], out_offset=None, in_=ctx["exch"],
                in_offset=IndirectOffsetOnAxis(ap=ridx_s[:], axis=0),
                element_offset=0 * 4104)
        if SIMEXCH:
            pr.do("vector", lambda e: e.memset(recvb2[:], 0.0), writes=["recvbA"])
        else:
            pr.do("gpsimd", ga2, reads=["polled0", "ridx_s"], writes=["recvbA"], dma=True)
        for g in range(8):
            gs = slice(g * 512, (g + 1) * 512)
            def f1(e, gs=gs):
                return e.tensor_scalar(out=P_in[:, gs], in0=recvb2[:, gs],
                                       scalar1=masks[:, 1:2], scalar2=None,
                                       op0=OP.mult, op1=OP.bypass)
            pr.do("vector", f1, reads=["recvbA", "masks"], writes=["P_in"])
            def f2(e, gs=gs):
                return e.scalar_tensor_tensor(out=P_in[:, gs], in0=Pmid[:, gs],
                                              scalar=masks[:, 2:3], in1=P_in[:, gs],
                                              op0=OP.mult, op1=OP.add)
            pr.do("vector", f2, reads=["Pmid", "masks", "P_in"], writes=["P_in"])
            def f3(e, gs=gs):
                return e.scalar_tensor_tensor(out=P_in[:, gs], in0=recvb[:, gs],
                                              scalar=masks[:, 3:4], in1=P_in[:, gs],
                                              op0=OP.mult, op1=OP.add)
            pr.do("vector", f3, reads=["recvb2" if False else "recvb", "masks", "P_in"],
                  writes=["P_in"])
        pr.do("vector", lambda e: e.tensor_copy(out=Pinb[:], in_=P_in[:]),
              reads=["P_in"], writes=["Pinb"])

        pr.barrier()
        if ctx["stop_after"] == "pexch":
            od = ctx["dbg"]("P_in", [128, DIN])
            if od is not None:
                pr.do("sync", lambda e, od=od: e.dma_start(out=od[:], in_=P_in[:]),
                      reads=["P_in"], dma=True)
            pr.barrier()
    if ctx["stop_after"] == "pexch":
        ctx["att_es"].close()
        ctx["keep_es"].close()
        _finish_zero_out(ctx)
        return
    with ExitStack() as PH1X:
        def psl(name, shape, dt=F32):
            return PH1X.enter_context(nc.sbuf_tensor(name, list(shape), dt))
        def pp(name, shape, dt=F32):
            return PH1X.enter_context(nc.psum_tensor(name, list(shape), dt))

    ctx["att_es"].close()
    with ExitStack() as PH:
        def psl(name, shape, dt=F32):
            return PH.enter_context(nc.sbuf_tensor(name, list(shape), dt))
        def pp(name, shape, dt=F32):
            return PH.enter_context(nc.psum_tensor(name, list(shape), dt))
        hgn = psl("hgn", [128, 32, T], BF16)
        pA = pp("xpA", [128, 512]); pB = pp("xpB", [128, 512])
        pC = pp("xpC", [128, 512]); pD = pp("xpD", [128, 512])
        ptp = [pp("xptp%d" % i, [128, 1024], BF16) for i in range(2)]
        pvar = pp("xpvar", [128, 1])
        KA, KB, KC_, KD = "P$xpA", "P$xpB", "P$xpC", "P$xpD"
        KT = ["P$xptp0", "P$xptp1"]
        CORR = ExitStack()
        def pslc(name, shape, dt=F32):
            return CORR.enter_context(nc.sbuf_tensor(name, list(shape), dt))
        ecs2 = pslc("ecs2", [64, Q], BF16)
        ecs2t = pslc("ecs2t", [128, 64], BF16)
        yo2 = pslc("yo2", [128, 512], BF16)
        yos2 = pslc("yos2", [128, 512], BF16)
        ysl = [pslc("ysl0", [128, DIN], BF16)]
        ysl.append(ysl[0])
        ymr = [pslc("ymr0", [128, DIN], BF16)]
        ymr.append(ymr[0])
        sgr = pslc("sgr", [128, 32, Q], BF16)
        sqv = pslc("sqv", [128, 32, Q], BF16)
        for c in range(NCH):
            csl = slice(c * Q, (c + 1) * Q)
            cb = c % 2
            def e2(e, c=c, csl=csl):
                return e.activation(out=ecs2[:], in_=cs_keep[:, csl], func=AF.Exp,
                                    bias=dpre_keep[:, c:c + 1])
            pr.do("scalar", e2, reads=["cs_keep", "dpre"], writes=["ecs2"])
            def e2t(e):
                return e.transpose(out=ptp[0][0:128, 0:64], in_=ecs2[:],
                                   identity=consts["eye_bf"][0:64, 0:64])
            pr.do("tensor", e2t, reads=["ecs2", "eye_bf"], writes=[KT[0]])
            pr.do("vector", lambda e: e.tensor_copy(out=ecs2t[:], in_=ptp[0][0:128, 0:64]),
                  reads=[KT[0]], writes=["ecs2t"])
            def ldy(e, c=c, cb=cb):
                return e.dma_start(out=ymr[cb][:], in_=ctx["ymain_sp"][c * Q:(c + 1) * Q, :])
            pr.do("sync", ldy, writes=["ymr0"], dma=True)
            for g in range(8):
                gs = slice(g * 512, (g + 1) * 512)
                pX = pA if g % 2 == 0 else pB
                kX = KA if g % 2 == 0 else KB
                def cm(e, csl=csl, gs=gs, pX=pX):
                    return e.matmul(out=pX[:], lhsT=C_keep[:, csl], rhs=Pinb[:, gs],
                                    start=True, stop=True)
                pr.do("tensor", cm, reads=["C_keep", "Pinb"], writes=[kX])
                pr.do("scalar", lambda e, pX=pX: e.copy(out=yo2[:], in_=pX[:]),
                      reads=[kX], writes=["yo2"])
                def sc2(e, g=g):
                    return e.tensor_tensor(
                        out=yos2[:].rearrange("p (h q) -> p h q", h=8),
                        in0=yo2[:].rearrange("p (h q) -> p h q", h=8),
                        in1=ecs2t[:, g * 8:(g + 1) * 8].unsqueeze(2)
                        .broadcast_to((128, 8, 64)), op=OP.mult)
                pr.do("gpsimd", sc2, reads=["yo2", "ecs2t"], writes=["yos2"])
                def fin(e, gs=gs, cb=cb):
                    return e.tensor_tensor(out=ysl[cb][:, gs], in0=ymr[cb][:, gs],
                                           in1=yos2[:], op=OP.add)
                pr.do("vector", fin, reads=["ymr0", "yos2"], writes=["ysl0"])
            def ldsg(e, csl=csl):
                return e.dma_start(
                    out=sgr[:], in_=ctx["sgate_sp"][:, csl]
                    .rearrange("(j p) t -> p j t", p=128))
            pr.do("sync", ldsg, writes=["sgr"], dma=True)
            for half in range(4):
                for j8 in range(8):
                    j = half * 8 + j8
                    def tp(e, j=j, j8=j8, cb=cb):
                        return e.transpose(out=ptp[1][:, j8 * 128:(j8 + 1) * 128],
                                           in_=ysl[cb][:, j * 128:(j + 1) * 128]
                                           .rearrange("p q -> p q"),
                                           identity=consts["eye_bf"][:])
                    pr.do("tensor", tp, reads=["ysl0", "eye_bf"], writes=[KT[1]])
                def hg(e, half=half, csl=csl):
                    return e.tensor_tensor(
                        out=hgn[:, half * 8:(half + 1) * 8, csl],
                        in0=ptp[1][:].rearrange("p (j q) -> p j q", j=8),
                        in1=sgr[:, half * 8:(half + 1) * 8, :], op=OP.mult)
                pr.do("vector", hg, reads=[KT[1], "sgr"], writes=["hgn"])
            def sq(e, csl=csl):
                return e.activation(out=sqv[:], in_=hgn[:, :, csl], func=AF.Square)
            pr.do("scalar", sq, reads=["hgn"], writes=["sqv"])
            for j in range(32):
                def vm(e, j=j):
                    return e.matmul(out=pvar[:], lhsT=sqv[:, j, :],
                                    rhs=onescol_bf[:], start=(j == 0), stop=(j == 31))
                pr.do("tensor", vm, reads=["sqv", "onescol_bf"], writes=["P$xpvar"])
            def vr(e, c=c):
                return e.tensor_scalar(out=rstd_res[:, c:c + 1], in0=pvar[:],
                                       scalar1=1.0 / DIN, scalar2=None,
                                       op0=OP.mult, op1=OP.bypass)
            pr.do("vector", vr, reads=["P$xpvar"], writes=["rstd1"])
            def vs(e, c=c):
                return e.activation(out=rstd_res[:, c:c + 1], in_=rstd_res[:, c:c + 1],
                                    func=AF.Sqrt, bias=consts["epscol"][:])
            pr.do("scalar", vs, reads=["rstd1", "epscol"], writes=["rstd2"])
            def vc(e, c=c):
                return e.reciprocal(out=rstd_res[:, c:c + 1], in_=rstd_res[:, c:c + 1])
            pr.do("vector", vc, reads=["rstd2"], writes=["rstd_res"])

        pr.barrier()
        if ctx["stop_after"] == "pcorr":
            _finish_zero_out(ctx)
            return

        pr.barrier()
        CORR.close()
        wos = psl("wos", [128, 32, 512], BF16)
        nws = psl("nws", [128, 32])
        osl = [psl("xosl%d" % i, [128, 512]) for i in range(2)]
        pr.do("sync", lambda e: e.dma_start(out=nws[:], in_=ctx["norm_w"][:]),
              writes=["nws"], dma=True)
        for dg in range(4):
            def ldw(e, dg=dg):
                return e.dma_start(
                    out=wos[:], in_=ctx["w_outT"][:, dg * 512:(dg + 1) * 512]
                    .rearrange("(j p) c -> p j c", p=128))
            pr.do("gpsimd", ldw, writes=["wos"], dma=True)
            for j in range(32):
                def nsc(e, j=j):
                    return e.tensor_scalar(out=wos[:, j, :], in0=wos[:, j, :],
                                           scalar1=nws[:, j:j + 1], scalar2=None,
                                           op0=OP.mult, op1=OP.bypass)
                pr.do("vector", nsc, reads=["wos", "nws"], writes=["wos"])
            for t in range(16):
                pX = pC if t % 2 == 0 else pD
                kX = KC_ if t % 2 == 0 else KD
                for j in range(32):
                    def mm(e, j=j, t=t, pX=pX):
                        return e.matmul(out=pX[:], lhsT=hgn[:, j, t * 128:(t + 1) * 128],
                                        rhs=wos[:, j, :], start=(j == 0), stop=(j == 31))
                    pr.do("tensor", mm, reads=["hgn", "wos"], writes=[kX])
                b2 = t % 2
                def ev(e, t=t, pX=pX, b2=b2):
                    return e.tensor_scalar(out=osl[b2][:], in0=pX[:],
                                           scalar1=rstd_res[:, t // 1][:, t:t + 1]
                                           if False else rstd_res[:, t:t + 1],
                                           scalar2=None, op0=OP.mult, op1=OP.bypass)
                pr.do("vector", ev, reads=[kX, "rstd_res"], writes=["xosl%d" % b2])
                def ac(e, t=t, dg=dg, b2=b2):
                    return e.dma_start(
                        out=ctx["partial"][t * 128:(t + 1) * 128,
                                           dg * 512:(dg + 1) * 512],
                        in_=osl[b2][:], accum_op=OP.add)
                pr.do("gpsimd", ac, reads=["xosl%d" % b2], dma=True)
        pr.barrier()

    pr.barrier()
    ctx["keep_es"].close()
    if ctx["stop_after"] == "pattn":
        _finish_zero_out(ctx)
        return
    emit_experts_and_finale(ctx, x_lat)


def emit_experts_and_finale(ctx, x_lat):
    nc = ctx["nc"]; pr = ctx["pr"]; consts = ctx["consts"]; cwT = ctx["cwT"]

    cwrows = nc.dram_tensor("cwrows", [E, T], BF16, kind="Internal").ap()
    for i in range(4):
        def st(e, i=i):
            return e.dma_start(
                out=cwrows[i * 8:(i + 1) * 8, :],
                in_=cwT[i * 8:(i + 1) * 8, :])
        pr.do("scalar", st, reads=["cwT"], writes=["cwrows"], dma=True)

    with ExitStack() as PH:
        def psl(name, shape, dt=F32):
            return PH.enter_context(nc.sbuf_tensor(name, list(shape), dt))
        def pp(name, shape, dt=F32):
            return PH.enter_context(nc.psum_tensor(name, list(shape), dt))
        ph = [pp("eph%d" % i, [128, 512]) for i in range(2)]
        pz = pp("epz", [128, 2048])
        pbc = pp("epbc", [128, 512])
        KH = ["P$eph0", "P$eph1"]
        KZ = "P$epz"
        KBC = "P$epbc"

        x_lat = psl("x_lat9", [128, 8, T], BF16)
        for lt in range(8):
            def lx(e, lt=lt):
                return e.dma_start(out=x_lat[:, lt, :], in_=ctx["xlat_sp"][lt])
            pr.do("sync", lx, writes=["x_lat"], dma=True)
        w1b = [psl("w1b%d" % i, [128, 8, INTER], BF16) for i in range(2)]
        w2b = [psl("w2b%d" % i, [128, 8, LAT], BF16) for i in range(2)]
        hp = [psl("hp%d" % i, [128, 8, T], BF16) for i in range(2)]
        rl = psl("erl", [128, 512], BF16)
        bcw = [psl("ebcw0", [128, T], BF16)]
        bcw.append(bcw[0])
        zsl = psl("ezsl", [128, 2048])

        cwr_s = psl("cwr_s", [1, T], BF16)

        def hprime(e_i, wb):
            def lcw(e, e_i=e_i):
                return e.dma_start(out=cwr_s[:],
                                   in_=cwrows[e_i:e_i + 1, :])
            pr.do("sync", lcw, reads=["cwrows"], writes=["cwr_s"], dma=True)
            for j4 in range(2):
                def lw1(e, e_i=e_i, wb=wb, j4=j4):
                    return e.dma_start(
                        out=w1b[wb][:, j4 * 4:(j4 + 1) * 4, :],
                        in_=ctx["w1T"][e_i].rearrange("(j p) i -> p j i", p=128)
                        [:, j4 * 4:(j4 + 1) * 4, :])
                pr.do("gpsimd", lw1, writes=["w1b%d" % wb], dma=True)
                def lw2(e, e_i=e_i, wb=wb, j4=j4):
                    return e.dma_start(
                        out=w2b[wb][:, j4 * 4:(j4 + 1) * 4, :],
                        in_=ctx["w2T"][e_i].rearrange("(j p) l -> p j l", p=128)
                        [:, j4 * 4:(j4 + 1) * 4, :])
                pr.do("gpsimd", lw2, writes=["w2b%d" % wb], dma=True)
            for tg in range(4):
                def bm(e, e_i=e_i, tg=tg):
                    return e.matmul(out=pbc[:], lhsT=consts["ones1_bf"][:],
                                    rhs=cwr_s[:, tg * TG:(tg + 1) * TG],
                                    start=True, stop=True)
                pr.do("tensor", bm, reads=["ones1bf", "cwr_s"], writes=[KBC])
                def bev(e, tg=tg, wb=wb):
                    return e.copy(out=bcw[wb][:, tg * TG:(tg + 1) * TG], in_=pbc[:])
                pr.do("scalar", bev, reads=[KBC], writes=["ebcw0"])
            hb = hp[wb]
            for it in range(8):
                for tg in range(4):
                    px = ph[(it * 4 + tg) % 2]; kx = KH[(it * 4 + tg) % 2]
                    for k in range(8):
                        def mm(e, k=k, it=it, tg=tg, px=px, wb=wb):
                            return e.matmul(
                                out=px[:], lhsT=w1b[wb][:, k, it * 128:(it + 1) * 128],
                                rhs=x_lat[:, k, tg * TG:(tg + 1) * TG],
                                start=(k == 0), stop=(k == 7))
                        pr.do("tensor", mm, reads=["w1b%d" % wb, "x_lat"], writes=[kx])
                    def rel(e, px=px):
                        return e.activation(out=rl[:], in_=px[:], func=AF.Relu)
                    pr.do("scalar", rel, reads=[kx], writes=["erl"])
                    def t1(e, it=it, tg=tg, wb=wb):
                        return e.tensor_tensor(
                            out=hb[:, it, tg * TG:(tg + 1) * TG], in0=rl[:],
                            in1=bcw[wb][:, tg * TG:(tg + 1) * TG], op=OP.mult)
                    pr.do("vector", t1, reads=["erl", "ebcw0"], writes=["ehp%d" % wb])
                    def t2(e, it=it, tg=tg, wb=wb):
                        return e.tensor_tensor(
                            out=hb[:, it, tg * TG:(tg + 1) * TG],
                            in0=hb[:, it, tg * TG:(tg + 1) * TG],
                            in1=rl[:], op=OP.mult)
                    pr.do("vector", t2, reads=["ehp%d" % wb, "erl"], writes=["ehp%d" % wb])

        for ep in range(E // 2):
            hprime(2 * ep, 0)
            hprime(2 * ep + 1, 1)
            for lt in range(8):
                for e2 in range(2):
                    for tg in range(4):
                        for k in range(8):
                            def mm(e, k=k, lt=lt, tg=tg, e2=e2):
                                return e.matmul(
                                    out=pz[:, tg * TG:(tg + 1) * TG],
                                    lhsT=w2b[e2][:, k, lt * 128:(lt + 1) * 128],
                                    rhs=hp[e2][:, k, tg * TG:(tg + 1) * TG],
                                    start=(e2 == 0 and k == 0),
                                    stop=(e2 == 1 and k == 7))
                            pr.do("tensor", mm, reads=["w2b%d" % e2, "ehp%d" % e2],
                                  writes=[KZ])
                def ev(e, lt=lt):
                    return e.tensor_copy(out=zsl[:], in_=pz[:])
                pr.do("vector", ev, reads=[KZ], writes=["ezsl"])
                def ac(e, lt=lt, ep=ep):
                    return e.dma_start(out=ctx["ylat_sp"][lt], in_=zsl[:],
                                       accum_op=(OP.add if ep > 0 else OP.bypass))
                pr.do("gpsimd", ac, reads=["ezsl"], dma=True)
        pr.barrier()
    pr.barrier()
    if ctx["stop_after"] == "pexp":
        _finish_zero_out(ctx)
        return

    with ExitStack() as PH:
        def psl(name, shape, dt=F32):
            return PH.enter_context(nc.sbuf_tensor(name, list(shape), dt))
        def pp(name, shape, dt=F32):
            return PH.enter_context(nc.psum_tensor(name, list(shape), dt))
        ylb = psl("ylb", [128, 8, T], BF16)
        fc2w = psl("fc2w", [128, 8, D], BF16)
        prd = [psl("prd%d" % i, [128, 512]) for i in range(2)]
        out_s = [psl("outs%d" % i, [128, 512]) for i in range(2)]
        pf = [pp("fpf%d" % i, [128, 512]) for i in range(2)]
        KF2 = ["P$fpf0", "P$fpf1"]
        for lt in range(8):
            def ly(e, lt=lt):
                return e.dma_start(out=ylb[:, lt, :], in_=ctx["ylat_sp"][lt])
            pr.do("gpsimd", ly, writes=["ylb"], dma=True)
        for j4 in range(2):
            def lw(e, j4=j4):
                return e.dma_start(
                    out=fc2w[:, j4 * 4:(j4 + 1) * 4, :],
                    in_=ctx["fc2T"].rearrange("(j p) d -> p j d", p=128)
                    [:, j4 * 4:(j4 + 1) * 4, :])
            pr.do("gpsimd", lw, writes=["fc2w"], dma=True)
        for t in range(16):
            for dg in range(4):
                ib = (t * 4 + dg) % 2
                px = pf[ib]; kx = KF2[ib]
                for k in range(8):
                    def mm(e, k=k, t=t, dg=dg, px=px):
                        return e.matmul(out=px[:], lhsT=ylb[:, k, t * 128:(t + 1) * 128],
                                        rhs=fc2w[:, k, dg * 512:(dg + 1) * 512],
                                        start=(k == 0), stop=(k == 7))
                    pr.do("tensor", mm, reads=["ylb", "fc2w"], writes=[kx])
                def lp(e, t=t, dg=dg, ib=ib):
                    return e.dma_start(
                        out=prd[ib][:],
                        in_=ctx["partial"][t * 128:(t + 1) * 128,
                                           dg * 512:(dg + 1) * 512])
                pr.do("sync", lp, writes=["prd%d" % ib], dma=True)
                def fin(e, ib=ib, px=px):
                    return e.tensor_tensor(out=out_s[ib][:], in0=px[:],
                                           in1=prd[ib][:], op=OP.add)
                pr.do("vector", fin, reads=[kx, "prd%d" % ib], writes=["outs%d" % ib])
                def sto(e, t=t, dg=dg, ib=ib):
                    return e.dma_start(
                        out=ctx["y_out"][t * 128:(t + 1) * 128,
                                         dg * 512:(dg + 1) * 512],
                        in_=out_s[ib][:])
                pr.do("scalar", sto, reads=["outs%d" % ib], dma=True)


# version 06
# speedup vs baseline: 1.2632x; 1.2632x over previous
import numpy as np
import sys

sys.path.insert(0, "/opt/trn_rl_repo")

import concourse.bass as bass
import concourse.mybir as mybir
from concourse.bass import IndirectOffsetOnAxis
from concourse.bass_utils import run_bass_kernel_spmd
from contextlib import ExitStack
import ml_dtypes

F32 = mybir.dt.float32
BF16 = mybir.dt.bfloat16
I32 = mybir.dt.int32
U32 = mybir.dt.uint32
OP = mybir.AluOpType
AF = mybir.ActivationFunctionType
AX = mybir.AxisListType

DEBUG = []

D = 2048
DIN = 4096
H = 64
HP = 64
NST = 128
KC = 4
CONV = DIN + 2 * NST
NPROJ = CONV + 64
PROJ = 8512
Q = 128
T = 2048
NCH = T // Q
TG = 512
NG = T // TG
E = 32
LAT = 1024
INTER = 1024
SH = 4096
EPS = 1e-5
MASKNEG = -30000.0


def _f32(x):
    return np.ascontiguousarray(np.asarray(x), dtype=np.float32)



class Prog:

    ENGS = ("sync", "vector", "scalar", "gpsimd", "tensor")

    def __init__(self, nc):
        self.nc = nc
        self.steps = {e: [] for e in self.ENGS}
        self.sems = {}
        self.cnt = {}
        for e in self.ENGS:
            self.sems[("c", e)] = nc.alloc_semaphore(name="Sc_" + e)
            self.sems[("q", e)] = nc.alloc_semaphore(name="Sq_" + e)
            self.cnt[("c", e)] = 0
            self.cnt[("q", e)] = 0
        self.writer = {}
        self.readers = {}
        self.water = {e: {} for e in self.ENGS}

    def _need_wait(self, eng, semid, cnt, src_eng):
        if cnt <= 0:
            return None
        if semid[0] == "c" and src_eng == eng and eng == "tensor":
            return None
        if self.water[eng].get(semid, 0) >= cnt:
            return None
        return (semid, cnt)

    def do(self, eng, fn, reads=(), writes=(), dma=False, tag=None):
        if dma:
            writes = list(writes) + ["__q_" + eng]
        excl = [k for k in reads if k.startswith("P$")]
        if excl:
            reads = [k for k in reads if not k.startswith("P$")]
            writes = list(writes) + excl
        waits = {}
        def add_wait(semid, cnt, src_eng):
            r = self._need_wait(eng, semid, cnt, src_eng)
            if r:
                waits[r[0]] = max(waits.get(r[0], 0), r[1])
        for k in reads:
            w = self.writer.get(k)
            if w:
                add_wait(w[0], w[1], w[2])
        for k in writes:
            w = self.writer.get(k)
            if w:
                add_wait(w[0], w[1], w[2])
            for semid, (rcnt, reng) in self.readers.get(k, {}).items():
                add_wait(semid, rcnt, reng)
        semid = ("q", eng) if dma else ("c", eng)
        inc = 16 if dma else 1
        self.cnt[semid] += inc
        after = self.cnt[semid]
        for k in writes:
            self.writer[k] = (semid, after, eng)
            self.readers[k] = {}
        for k in reads:
            self.readers.setdefault(k, {})[semid] = (after, eng)
        for sid, v in waits.items():
            self.water[eng][sid] = max(self.water[eng].get(sid, 0), v)
        sems = self.sems
        mysem = sems[semid]
        wl = list(waits.items())

        def wrapped(engine):
            for sid, v in wl:
                engine.wait_ge(sems[sid], v)
            r = fn(engine)
            r.then_inc(mysem, inc)

        self.steps[eng].append(wrapped)

    def raw(self, eng, fn):
        self.steps[eng].append(fn)

    def barrier(self):
        snap = dict(self.cnt)
        for e in self.ENGS:
            sems = self.sems
            for semid, v in snap.items():
                if v > self.water[e].get(semid, 0):
                    def wrapped(engine, semid=semid, v=v):
                        engine.wait_ge(sems[semid], v)
                    self.steps[e].append(wrapped)
                    self.water[e][semid] = v

    def finalize(self, block):
        for ename in self.ENGS:
            dec = getattr(block, ename)
            steps = self.steps[ename]

            def make(steps):
                def body(eng):
                    for fn in steps:
                        fn(eng)
                return body

            dec(make(steps))


def cdiv(a, b):
    return (a + b - 1) // b



def host_prepare(inputs):
    g = {}
    w_in = _f32(inputs["L0_attn_in_proj_w"])
    perm = np.concatenate([np.arange(DIN, DIN + CONV),
                           np.arange(DIN + CONV, PROJ),
                           np.arange(0, DIN)])
    g["w_inT"] = np.ascontiguousarray(w_in[perm].T)
    g["conv_w"] = _f32(inputs["L0_attn_conv_w"])
    g["conv_b"] = _f32(inputs["L0_attn_conv_b"])[:, None]
    g["dt_bias"] = _f32(inputs["L0_attn_dt_bias"])[:, None]
    g["A_log"] = _f32(inputs["L0_attn_A_log"])[:, None]
    g["norm_w"] = np.ascontiguousarray(
        _f32(inputs["L0_attn_norm_w"]).reshape(32, 128).T)
    g["w_outT"] = np.ascontiguousarray(_f32(inputs["L0_attn_out_proj_w"]).T)
    g["gate_wT"] = np.ascontiguousarray(_f32(inputs["L0_ffn_gate_w"]).T)
    g["gate_bias"] = _f32(inputs["L0_ffn_gate_bias"])[None, :]
    g["fc1T"] = np.ascontiguousarray(_f32(inputs["L0_ffn_fc1_w"]).T)
    g["fc2T"] = np.ascontiguousarray(_f32(inputs["L0_ffn_fc2_w"]).T)
    g["w1T"] = np.ascontiguousarray(_f32(inputs["L0_ffn_w1"]).transpose(0, 2, 1))
    g["w2T"] = np.ascontiguousarray(_f32(inputs["L0_ffn_w2"]).transpose(0, 2, 1))
    g["w_suT"] = np.ascontiguousarray(_f32(inputs["L0_ffn_shared_up_w"]).T)
    g["w_sdT"] = np.ascontiguousarray(_f32(inputs["L0_ffn_shared_down_w"]).T)

    g["eye_bf"] = np.eye(128, dtype=ml_dtypes.bfloat16)
    g["eye_f32"] = np.eye(128, dtype=np.float32)
    g["ones1_f32"] = np.ones((1, 128), dtype=np.float32)
    g["ones3_bf"] = np.ones((3, Q), dtype=ml_dtypes.bfloat16)
    em = np.zeros((128, Q), dtype=np.float32)
    for s in range(1, Q):
        em[s - 1, s] = MASKNEG
    g["emask"] = em.astype(ml_dtypes.bfloat16)
    u = (np.arange(128)[:, None] >= np.arange(Q)[None, :]).astype(np.float32)
    g["urep"] = np.tile(u, (1, 4)).astype(ml_dtypes.bfloat16)
    i64 = np.zeros((64, H * Q), dtype=np.float32)
    for h in range(H):
        i64[h, h * Q:(h + 1) * Q] = 1.0
    g["i64"] = i64.astype(ml_dtypes.bfloat16)
    g["i64d"] = np.concatenate([i64, i64], 0).astype(ml_dtypes.bfloat16)
    ih = np.zeros((64, 32 * Q), dtype=np.float32)
    for hh in range(32):
        ih[hh, hh * Q:(hh + 1) * Q] = 1.0
        ih[32 + hh, hh * Q:(hh + 1) * Q] = 1.0
    g["i64h"] = np.concatenate([ih, ih], 0).astype(ml_dtypes.bfloat16)
    Dfull = np.repeat(_f32(inputs["L0_attn_D"]), HP)
    g["Dcols"] = np.ascontiguousarray(Dfull.reshape(32, 128).T)
    g["nonce"] = np.random.default_rng().standard_normal((1, 4)).astype(np.float32)
    return g


def build_core_inputs(inputs, g, core):
    x = np.asarray(inputs["L0_attn_hidden_states"])
    b, seg = core // 4, core % 4
    lo = seg * T
    xo = x[b, lo:lo + T]
    halo_o = x[b, lo - 3:lo] if seg > 0 else np.zeros((3, D), np.float32)
    if seg == 2:
        flo = 0
    elif seg == 3:
        flo = T
    else:
        flo = lo
    xf = x[b, flo:flo + T]
    halo_f = x[b, flo - 3:flo] if flo > 0 else np.zeros((3, D), np.float32)
    m = {}
    m["x_own"] = _f32(np.concatenate([halo_o, xo], 0))
    m["x_for"] = _f32(np.concatenate([halo_f, xf], 0))
    m["halo_mask"] = np.full((128, 1), 0.0 if seg == 0 else 1.0, np.float32)
    m["halo_mask_f"] = np.full((128, 1), 0.0 if flo == 0 else 1.0, np.float32)
    me = core % 2
    m["widx"] = (np.arange(128, dtype=np.uint32)[:, None] + 128 * me)
    m["ridx"] = (np.arange(128, dtype=np.uint32)[:, None] + 128 * (1 - me))
    rm = np.zeros((128, 4), np.float32)
    rm[:, 0] = 1.0 if seg >= 2 else 0.0
    if seg in (1, 2, 3):
        rm[:, seg] = 1.0
    m["rolemask"] = rm
    m.update(g)
    return m



def build_program(stop_after="all"):
    nc = bass.Bass("TRN2", target_bir_lowering=False)

    def din(name, shape, dt=F32):
        return nc.dram_tensor(name, list(shape), dt, kind="ExternalInput").ap()

    dbg_outs = {}

    def dbg(name, shape, dt=F32):
        if name in DEBUG:
            dbg_outs[name] = nc.dram_tensor(
                "dbg_" + name, list(shape), dt, kind="ExternalOutput").ap()
            return dbg_outs[name]
        return None

    x_own = din("x_own", [T + 3, D])
    x_for = din("x_for", [T + 3, D])
    halo_mask = din("halo_mask", [128, 1])
    halo_mask_f = din("halo_mask_f", [128, 1])
    widx = din("widx", [128, 1], U32)
    ridx = din("ridx", [128, 1], U32)
    w_inT = din("w_inT", [D, PROJ])
    conv_w = din("conv_w", [CONV, KC])
    conv_b = din("conv_b", [CONV, 1])
    dt_bias = din("dt_bias", [H, 1])
    A_log = din("A_log", [H, 1])
    norm_w = din("norm_w", [128, 32])
    w_outT = din("w_outT", [DIN, D])
    gate_wT = din("gate_wT", [D, E])
    gate_bias = din("gate_bias", [1, E])
    fc1T = din("fc1T", [D, LAT])
    fc2T = din("fc2T", [LAT, D])
    w1T = din("w1T", [E, LAT, INTER])
    w2T = din("w2T", [E, INTER, LAT])
    w_suT = din("w_suT", [D, SH])
    w_sdT = din("w_sdT", [SH, D])
    eye_bf = din("eye_bf", [128, 128], BF16)
    eye_f32 = din("eye_f32", [128, 128])
    ones1_f32 = din("ones1_f32", [1, 128])
    ones3_bf = din("ones3_bf", [3, Q], BF16)
    emask = din("emask", [128, Q], BF16)
    urep = din("urep", [128, 4 * Q], BF16)
    i64 = din("i64", [H, H * Q], BF16)
    i64h = din("i64h", [128, 32 * Q], BF16)
    i64d = din("i64d", [128, H * Q], BF16)
    Dcols = din("Dcols", [128, 32])
    nonce = din("nonce", [1, 4])
    rolemask = din("rolemask", [128, 4])

    y_out = nc.dram_tensor("y_out", [T, D], F32, kind="ExternalOutput").ap()

    xn_sp = nc.dram_tensor("xn_sp", [16, 128, T + 3], BF16, kind="Internal").ap()
    partial = nc.dram_tensor("partial", [T, D], F32, kind="Internal").ap()
    ymain_sp = nc.dram_tensor("ymain_sp", [T, DIN], BF16, kind="Internal").ap()
    sgate_sp = nc.dram_tensor("sgate_sp", [DIN, T], BF16, kind="Internal").ap()
    csrows = nc.dram_tensor("csrows", [3, H, T], BF16, kind="Internal").ap()
    exch = nc.dram_tensor("exch", [2 * 128, 2 * 4104], F32, kind="Internal",
                          addr_space="Shared").ap()
    exfl = nc.dram_tensor("exfl", [2 * 128, 4], F32, kind="Internal",
                          addr_space="Shared").ap()

    pr = Prog(nc)
    ES = ExitStack()

    def sb(name, shape, dt=F32):
        return ES.enter_context(nc.sbuf_tensor(name, list(shape), dt))

    def ps(name, shape, dt=F32):
        return ES.enter_context(nc.psum_tensor(name, list(shape), dt))

    with nc.Block() as block:
        cw_res = sb("cw_res", [128, 16, E])
        cwT = sb("cwT", [E, T], BF16)
        consts = {}
        consts["eye_bf"] = sb("c_eye_bf", [128, 128], BF16)
        consts["eye_f32"] = sb("c_eye_f32", [128, 128])
        consts["ones1_f32"] = sb("c_ones1", [1, 128])
        consts["ones1_bf"] = sb("c_ones1b", [1, 128], BF16)
        consts["gwT"] = sb("c_gwT", [128, 16, E])
        consts["gbt"] = sb("c_gbt", [128, E])
        consts["epscol"] = sb("c_eps", [128, 1])
        gb_row = sb("c_gbrow", [1, E])

        with ExitStack() as PH0:
            def ps0(name, shape, dt=F32):
                return PH0.enter_context(nc.psum_tensor(name, list(shape), dt))

            def sb0(name, shape, dt=F32):
                return PH0.enter_context(nc.sbuf_tensor(name, list(shape), dt))

            xtok = [sb0("xtok%d" % i, [128, D]) for i in range(2)]
            gmt = [sb0("gmt%d" % i, [128, 256]) for i in range(2)]
            xnt = [sb0("xnt%d" % i, [128, D]) for i in range(2)]
            xnT = [sb0("xnT%d" % i, [128, 16, 128]) for i in range(2)]
            xnb = [sb0("xnb%d" % i, [128, 16, 128], BF16) for i in range(2)]
            stat = sb0("stat", [128, 8])
            dummy = sb0("p0dummy", [128, D], BF16)
            scr = sb0("p0scr", [128, 64])
            pT = [ps0("P$pT%d" % i, [128, 128]) for i in range(4)]
            pL = ps0("pL", [128, E])
            pC = ps0("pC", [32, 128])
            pB = ps0("pB", [128, E])

            pr.do("vector", lambda e: e.memset(consts["epscol"][:], EPS), writes=["epscol"])
            pr.do("sync", lambda e: e.dma_start(out=consts["eye_bf"][:], in_=eye_bf[:]),
                  writes=["eye_bf"], dma=True)
            pr.do("sync", lambda e: e.dma_start(out=consts["eye_f32"][:], in_=eye_f32[:]),
                  writes=["eye_f32"], dma=True)
            pr.do("sync", lambda e: e.dma_start(out=consts["ones1_f32"][:], in_=ones1_f32[:]),
                  writes=["ones1"], dma=True)
            pr.do("gpsimd", lambda e: e.dma_start(out=consts["ones1_bf"][:], in_=ones1_f32[:]),
                  writes=["ones1bf"], dma=True)
            pr.do("sync", lambda e: e.dma_start(
                out=consts["gwT"][:], in_=gate_wT.rearrange("(j p) e -> p j e", p=128)),
                writes=["gwT"], dma=True)
            pr.do("sync", lambda e: e.dma_start(out=gb_row[:], in_=gate_bias[:]),
                  writes=["gbrow"], dma=True)
            pr.do("tensor", lambda e: e.matmul(out=pB[:], lhsT=consts["ones1_f32"][:],
                                               rhs=gb_row[:], start=True, stop=True),
                  reads=["ones1", "gbrow"], writes=["P$pB"])
            pr.do("vector", lambda e: e.tensor_copy(out=consts["gbt"][:], in_=pB[:]),
                  reads=["P$pB"], writes=["gbt"])

            def ln_tile(i, x_dram, col_off, n_rows, buf):
                xt, xn_t, xT = xtok[buf], xnt[buf], xnT[buf]
                kxt, kxn, kxT = "xtok%d" % buf, "xnt%d" % buf, "xnT%d" % buf
                pr.do("sync", lambda e: e.dma_start(
                    out=xt[0:n_rows, :], in_=x_dram[col_off:col_off + n_rows, :]),
                    writes=[kxt], dma=True)
                pr.do("scalar", lambda e: e.activation(
                    out=dummy[0:n_rows, :], in_=xt[0:n_rows, :], func=AF.Copy,
                    accum_out=stat[0:n_rows, 0:1]), reads=[kxt], writes=["stat", "p0dummy"])
                pr.do("scalar", lambda e: e.activation(
                    out=dummy[0:n_rows, :], in_=xt[0:n_rows, :], func=AF.Square,
                    accum_out=stat[0:n_rows, 1:2]), reads=[kxt], writes=["stat2", "p0dummy"])
                pr.do("vector", lambda e: e.tensor_scalar(
                    out=stat[0:n_rows, 2:3], in0=stat[0:n_rows, 0:1],
                    scalar1=1.0 / D, scalar2=None, op0=OP.mult, op1=OP.bypass),
                    reads=["stat"], writes=["mu"])
                pr.do("vector", lambda e: e.tensor_tensor(
                    out=stat[0:n_rows, 3:4], in0=stat[0:n_rows, 2:3],
                    in1=stat[0:n_rows, 2:3], op=OP.mult),
                    reads=["mu"], writes=["mu2"])
                pr.do("vector", lambda e: e.scalar_tensor_tensor(
                    out=stat[0:n_rows, 4:5], in0=stat[0:n_rows, 1:2],
                    scalar=1.0 / D, in1=stat[0:n_rows, 3:4],
                    op0=OP.mult, op1=OP.subtract),
                    reads=["stat2", "mu2"], writes=["var"])
                pr.do("scalar", lambda e: e.activation(
                    out=stat[0:n_rows, 5:6], in_=stat[0:n_rows, 4:5],
                    func=AF.Sqrt, bias=consts["epscol"][0:n_rows, :]),
                    reads=["var", "epscol"], writes=["sd"])
                pr.do("vector", lambda e: e.reciprocal(
                    out=stat[0:n_rows, 6:7], in_=stat[0:n_rows, 5:6]),
                    reads=["sd"], writes=["rstd"])
                pr.do("vector", lambda e: e.tensor_scalar(
                    out=xn_t[0:n_rows, :], in0=xt[0:n_rows, :],
                    scalar1=stat[0:n_rows, 2:3], scalar2=stat[0:n_rows, 6:7],
                    op0=OP.subtract, op1=OP.mult),
                    reads=[kxt, "mu", "rstd"], writes=[kxn])
                for jj in range(16):
                    pj = pT[jj % 4]
                    kpj = "P$pT%d" % (jj % 4)

                    def tp(e, jj=jj, pj=pj, xn_t=xn_t):
                        return e.transpose(out=pj[:, 0:n_rows],
                                           in_=xn_t[0:n_rows, jj * 128:(jj + 1) * 128],
                                           identity=consts["eye_f32"][0:n_rows, 0:n_rows])
                    pr.do("tensor", tp, reads=[kxn, "eye_f32"], writes=[kpj])
                    def ev_f(e, jj=jj, pj=pj, xT=xT):
                        return e.tensor_copy(out=xT[:, jj, 0:n_rows], in_=pj[:, 0:n_rows])
                    pr.do("vector", ev_f, reads=[kpj], writes=[kxT])
                    def ev_b(e, jj=jj, pj=pj, buf=buf):
                        return e.copy(out=xnb[buf][:, jj, 0:n_rows],
                                      in_=pj[:, 0:n_rows])
                    pr.do("scalar", ev_b, reads=[kpj], writes=["xnb%d" % buf])
                def spx(e, buf=buf, col_off=col_off, n_rows=n_rows):
                    return e.dma_start(
                        out=xn_sp[:, :, col_off:col_off + n_rows]
                        .rearrange("j p t -> p j t"),
                        in_=xnb[buf][:, :, 0:n_rows])
                pr.do("scalar", spx, reads=["xnb%d" % buf], dma=True)

            def routing_tile(i, buf):
                xT = xnT[buf]
                kxT = "xnT%d" % buf
                for k in range(16):
                    def mm(e, k=k, xT=xT):
                        return e.matmul(out=pL[:], lhsT=xT[:, k, :],
                                        rhs=consts["gwT"][:, k, :],
                                        start=(k == 0), stop=(k == 15))
                    pr.do("tensor", mm, reads=[kxT, "gwT"], writes=["P$pL"])
                sc = scr[:, 0:32]
                sbv = scr[:, 32:64]
                pr.do("scalar", lambda e: e.activation(out=sc, in_=pL[:], func=AF.Sigmoid),
                      reads=["P$pL"], writes=["sc"])
                pr.do("vector", lambda e: e.tensor_tensor(out=sbv, in0=sc, in1=consts["gbt"][:],
                                                          op=OP.add),
                      reads=["sc", "gbt"], writes=["sbv"])
                gm = gmt[buf]
                kgm = "gmt%d" % buf
                for ggi in range(4):
                    def m8(e, ggi=ggi, gm=gm):
                        return e.max(out=gm[:, ggi * 8:(ggi + 1) * 8],
                                     in_=sbv[:, ggi * 8:(ggi + 1) * 8])
                    pr.do("vector", m8, reads=["sbv"], writes=[kgm])
                pr.do("vector", lambda e: e.tensor_tensor(
                    out=gm[:, 32:36],
                    in0=gm[:, 0:32].rearrange("p (g k) -> p g k", g=4)[:, :, 0],
                    in1=gm[:, 0:32].rearrange("p (g k) -> p g k", g=4)[:, :, 1],
                    op=OP.add), reads=[kgm], writes=[kgm + "gs"])
                pr.do("vector", lambda e: e.memset(gm[:, 36:40], -1e30), writes=[kgm + "pad"])
                pr.do("vector", lambda e: e.max(out=gm[:, 40:48], in_=gm[:, 32:40]),
                      reads=[kgm + "gs", kgm + "pad"], writes=[kgm + "g8"])
                pr.do("vector", lambda e: e.tensor_scalar(
                    out=gm[:, 48:52], in0=gm[:, 32:36], scalar1=gm[:, 41:42],
                    scalar2=None, op0=OP.is_ge, op1=OP.bypass),
                    reads=[kgm + "gs", kgm + "g8"], writes=[kgm + "keep"])
                pr.do("vector", lambda e: e.tensor_scalar(
                    out=gm[:, 52:56], in0=gm[:, 48:52], scalar1=1e5, scalar2=-1e5,
                    op0=OP.mult, op1=OP.add), reads=[kgm + "keep"], writes=[kgm + "t"])
                pr.do("vector", lambda e: e.tensor_tensor(
                    out=gm[:, 64:96],
                    in0=gm[:, 52:56].unsqueeze(2).broadcast_to((128, 4, 8)),
                    in1=sbv.rearrange("p (g k) -> p g k", g=4), op=OP.add),
                    reads=[kgm + "t", "sbv"], writes=[kgm + "masked"])
                pr.do("vector", lambda e: e.max(out=gm[:, 96:104], in_=gm[:, 64:96]),
                      reads=[kgm + "masked"], writes=[kgm + "m8"])
                pr.do("vector", lambda e: e.tensor_scalar(
                    out=gm[:, 104:136], in0=gm[:, 64:96], scalar1=gm[:, 103:104],
                    scalar2=None, op0=OP.is_ge, op1=OP.bypass),
                    reads=[kgm + "masked", kgm + "m8"], writes=[kgm + "sel"])
                pr.do("vector", lambda e: e.scalar_tensor_tensor(
                    out=gm[:, 136:168], in0=gm[:, 104:136], scalar=1.0, in1=sc,
                    op0=OP.mult, op1=OP.mult, accum_out=gm[:, 168:169]),
                    reads=[kgm + "sel", "sc"], writes=[kgm + "ssel", kgm + "ssum"])
                pr.do("vector", lambda e: e.reciprocal(out=gm[:, 169:170], in_=gm[:, 168:169]),
                      reads=[kgm + "ssum"], writes=[kgm + "rec"])

                def cwf(e, i=i, gm=gm):
                    return e.tensor_scalar(
                        out=cw_res[:, i, :], in0=gm[:, 136:168], scalar1=gm[:, 169:170],
                        scalar2=2.5, op0=OP.mult, op1=OP.mult)
                pr.do("vector", cwf, reads=[kgm + "ssel", kgm + "rec"], writes=["cw%d" % i])
                def cwt(e, i=i):
                    return e.transpose(out=pC[:], in_=cw_res[:, i, :],
                                       identity=consts["eye_f32"][:])
                pr.do("tensor", cwt, reads=["cw%d" % i, "eye_f32"], writes=["P$pC"])

                def cwev(e, i=i):
                    return e.copy(out=cwT[:, i * 128:(i + 1) * 128], in_=pC[:])
                pr.do("scalar", cwev, reads=["P$pC"], writes=["cwT"])

            ln_tile(-1, x_own, 0, 3, 0)
            for i in range(16):
                ln_tile(i, x_own, 3 + i * 128, 128, i % 2)
                routing_tile(i, i % 2)

            od = dbg("xn_fm", [16, 128, T + 3], BF16)
            if od is not None:
                pr.do("sync", lambda e, od=od: e.dma_start(out=od[:], in_=xn_sp[:]),
                      dma=True)
            od = dbg("cw", [128, 16, E])
            if od is not None:
                pr.do("sync", lambda e, od=od: e.dma_start(out=od[:], in_=cw_res[:]),
                      reads=["cw%d" % i for i in range(16)], dma=True)
            od = dbg("cwT", [E, T], BF16)
            if od is not None:
                pr.do("sync", lambda e, od=od: e.dma_start(out=od[:], in_=cwT[:]),
                      reads=["cwT"], dma=True)
            pr.barrier()

        pr.barrier()

        if stop_after == "p0":
            ztile = sb("zz", [128, D])
            pr.do("vector", lambda e: e.memset(ztile[:], 0.0), writes=["zz"])
            for i in range(16):
                def st(e, i=i):
                    return e.dma_start(out=y_out[i * 128:(i + 1) * 128, :], in_=ztile[:])
                pr.do("scalar", st, reads=["zz"], dma=True)
            pr.barrier()
            pr.finalize(block)
            ES.close()
            return nc

        ctx = dict(
            nc=nc, pr=pr, ES=ES, sb=sb, ps=ps, consts=consts,
            xn_sp=xn_sp, cw_res=cw_res, cwT=cwT, dbg=dbg,
            stop_after=stop_after,
            x_own=x_own, x_for=x_for, halo_mask=halo_mask,
            halo_mask_f=halo_mask_f, widx=widx, ridx=ridx, w_inT=w_inT,
            conv_w=conv_w, conv_b=conv_b, dt_bias=dt_bias, A_log=A_log,
            norm_w=norm_w, w_outT=w_outT, fc1T=fc1T, fc2T=fc2T, w1T=w1T,
            w2T=w2T, w_suT=w_suT, w_sdT=w_sdT, ones3_bf=ones3_bf,
            emask=emask, urep=urep, i64=i64, i64h=i64h, i64d=i64d, Dcols=Dcols,
            nonce=nonce, rolemask=rolemask, y_out=y_out, partial=partial, ymain_sp=ymain_sp,
            sgate_sp=sgate_sp, csrows=csrows, exch=exch, exfl=exfl,
        )
        emit_phases(ctx)

        pr.barrier()
        pr.finalize(block)
    ES.close()
    return nc


def kernel(**inputs):
    g = host_prepare(inputs)
    maps = [build_core_inputs(inputs, g, c) for c in range(8)]
    nc = build_program()
    res = run_bass_kernel_spmd(nc, maps, core_ids=list(range(8)))
    x = np.asarray(inputs["L0_attn_hidden_states"])
    out = np.empty((2, 8192, D), np.float32)
    for c in range(8):
        b, seg = c // 4, c % 4
        out[b, seg * T:(seg + 1) * T] = np.asarray(res.results[c]["y_out"])
    return out



def emit_phases(ctx):
    nc = ctx["nc"]; pr = ctx["pr"]; sb = ctx["sb"]; ES = ctx["ES"]
    consts = ctx["consts"]; cwT = ctx["cwT"]
    dbg = ctx["dbg"]; stop_after = ctx["stop_after"]

    xlat_sp = nc.dram_tensor("xlat_sp", [8, 128, T], BF16, kind="Internal").ap()
    ctx["xlat_sp"] = xlat_sp
    w_in_bf = nc.dram_tensor("w_in_bf", [D, PROJ], BF16, kind="Internal").ap()
    ylat_sp = nc.dram_tensor("ylat_sp", [8, 128, T], F32, kind="Internal").ap()
    ctx["w_in_bf"] = w_in_bf
    ctx["ylat_sp"] = ylat_sp

    with ExitStack() as PH:
        def psl(name, shape, dt=F32):
            return PH.enter_context(nc.sbuf_tensor(name, list(shape), dt))
        wtmp = [psl("wc%d" % i, [128, 16, 512], BF16) for i in range(2)]
        for b in range(17):
            buf = wtmp[b % 2]; k = "wc%d" % (b % 2)
            lo = b * 512
            w = min(512, PROJ - lo)
            def ld(e, buf=buf, lo=lo, w=w):
                return e.dma_start(
                    out=buf[:, :, 0:w],
                    in_=ctx["w_inT"][:, lo:lo + w].rearrange("(j p) c -> p j c", p=128))
            pr.do("gpsimd", ld, writes=[k], dma=True)
            def st(e, buf=buf, lo=lo, w=w):
                return e.dma_start(
                    out=w_in_bf[:, lo:lo + w].rearrange("(j p) c -> p j c", p=128),
                    in_=buf[:, :, 0:w])
            pr.do("scalar", st, reads=[k], dma=True)
        pr.barrier()

    with ExitStack() as PH:
        def psl(name, shape, dt=F32):
            return PH.enter_context(nc.sbuf_tensor(name, list(shape), dt))
        def pp(name, shape, dt=F32):
            return PH.enter_context(nc.psum_tensor(name, list(shape), dt))

        x_lat = psl("x_lat", [128, 8, T], BF16)
        xslab = [psl("xsl%d" % i, [128, 16, TG], BF16) for i in range(2)]
        def load_xslab(tg, b):
            def ld(e, tg=tg, b=b):
                return e.dma_start(
                    out=xslab[b][:],
                    in_=ctx["xn_sp"][:, :, 3 + tg * TG:3 + (tg + 1) * TG]
                    .rearrange("j p t -> p j t"))
            pr.do("sync", ld, writes=["xslab%d" % b], dma=True)
        fc1w = psl("fc1w", [128, 16, LAT], BF16)
        for j4 in range(4):
            def ld(e, j4=j4):
                return e.dma_start(
                    out=fc1w[:, j4 * 4:(j4 + 1) * 4, :],
                    in_=ctx["fc1T"].rearrange("(j p) l -> p j l", p=128)[:, j4 * 4:(j4 + 1) * 4, :])
            pr.do("gpsimd", ld, writes=["fc1w"], dma=True)
        pfc = [pp("pfc%d" % i, [128, 512]) for i in range(2)]
        for tg in range(NG):
            b = tg % 2
            load_xslab(tg, b)
            for lt in range(8):
                pj = pfc[lt % 2]; kp = "P$pfc%d" % (lt % 2)
                for k in range(16):
                    def mm(e, k=k, lt=lt, tg=tg, pj=pj, b=b):
                        return e.matmul(out=pj[:], lhsT=fc1w[:, k, lt * 128:(lt + 1) * 128],
                                        rhs=xslab[b][:, k, :],
                                        start=(k == 0), stop=(k == 15))
                    pr.do("tensor", mm, reads=["fc1w", "xslab%d" % b], writes=[kp])
                def ev(e, lt=lt, tg=tg, pj=pj):
                    return e.copy(out=x_lat[:, lt, tg * TG:(tg + 1) * TG], in_=pj[:])
                pr.do("scalar", ev, reads=[kp], writes=["x_lat"])

        for lt in range(8):
            def spx(e, lt=lt):
                return e.dma_start(out=xlat_sp[lt], in_=x_lat[:, lt, :])
            pr.do("scalar", spx, reads=["x_lat"], dma=True)
        wsu = [psl("wsu%d" % i, [128, 16, 128], BF16) for i in range(2)]
        wsd = [psl("wsd%d" % i, [128, 512], BF16) for i in range(2)]
        s1g = psl("s1g", [128, 32, TG], BF16)
        rl32 = psl("rl32", [128, TG], BF16)
        xread = [psl("xrd%d" % i, [128, 512]) for i in range(2)]
        oslab = [psl("osl%d" % i, [128, 512]) for i in range(2)]
        pup = [pp("pup%d" % i, [128, 512]) for i in range(2)]
        pdn = [pp("pdn%d" % i, [128, 512]) for i in range(4)]
        for tg in range(NG):
            bx = tg % 2
            load_xslab(tg, bx)
            for sl in range(32):
                b = sl % 2
                def ldw(e, sl=sl, b=b):
                    return e.dma_start(
                        out=wsu[b][:],
                        in_=ctx["w_suT"][:, sl * 128:(sl + 1) * 128]
                        .rearrange("(j p) c -> p j c", p=128))
                pr.do("gpsimd", ldw, writes=["wsu%d" % b], dma=True)
                pj = pup[b]; kp = "P$pup%d" % b
                for k in range(16):
                    def mm(e, k=k, sl=sl, tg=tg, pj=pj, b=b, bx=bx):
                        return e.matmul(out=pj[:], lhsT=wsu[b][:, k, :],
                                        rhs=xslab[bx][:, k, :],
                                        start=(k == 0), stop=(k == 15))
                    pr.do("tensor", mm, reads=["wsu%d" % b, "xslab%d" % bx], writes=[kp])
                def rel(e, sl=sl, pj=pj):
                    return e.activation(out=rl32[:], in_=pj[:], func=AF.Relu)
                pr.do("scalar", rel, reads=[kp], writes=["rl32"])
                def sq(e, sl=sl):
                    return e.activation(out=s1g[:, sl, :], in_=rl32[:], func=AF.Square)
                pr.do("scalar", sq, reads=["rl32"], writes=["s1g"])
            for dgrp in range(4):
                for ksl in range(32):
                    b = ksl % 2
                    def ldw(e, ksl=ksl, dgrp=dgrp, b=b):
                        return e.dma_start(
                            out=wsd[b][:],
                            in_=ctx["w_sdT"][ksl * 128:(ksl + 1) * 128,
                                             dgrp * 512:(dgrp + 1) * 512])
                    pr.do("gpsimd", ldw, writes=["wsd%d" % b], dma=True)
                    for ts in range(4):
                        pj = pdn[ts]; kp = "P$pdn%d" % ts
                        def mm(e, ksl=ksl, ts=ts, pj=pj, b=b):
                            return e.matmul(out=pj[:],
                                            lhsT=s1g[:, ksl, ts * 128:(ts + 1) * 128],
                                            rhs=wsd[b][:],
                                            start=(ksl == 0), stop=(ksl == 31))
                        pr.do("tensor", mm, reads=["wsd%d" % b, "s1g"], writes=[kp])
                for ts in range(4):
                    b2 = ts % 2
                    row = tg * TG + ts * 128
                    def ldx(e, row=row, dgrp=dgrp, b2=b2):
                        return e.dma_start(out=xread[b2][:],
                                           in_=ctx["x_own"][3 + row:3 + row + 128,
                                                            dgrp * 512:(dgrp + 1) * 512])
                    pr.do("sync", ldx, writes=["xrd%d" % b2], dma=True)
                    pj = pdn[ts]; kp = "P$pdn%d" % ts
                    def ev(e, ts=ts, b2=b2, pj=pj):
                        return e.tensor_tensor(out=oslab[b2][:], in0=pj[:],
                                               in1=xread[b2][:], op=OP.add)
                    pr.do("vector", ev, reads=[kp, "xrd%d" % b2], writes=["osl%d" % b2])
                    def st(e, row=row, dgrp=dgrp, b2=b2):
                        return e.dma_start(
                            out=ctx["partial"][row:row + 128, dgrp * 512:(dgrp + 1) * 512],
                            in_=oslab[b2][:])
                    pr.do("scalar", st, reads=["osl%d" % b2], dma=True)
        pr.barrier()
    pr.barrier()
    if stop_after == "p1":
        _finish_zero_out(ctx)
        return
    emit_attn(ctx, None)


def _finish_zero_out(ctx):
    nc, pr, sb = ctx["nc"], ctx["pr"], ctx["sb"]
    ztile = sb("zz", [128, D])
    pr.do("vector", lambda e: e.memset(ztile[:], 0.0), writes=["zz"])
    for i in range(16):
        def st(e, i=i):
            return e.dma_start(out=ctx["y_out"][i * 128:(i + 1) * 128, :], in_=ztile[:])
        pr.do("scalar", st, reads=["zz"], dma=True)
    pr.barrier()


def emit_attn(ctx, x_lat):
    nc = ctx["nc"]; pr = ctx["pr"]; sb = ctx["sb"]
    consts = ctx["consts"]

    w_in_bf = ctx["w_in_bf"]

    Pinb = sb("Pinb", [128, DIN], BF16)
    ctx["Pinb"] = Pinb
    keep_es = ExitStack()
    ctx["keep_es"] = keep_es
    def sb(name, shape, dt=F32):
        return keep_es.enter_context(nc.sbuf_tensor(name, list(shape), dt))
    flogD = sb("flogD", [64, 1])
    ologD = sb("ologD", [64, 1])
    C_keep = sb("C_keep", [128, T], BF16)
    cs_keep = sb("cs_keep", [64, T])
    dpre_keep = sb("dpre_keep", [64, NCH])
    rstd_res = sb("rstd_res", [128, NCH])
    Ddiag = sb("Ddiag", [128, 32, 128], BF16)
    convw = sb("convw_s", [128, 34, KC])
    convb = sb("convb_s", [128, 34])
    dtb_c = sb("dtb_c", [64, 1])
    nAcol = sb("nAcol", [64, 1])
    ones64 = sb("ones64", [64, Q])
    onesb1 = sb("onesb1", [1, 128], BF16)
    onescol_bf = sb("onescol_bf", [128, 1], BF16)
    zcol = sb("zcol", [64, 1])

    pr.do("sync", lambda e: e.dma_start(
        out=convw[:, 0:34, :],
        in_=ctx["conv_w"].rearrange("(s p) k -> p s k", p=128)), writes=["convw"], dma=True)
    pr.do("sync", lambda e: e.dma_start(
        out=convb[:, 0:34],
        in_=ctx["conv_b"].rearrange("(s p) o -> p s o", p=128)[:, :, 0],
        allow_slow_non_contiguous=True),
        writes=["convb"], dma=True)
    pr.do("sync", lambda e: e.dma_start(out=dtb_c[:], in_=ctx["dt_bias"][:]),
          writes=["dtb"], dma=True)
    pr.do("vector", lambda e: e.memset(ones64[:], 1.0), writes=["ones64"])
    pr.do("vector", lambda e: e.memset(onesb1[:], 1.0), writes=["onesb1"])
    pr.do("vector", lambda e: e.memset(onescol_bf[:], 1.0), writes=["onescol_bf"])
    pr.do("vector", lambda e: e.memset(zcol[:], 0.0), writes=["zcol"])
    tmpa = sb("tmpa", [64, 1])
    pr.do("sync", lambda e: e.dma_start(out=tmpa[:], in_=ctx["A_log"][:]),
          writes=["tmpa"], dma=True)
    pr.do("scalar", lambda e: e.activation(out=nAcol[:], in_=tmpa[:], func=AF.Exp),
          reads=["tmpa"], writes=["nA"])
    pr.do("vector", lambda e: e.tensor_scalar(out=nAcol[:], in0=nAcol[:], scalar1=-1.0,
                                              scalar2=None, op0=OP.mult, op1=OP.bypass),
          reads=["nA"], writes=["nA"])
    Dc = sb("Dc_s", [128, 32])
    pr.do("sync", lambda e: e.dma_start(out=Dc[:], in_=ctx["Dcols"][:]),
          writes=["Dc"], dma=True)
    for j in range(32):
        def mkdiag(e, j=j):
            return e.tensor_scalar(
                out=Ddiag[:, j, :], in0=consts["eye_bf"][:], scalar1=Dc[:, j:j + 1],
                scalar2=None, op0=OP.mult, op1=OP.bypass)
        pr.do("vector", mkdiag, reads=["eye_bf", "Dc"], writes=["Ddiag"])

    att_es = ExitStack()
    P_state = att_es.enter_context(nc.sbuf_tensor("P_state", [128, DIN], F32))
    fO = att_es.enter_context(nc.sbuf_tensor("fO", [128, DIN], F32))
    ctx["att_es"] = att_es
    pr.do("vector", lambda e: e.memset(P_state[:], 0.0), writes=["P_state"])
    emit_front(ctx, own=False, P_state=P_state, fO=fO, flogD=flogD,
               ologD=ologD, C_keep=C_keep, cs_keep=cs_keep,
               dpre_keep=dpre_keep,
               statics=dict(convw=convw, convb=convb, dtb_c=dtb_c, nAcol=nAcol,
                            ones64=ones64, onesb1=onesb1, zcol=zcol,
                            Ddiag=Ddiag, w_in_bf=w_in_bf,
                            onescol_bf=onescol_bf))
    pr.do("vector", lambda e: e.tensor_copy(out=fO[:], in_=P_state[:]),
          reads=["P_state"], writes=["fO"])
    pr.do("vector", lambda e: e.memset(P_state[:], 0.0), writes=["P_state"])
    pr.barrier()
    if ctx["stop_after"] == "pfor":
        ctx["att_es"].close()
        ctx["keep_es"].close()
        _finish_zero_out(ctx)
        return
    emit_front(ctx, own=True, P_state=P_state, fO=fO, flogD=flogD,
               ologD=ologD, C_keep=C_keep, cs_keep=cs_keep,
               dpre_keep=dpre_keep,
               statics=dict(convw=convw, convb=convb, dtb_c=dtb_c, nAcol=nAcol,
                            ones64=ones64, onesb1=onesb1, zcol=zcol,
                            Ddiag=Ddiag, w_in_bf=w_in_bf,
                            onescol_bf=onescol_bf))
    pr.barrier()
    if ctx["stop_after"] == "pown":
        ctx["att_es"].close()
        ctx["keep_es"].close()
        _finish_zero_out(ctx)
        return
    emit_exchange_and_tail(ctx, x_lat, P_state, fO, flogD, ologD,
                           C_keep, cs_keep, dpre_keep, rstd_res,
                           onesb1, onescol_bf, zcol)


def convw_col(st, sl, kk):
    return st["convw"][:, sl, kk:kk + 1]


def emit_front(ctx, own, P_state, fO, flogD, ologD, C_keep, cs_keep,
               dpre_keep, statics):
    nc = ctx["nc"]; pr = ctx["pr"]; sb = ctx["sb"]
    consts = ctx["consts"]
    st = statics
    tagc = "o" if own else "f"

    with ExitStack() as PH:
        def psl(name, shape, dt=F32):
            return PH.enter_context(nc.sbuf_tensor(tagc + name, list(shape), dt))
        def pp(name, shape, dt=F32):
            return PH.enter_context(nc.psum_tensor(tagc + name, list(shape), dt))

        pA = pp("pA", [128, 512]); pB = pp("pB", [128, 512])
        pC = pp("pC", [128, 512]); pD = pp("pD", [128, 512])
        pE = pp("pE", [128, 512]); pF = pp("pF", [128, 512])
        ptp = [pp("ptp%d" % i, [128, 1024], BF16) for i in range(2)]
        KA, KB, KC_, KD, KE, KF = "P$pA", "P$pB", "P$pC", "P$pD", "P$pE", "P$pF"
        KT = ["P$ptp0", "P$ptp1"]

        winb = [psl("winb%d" % i, [128, 16, 128], BF16) for i in range(2)]
        xc = [psl("xc0", [128, 515])]
        xc.append(xc[0])
        tails = psl("tails", [128, 34, 3])
        xfm = psl("xfm", [128, 32, TG], BF16)
        bc = psl("bc", [128, 2, TG], BF16)
        dtt = psl("dtt", [64, TG])
        sp1 = psl("sp1", [64, TG])
        sp2 = psl("sp2", [64, TG])
        dtTb = psl("dtTb", [64, TG], BF16)
        csT = cs_keep
        runT = psl("runT", [64, 1])
        cdc = psl("cdc", [64, 1])
        cdrow = psl("cdrow", [1, 64], BF16)
        cdrow4 = psl("cdrow4", [1, 512], BF16)
        xdt_tok = psl("xdtok", [128, DIN], BF16)
        dt_tok = psl("dttok", [128, 64], BF16)
        dec_tok = psl("dectok", [128, 64], BF16)
        decfm = psl("decfm", [64, Q], BF16)
        xdg = psl("xdg", [128, 512], BF16)
        B_tok = psl("B_tok", [128, 128], BF16)
        Pbg = psl("Pbg", [128, 512], BF16)
        t1s = psl("t1s", [128, 512])
        xnf = psl("xnf", [128, 16, 515 if not own else 259], BF16)
        Msl = None
        if not own:
            xtok = [psl("xtk0", [128, D])]
            xnt = [psl("xnt0", [128, D], BF16)]
            lstat = psl("lstat", [128, 8])
            ldum = psl("ldum", [128, D], BF16)
        if own:
            csHI = psl("csHI", [64, TG], BF16)
            csMID = psl("csMID", [64, TG], BF16)
            csLOp = psl("csLOp", [64, TG], BF16)
            ncsLO = psl("ncsLO", [64, TG], BF16)
            lhsB = psl("lhsB", [128, TG], BF16)
            nhi = psl("nhi", [64, TG], BF16)
            nmid = psl("nmid", [64, TG], BF16)
            ecsfm = psl("ecsfm", [64, TG], BF16)
            ecs_tok = psl("ecstok", [128, 64], BF16)
            rhsA = psl("rhsA", [3, 16, Q], BF16)
            Mslab = psl("Msl", [128, H * Q // 2], BF16)
            Gs = psl("Gs", [128, 128], BF16)
            yo = psl("yo", [128, 512], BF16)
            yos = psl("yos", [128, 512], BF16)
            ysl2 = psl("ysl2", [128, 512], BF16)
            sgsl = psl("sgsl", [128, 512], BF16)
            i64_s = psl("i64_s", [128, 32 * Q], BF16)
            urep_s = psl("urep_s", [128, 4 * Q], BF16)
            emask_s = psl("emask_s", [128, Q], BF16)
            o3 = psl("ones3_s", [3, Q], BF16)
            pr.do("sync", lambda e: e.dma_start(out=emask_s[:], in_=ctx["emask"][:]),
                  writes=["em"], dma=True)
            pr.do("sync", lambda e: e.dma_start(out=urep_s[:], in_=ctx["urep"][:]),
                  writes=["ur"], dma=True)
            pr.do("sync", lambda e: e.dma_start(out=i64_s[:], in_=ctx["i64h"][:]),
                  writes=["i64"], dma=True)
            pr.do("sync", lambda e: e.dma_start(out=o3[:], in_=ctx["ones3_bf"][:]),
                  writes=["o3"], dma=True)
            st = dict(st)
            st.update(i64=i64_s, urep=urep_s, emask=emask_s, ones3=o3)

        x_dram = ctx["x_own"] if own else ctx["x_for"]
        hmask = ctx["halo_mask"] if own else ctx["halo_mask_f"]
        hmask_s = psl("hms", [128, 1])
        pr.do("sync", lambda e: e.dma_start(out=hmask_s[:], in_=hmask[:]),
              writes=[tagc + "hm"], dma=True)
        pr.do("vector", lambda e: e.memset(runT[:], 0.0), writes=["runT"])

        def foreign_ln(tg):
            rows = [(tg * TG + i * 128, 128, 3 + i * 128, 0) for i in range(4)]
            if tg == 0:
                rows = [(0, 3, 0, 0)] + [(r + 3, n, c, b) for (r, n, c, b) in rows]
            else:
                rows = [(r + 3, n, c, b) for (r, n, c, b) in rows]
            for (row, n, colo, b) in rows:
                kxt, kxn = "fxtk%d" % b, "fxnt%d" % b
                def ld(e, row=row, n=n, b=b):
                    return e.dma_start(out=xtok[b][0:n, :], in_=x_dram[row:row + n, :])
                pr.do("sync", ld, writes=[kxt], dma=True)
                def st1(e, n=n, b=b):
                    return e.activation(out=ldum[0:n, :], in_=xtok[b][0:n, :],
                                        func=AF.Copy, accum_out=lstat[0:n, 0:1])
                pr.do("scalar", st1, reads=[kxt], writes=["fst1", "fldum"])
                def st2(e, n=n, b=b):
                    return e.activation(out=ldum[0:n, :], in_=xtok[b][0:n, :],
                                        func=AF.Square, accum_out=lstat[0:n, 1:2])
                pr.do("scalar", st2, reads=[kxt], writes=["fst2", "fldum"])
                def s3(e, n=n):
                    return e.tensor_scalar(out=lstat[0:n, 2:3], in0=lstat[0:n, 0:1],
                                           scalar1=1.0 / D, scalar2=None,
                                           op0=OP.mult, op1=OP.bypass)
                pr.do("vector", s3, reads=["fst1"], writes=["fmu"])
                def s4(e, n=n):
                    return e.tensor_tensor(out=lstat[0:n, 3:4], in0=lstat[0:n, 2:3],
                                           in1=lstat[0:n, 2:3], op=OP.mult)
                pr.do("vector", s4, reads=["fmu"], writes=["fmu2"])
                def s5(e, n=n):
                    return e.scalar_tensor_tensor(out=lstat[0:n, 4:5], in0=lstat[0:n, 1:2],
                                                  scalar=1.0 / D, in1=lstat[0:n, 3:4],
                                                  op0=OP.mult, op1=OP.subtract)
                pr.do("vector", s5, reads=["fst2", "fmu2"], writes=["fvar"])
                def s6(e, n=n):
                    return e.activation(out=lstat[0:n, 5:6], in_=lstat[0:n, 4:5],
                                        func=AF.Sqrt, bias=consts["epscol"][0:n, :])
                pr.do("scalar", s6, reads=["fvar", "epscol"], writes=["fsd"])
                def s7(e, n=n):
                    return e.reciprocal(out=lstat[0:n, 6:7], in_=lstat[0:n, 5:6])
                pr.do("vector", s7, reads=["fsd"], writes=["frstd"])
                def s8(e, n=n, b=b):
                    return e.tensor_scalar(out=xnt[b][0:n, :], in0=xtok[b][0:n, :],
                                           scalar1=lstat[0:n, 2:3], scalar2=lstat[0:n, 6:7],
                                           op0=OP.subtract, op1=OP.mult)
                pr.do("vector", s8, reads=[kxt, "fmu", "frstd"], writes=[kxn])
                for jj in range(16):
                    pt = ptp[jj % 2]; kp = KT[jj % 2]
                    sl8 = (jj // 2) % 4
                    def tp(e, jj=jj, n=n, b=b, pt=pt, sl8=sl8):
                        return e.transpose(out=pt[:, sl8 * 128:sl8 * 128 + n],
                                           in_=xnt[b][0:n, jj * 128:(jj + 1) * 128],
                                           identity=consts["eye_bf"][0:n, 0:n])
                    pr.do("tensor", tp, reads=[kxn, "eye_bf"], writes=[kp])
                    def ev(e, jj=jj, n=n, colo=colo, pt=pt, sl8=sl8):
                        return e.copy(out=xnf[:, jj, colo:colo + n],
                                      in_=pt[:, sl8 * 128:sl8 * 128 + n])
                    pr.do("scalar", ev, reads=[kp], writes=["xnf"])

        def own_slab(tg, half):
            lo = tg * TG + half * 256
            def ld(e, lo=lo):
                return e.dma_start(
                    out=xnf[:], in_=ctx["xn_sp"][:, :, lo:lo + 259]
                    .rearrange("j p t -> p j t"))
            pr.do("sync", ld, writes=["xnf"], dma=True)

        NSL = 67 if own else 35

        for tg in range(NG):
            import os as _os2
            if not own and _os2.environ.get("KK_NOLN", "0") != "1":
                foreign_ln(tg)
            HALVES = 2 if own else 1
            HW = 256 if own else TG
            for half in range(HALVES):
                if own:
                    own_slab(tg, half)
                if tg == 0 and half == 0 and _os2.environ.get("KK_NOHALO", "0") != "1":
                    for sl in range(34):
                        b = sl % 2
                        def ldw(e, sl=sl, b=b):
                            return e.dma_start(
                                out=winb[b][:],
                                in_=st["w_in_bf"][:, sl * 128:(sl + 1) * 128]
                                .rearrange("(j p) c -> p j c", p=128))
                        pr.do("gpsimd", ldw, writes=[tagc + "winb%d" % b], dma=True)
                        for k in range(16):
                            def mm(e, k=k, sl=sl, b=b):
                                return e.matmul(out=pF[:, 0:3], lhsT=winb[b][:, k, :],
                                                rhs=xnf[:, k, 0:3],
                                                start=(k == 0), stop=(k == 15))
                            pr.do("tensor", mm, reads=[tagc + "winb%d" % b, "xnf"],
                                  writes=[KF])
                        def evh(e, sl=sl):
                            return e.tensor_scalar(out=tails[:, sl, :], in0=pF[:, 0:3],
                                                   scalar1=hmask_s[:], scalar2=None,
                                                   op0=OP.mult, op1=OP.bypass)
                        pr.do("vector", evh, reads=[KF, tagc + "hm"],
                              writes=["tails%d" % sl])
                for sl in range(NSL):
                    b = sl % 2
                    pj = pA if sl % 2 == 0 else pB
                    kp = KA if sl % 2 == 0 else KB
                    M_ = 64 if sl == 34 else 128
                    woff = (sl * 128 if sl < 34 else
                            (4352 if sl == 34 else 4416 + (sl - 35) * 128))
                    def ldw(e, woff=woff, M_=M_, b=b):
                        return e.dma_start(
                            out=winb[b][:, :, 0:M_],
                            in_=st["w_in_bf"][:, woff:woff + M_]
                            .rearrange("(j p) c -> p j c", p=128))
                    pr.do("gpsimd", ldw, writes=[tagc + "winb%d" % b], dma=True)
                    for k in range(16):
                        def mm(e, k=k, sl=sl, b=b, pj=pj, M_=M_, half=half, HW=HW):
                            return e.matmul(out=pj[0:M_, 0:HW],
                                            lhsT=winb[b][:, k, 0:M_],
                                            rhs=xnf[:, k, 3:3 + HW],
                                            start=(k == 0), stop=(k == 15))
                        pr.do("tensor", mm, reads=[tagc + "winb%d" % b, "xnf"],
                              writes=[kp])
                    ho = half * 256 if own else 0
                    if _os2.environ.get("KK_NOCONV", "0") == "1":
                        continue
                    if sl < 34:
                        xb = xc[0]; kxc = tagc + "xc0"
                        def cp0(e, sl=sl, xb=xb):
                            return e.tensor_copy(out=xb[:, 0:3], in_=tails[:, sl, :])
                        pr.do("vector", cp0, reads=["tails%d" % sl], writes=[kxc])
                        def cp1(e, xb=xb, pj=pj, HW=HW):
                            return e.copy(out=xb[:, 3:3 + HW], in_=pj[:, 0:HW])
                        pr.do("scalar", cp1, reads=[kp], writes=[kxc + "m"])
                        def tl(e, sl=sl, xb=xb, HW=HW):
                            return e.tensor_copy(out=tails[:, sl, :],
                                                 in_=xb[:, HW:HW + 3])
                        pr.do("vector", tl, reads=[kxc + "m"], writes=["tails%d" % sl])
                        def c0(e, sl=sl, xb=xb, HW=HW):
                            return e.tensor_scalar(out=t1s[:, 0:HW],
                                                   in0=xb[:, 0:HW],
                                                   scalar1=convw_col(st, sl, 0),
                                                   scalar2=st["convb"][:, sl:sl + 1],
                                                   op0=OP.mult, op1=OP.add)
                        pr.do("vector", c0, reads=[kxc, kxc + "m", "convw", "convb"],
                              writes=["cacc"])
                        for kk in range(1, 4):
                            def ck(e, sl=sl, kk=kk, xb=xb, HW=HW):
                                return e.scalar_tensor_tensor(
                                    out=t1s[:, 0:HW], in0=xb[:, kk:kk + HW],
                                    scalar=convw_col(st, sl, kk), in1=t1s[:, 0:HW],
                                    op0=OP.mult, op1=OP.add)
                            pr.do("vector", ck, reads=[kxc, kxc + "m", "convw", "cacc"],
                                  writes=["cacc"])
                        if sl < 32:
                            def si(e, sl=sl, ho=ho, HW=HW):
                                return e.activation(out=xfm[:, sl, ho:ho + HW],
                                                    in_=t1s[:, 0:HW], func=AF.Silu)
                            pr.do("scalar", si, reads=["cacc"],
                                  writes=[tagc + "xfm%d" % sl])
                        else:
                            def si(e, sl=sl, ho=ho, HW=HW):
                                return e.activation(out=bc[:, sl - 32, ho:ho + HW],
                                                    in_=t1s[:, 0:HW], func=AF.Silu)
                            pr.do("scalar", si, reads=["cacc"],
                                  writes=[tagc + "bc%d" % (sl - 32)])
                    elif sl == 34:
                        def evd(e, pj=pj, ho=ho, HW=HW):
                            return e.tensor_scalar(out=sp1[:, ho:ho + HW],
                                                   in0=pj[0:64, 0:HW],
                                                   scalar1=st["dtb_c"][:], scalar2=None,
                                                   op0=OP.add, op1=OP.bypass)
                        pr.do("vector", evd, reads=[kp, "dtb"], writes=["sp1"])
                    else:
                        gsl = sl - 35
                        def sg(e, pj=pj, HW=HW):
                            return e.activation(out=sgsl[:, 0:HW], in_=pj[:, 0:HW],
                                                func=AF.Silu)
                        pr.do("scalar", sg, reads=[kp], writes=["sgsl"])
                        def sst(e, gsl=gsl, tg=tg, ho=ho, HW=HW):
                            return e.dma_start(
                                out=ctx["sgate_sp"][gsl * 128:(gsl + 1) * 128,
                                                    tg * TG + ho:tg * TG + ho + HW],
                                in_=sgsl[:, 0:HW])
                        pr.do("scalar", sst, reads=["sgsl"], dma=True)

            import os as _os
            if int(_os.environ.get("KK_FRONT_LVL", "9")) < 1:
                continue
            tsl = slice(tg * TG, (tg + 1) * TG)
            pr.do("scalar", lambda e: e.activation(
                out=sp2[:], in_=sp1[:], func=AF.Abs), reads=["sp1"], writes=["sp2"])
            pr.do("scalar", lambda e: e.activation(
                out=sp2[:], in_=sp2[:], func=AF.Exp, scale=-1.0),
                reads=["sp2"], writes=["sp2"])
            pr.do("scalar", lambda e: e.activation(
                out=sp2[:], in_=sp2[:], func=AF.Ln, bias=1.0),
                reads=["sp2"], writes=["sp2"])
            pr.do("vector", lambda e: e.tensor_scalar(
                out=dtt[:], in0=sp1[:], scalar1=0.0, scalar2=None,
                op0=OP.max, op1=OP.bypass), reads=["sp1"], writes=["dtt"])
            pr.do("vector", lambda e: e.tensor_tensor(
                out=dtt[:], in0=dtt[:], in1=sp2[:], op=OP.add),
                reads=["dtt", "sp2"], writes=["dtt"])
            pr.do("vector", lambda e: e.tensor_copy(
                out=dtTb[:], in_=dtt[:]), reads=["dtt"], writes=["dtTb"])
            pr.do("vector", lambda e: e.tensor_scalar(
                out=sp1[:], in0=dtt[:], scalar1=st["nAcol"][:],
                scalar2=None, op0=OP.mult, op1=OP.bypass),
                reads=["dtt", "nA"], writes=["sp1"])
            for c4 in range(4):
                c = tg * 4 + c4
                csl = slice(c * Q, (c + 1) * Q)
                def scn(e, csl=csl, c4=c4):
                    return e.tensor_tensor_scan(
                        out=csT[:, csl], data0=st["ones64"][:, 0:Q],
                        data1=sp1[:, c4 * Q:(c4 + 1) * Q], initial=0.0,
                        op0=OP.mult, op1=OP.add)
                pr.do("vector", scn, reads=["sp1", "ones64"], writes=["csT"])
            if own:
                pr.do("vector", lambda e, tsl=tsl: e.tensor_copy(
                    out=csHI[:], in_=csT[:, tsl]), reads=["csT"], writes=["csHI"])
                pr.do("vector", lambda e, tsl=tsl: e.tensor_tensor(
                    out=sp2[:], in0=csT[:, tsl], in1=csHI[:], op=OP.subtract),
                    reads=["csT", "csHI"], writes=["sp2"])
                pr.do("vector", lambda e: e.tensor_copy(
                    out=csMID[:], in_=sp2[:]), reads=["sp2"], writes=["csMID"])
                pr.do("vector", lambda e: e.tensor_tensor(
                    out=sp2[:], in0=sp2[:], in1=csMID[:], op=OP.subtract),
                    reads=["sp2", "csMID"], writes=["sp2"])
                pr.do("vector", lambda e: e.tensor_copy(
                    out=csLOp[:], in_=sp2[:]), reads=["sp2"], writes=["csLOp"])
                pr.do("vector", lambda e: e.tensor_scalar(
                    out=ncsLO[:], in0=sp2[:], scalar1=-1.0, scalar2=None,
                    op0=OP.mult, op1=OP.bypass), reads=["sp2"], writes=["ncsLO"])
                for r, buf, kk in ((0, csHI, "csHI"), (1, csMID, "csMID"),
                                   (2, csLOp, "csLOp")):
                    def bn(e, r=r, buf=buf, tsl=tsl):
                        return e.dma_start(out=ctx["csrows"][r, :, tsl], in_=buf[:])
                    pr.do("scalar", bn, reads=[kk], writes=["csrows"], dma=True)
                pr.do("vector", lambda e: e.tensor_scalar(
                    out=nhi[:], in0=csHI[:], scalar1=-1.0, scalar2=None,
                    op0=OP.mult, op1=OP.bypass), reads=["csHI"], writes=["nhi"])
                pr.do("vector", lambda e: e.tensor_scalar(
                    out=nmid[:], in0=csMID[:], scalar1=-1.0, scalar2=None,
                    op0=OP.mult, op1=OP.bypass), reads=["csMID"], writes=["nmid"])
                for (dsta, srca, kk) in ((0, (nhi, 0), "nhi"), (32, (nmid, 0), "nmid"),
                                         (64, (nhi, 32), "nhi"), (96, (nmid, 32), "nmid")):
                    def mv(e, dsta=dsta, srca=srca):
                        buf, s0 = srca
                        return e.dma_start(out=lhsB[dsta:dsta + 32, :],
                                           in_=buf[s0:s0 + 32, :])
                    pr.do("scalar", mv, reads=[kk], writes=["lhsB"], dma=True)
                pr.do("scalar", lambda e, tsl=tsl: e.activation(
                    out=ecsfm[:], in_=csT[:, tsl], func=AF.Exp),
                    reads=["csT"], writes=["ecsfm"])

            import os as _os
            lvl = int(_os.environ.get("KK_FRONT_LVL", "9"))
            if lvl >= 2:
                for c4 in range(4):
                    c = tg * 4 + c4
                    emit_chunk(ctx, own, c, c4, locals())

        dst_logD = ologD if own else flogD
        pr.do("vector", lambda e: e.tensor_copy(out=dst_logD[:], in_=runT[:]),
              reads=["runT"], writes=["ologD" if own else "flogD"])
        if own:
            od = ctx["dbg"]("dtT", [64, T], BF16)
            if od is not None:
                pr.do("sync", lambda e, od=od: e.dma_start(out=od[:], in_=dtTb[:]),
                      reads=["dtTb"], dma=True)
            od = ctx["dbg"]("csT", [64, T])
            if od is not None:
                pr.do("sync", lambda e, od=od: e.dma_start(out=od[:], in_=csT[:]),
                      reads=["csT"], dma=True)
        od = ctx["dbg"]("P_%s" % tagc, [128, DIN])
        if od is not None:
            pr.do("sync", lambda e, od=od: e.dma_start(out=od[:], in_=P_state[:]),
                  reads=["P_state"], dma=True)
        od = ctx["dbg"]("cs_%s" % tagc, [64, T])
        if od is not None:
            pr.do("sync", lambda e, od=od: e.dma_start(out=od[:], in_=csT[:]),
                  reads=["csT"], dma=True)
        od = ctx["dbg"]("bc_%s" % tagc, [128, 2, TG], BF16)
        if od is not None:
            pr.do("sync", lambda e, od=od: e.dma_start(out=od[:], in_=bc[:]),
                  reads=[tagc + "bc0", tagc + "bc1"], dma=True)
        od = ctx["dbg"]("xfm_%s" % tagc, [128, 32, TG], BF16)
        if od is not None:
            pr.do("sync", lambda e, od=od: e.dma_start(out=od[:], in_=xfm[:]),
                  reads=[tagc + "xfm0", tagc + "xfm31"], dma=True)
        od = ctx["dbg"]("xdt_%s" % tagc, [128, DIN], BF16)
        if od is not None:
            pr.do("sync", lambda e, od=od: e.dma_start(out=od[:], in_=xdt_tok[:]),
                  reads=["xdtok"], dma=True)
        pr.barrier()


def emit_chunk(ctx, own, c, c4, L):
    nc = ctx["nc"]; pr = ctx["pr"]; consts = ctx["consts"]
    st = L["st"]; tagc = L["tagc"]
    csT = L["csT"]; dtTb = L["dtTb"]
    xfm = L["xfm"]; bc = L["bc"]
    ptp = L["ptp"]; KT = L["KT"]
    pA, pB, pC, pD, pE, pF = L["pA"], L["pB"], L["pC"], L["pD"], L["pE"], L["pF"]
    KA, KB, KC_, KD, KE, KF = L["KA"], L["KB"], L["KC_"], L["KD"], L["KE"], L["KF"]
    xdt_tok = L["xdt_tok"]; dt_tok = L["dt_tok"]; dec_tok = L["dec_tok"]
    decfm = L["decfm"]; xdg = L["xdg"]; B_tok = L["B_tok"]; Pbg = L["Pbg"]
    t1s = L["t1s"]; runT = L["runT"]; cdc = L["cdc"]; cdrow = L["cdrow"]
    cdrow4 = L["cdrow4"]
    P_state = L["P_state"]; dpre_keep = L["dpre_keep"]
    kxd = "xdtok"
    csl = slice(c * Q, (c + 1) * Q)
    cend = c * Q + Q - 1

    if own:
        pr.do("vector", lambda e: e.tensor_copy(
            out=L["C_keep"][:, csl], in_=bc[:, 1, c4 * Q:(c4 + 1) * Q]),
            reads=[tagc + "bc1"], writes=["C_keep"])
        pr.do("vector", lambda e: e.tensor_copy(
            out=dpre_keep[:, c:c + 1], in_=runT[:]), reads=["runT"], writes=["dpre"])
    pr.do("vector", lambda e: e.tensor_tensor(
        out=runT[:], in0=runT[:], in1=csT[:, cend:cend + 1], op=OP.add),
        reads=["runT", "csT"], writes=["runT"])
    pr.do("scalar", lambda e: e.activation(out=cdc[:], in_=csT[:, cend:cend + 1],
                                           func=AF.Exp), reads=["csT"], writes=["cdc"])
    pr.do("vector", lambda e: e.tensor_copy(out=dec_tok[0:64, 0:1], in_=cdc[:]),
          reads=["cdc"], writes=["cdcb"])
    def cdt(e):
        return e.transpose(out=ptp[0][0:1, 0:64], in_=dec_tok[0:64, 0:1],
                           identity=consts["eye_bf"][0:64, 0:64])
    pr.do("tensor", cdt, reads=["cdcb", "eye_bf"], writes=[KT[0]])
    pr.do("vector", lambda e: e.tensor_copy(out=cdrow[:], in_=ptp[0][0:1, 0:64]),
          reads=[KT[0]], writes=["cdrow"])

    pr.do("scalar", lambda e: e.activation(
        out=decfm[:], in_=csT[:, csl], func=AF.Exp, scale=-1.0,
        bias=csT[:, cend:cend + 1]), reads=["csT"], writes=["decfm"])
    csl4 = slice(c4 * Q, (c4 + 1) * Q)
    def tpd(e):
        return e.transpose(out=ptp[0][0:128, 0:64], in_=dtTb[:, csl4],
                           identity=consts["eye_bf"][0:64, 0:64])
    pr.do("tensor", tpd, reads=["dtTb", "eye_bf"], writes=[KT[0]])
    pr.do("vector", lambda e: e.tensor_copy(out=dt_tok[:], in_=ptp[0][0:128, 0:64]),
          reads=[KT[0]], writes=["dt_tok"])
    def tpe(e):
        return e.transpose(out=ptp[0][0:128, 0:64], in_=decfm[:],
                           identity=consts["eye_bf"][0:64, 0:64])
    pr.do("tensor", tpe, reads=["decfm", "eye_bf"], writes=[KT[0]])
    pr.do("vector", lambda e: e.tensor_copy(out=dec_tok[:], in_=ptp[0][0:128, 0:64]),
          reads=[KT[0]], writes=["dec_tok"])
    if own:
        def tpc(e):
            return e.transpose(out=ptp[0][0:128, 0:64], in_=L["ecsfm"][:, c4 * Q:(c4 + 1) * Q],
                               identity=consts["eye_bf"][0:64, 0:64])
        pr.do("tensor", tpc, reads=["ecsfm", "eye_bf"], writes=[KT[0]])
        pr.do("vector", lambda e: e.tensor_copy(out=L["ecs_tok"][:], in_=ptp[0][0:128, 0:64]),
              reads=[KT[0]], writes=["ecs_tok"])
    def tpb(e):
        return e.transpose(out=ptp[0][0:128, 64:192], in_=bc[:, 0, c4 * Q:(c4 + 1) * Q],
                           identity=consts["eye_bf"][:])
    pr.do("tensor", tpb, reads=[tagc + "bc0", "eye_bf"], writes=[KT[0]])
    pr.do("vector", lambda e: e.tensor_copy(out=B_tok[:], in_=ptp[0][0:128, 64:192]),
          reads=[KT[0]], writes=["B_tok"])

    for half in range(4):
        kp = KT[1]
        for j8 in range(8):
            j = half * 8 + j8
            def tp(e, j=j, j8=j8):
                return e.transpose(out=ptp[1][:, j8 * 128:(j8 + 1) * 128],
                                   in_=xfm[:, j, c4 * Q:(c4 + 1) * Q],
                                   identity=consts["eye_bf"][:])
            pr.do("tensor", tp, reads=[tagc + "xfm%d" % j, "eye_bf"], writes=[kp])
        def ev(e, half=half):
            return e.tensor_tensor(
                out=xdt_tok[:, half * 1024:(half + 1) * 1024]
                .rearrange("p (h q) -> p h q", h=16),
                in0=ptp[1][:].rearrange("p (h q) -> p h q", h=16),
                in1=dt_tok[:, half * 16:(half + 1) * 16]
                .unsqueeze(2).broadcast_to((128, 16, 64)),
                op=OP.mult)
        pr.do("vector", ev, reads=[kp, "dt_tok"], writes=[kxd])

    if own:
        emit_chunk_own(ctx, c, c4, L)

    import os as _os
    if int(_os.environ.get("KK_FRONT_LVL", "9")) < 3:
        return
    for g in range(8):
        pS = pE if g % 2 == 0 else pF
        kS = KE if g % 2 == 0 else KF
        def xdd(e, g=g):
            return e.tensor_tensor(
                out=xdg[:].rearrange("p (h q) -> p h q", h=8),
                in0=xdt_tok[:, g * 512:(g + 1) * 512]
                .rearrange("p (h q) -> p h q", h=8),
                in1=dec_tok[:, g * 8:(g + 1) * 8].unsqueeze(2)
                .broadcast_to((128, 8, 64)),
                op=OP.mult)
        pr.do("vector", xdd, reads=[kxd, "dec_tok"], writes=["xdg"])
        def smm(e, g=g, pS=pS):
            return e.matmul(out=pS[:], lhsT=B_tok[:], rhs=xdg[:],
                            start=True, stop=True)
        pr.do("tensor", smm, reads=["B_tok", "xdg"], writes=[kS])
        pC2 = pF if g % 2 == 0 else pE
        kC2 = KF if g % 2 == 0 else KE
        def cdx(e, g=g):
            return e.tensor_copy(
                out=cdrow4[:].rearrange("o (h q) -> o h q", h=8),
                in_=cdrow[:, g * 8:(g + 1) * 8].unsqueeze(2)
                .broadcast_to((1, 8, 64)))
        pr.do("gpsimd", cdx, reads=["cdrow"], writes=["cdrow4"])
        def cbm(e, g=g, pC2=pC2):
            return e.matmul(out=pC2[:], lhsT=consts["ones1_bf"][:],
                            rhs=cdrow4[:], start=True, stop=True)
        pr.do("tensor", cbm, reads=["ones1bf", "cdrow4"], writes=[kC2])
        def pu1(e, g=g, pC2=pC2):
            return e.tensor_tensor(out=t1s[:], in0=P_state[:, g * 512:(g + 1) * 512],
                                   in1=pC2[:], op=OP.mult)
        pr.do("vector", pu1, reads=["P_state", kC2], writes=["t1s"])
        def pu2(e, g=g, pS=pS):
            return e.tensor_tensor(out=P_state[:, g * 512:(g + 1) * 512],
                                   in0=t1s[:], in1=pS[:], op=OP.add)
        pr.do("vector", pu2, reads=["t1s", kS], writes=["P_state"])




def emit_chunk_own(ctx, c, c4, L):
    nc = ctx["nc"]; pr = ctx["pr"]; consts = ctx["consts"]
    st = L["st"]; tagc = L["tagc"]
    csT = L["csT"]; xfm = L["xfm"]; bc = L["bc"]
    ptp = L["ptp"]; KT = L["KT"]
    pA, pB, pC, pD, pE, pF = L["pA"], L["pB"], L["pC"], L["pD"], L["pE"], L["pF"]
    KA, KB, KC_, KD, KE, KF = L["KA"], L["KB"], L["KC_"], L["KD"], L["KE"], L["KF"]
    xdt_tok = L["xdtok"] if False else L["xdt_tok"]
    Pbg = L["Pbg"]; P_state = L["P_state"]
    kxd = "xdtok"
    csl = slice(c * Q, (c + 1) * Q)
    Msl = L["Mslab"]; kM = "Msl"
    Gs = L["Gs"]
    def gmm(e):
        return e.matmul(out=pF[:, 0:128], lhsT=bc[:, 0, c4 * Q:(c4 + 1) * Q],
                        rhs=bc[:, 1, c4 * Q:(c4 + 1) * Q], start=True, stop=True)
    pr.do("tensor", gmm, reads=[tagc + "bc0", tagc + "bc1"], writes=[KF])
    pr.do("scalar", lambda e: e.copy(out=Gs[:], in_=pF[:, 0:128]),
          reads=[KF], writes=["Gs"])
    csl4 = slice(c4 * Q, (c4 + 1) * Q)
    for dg in range(16):
        if dg % 4 == 0:
            hh0 = (dg // 4) * 16
            def lda(e, hh0=hh0):
                return e.dma_start(out=L["rhsA"][:],
                                   in_=ctx["csrows"][:, hh0:hh0 + 16, csl])
            pr.do("sync", lda, reads=["csrows"], writes=["rhsA"], dma=True)
        pDx = pA if dg % 2 == 0 else pB
        kDx = KA if dg % 2 == 0 else KB
        gsl = slice(dg * 512, (dg + 1) * 512)
        gsl8 = slice((dg % 4) * 512, (dg % 4 + 1) * 512)
        gslH = slice((dg % 8) * 512, (dg % 8 + 1) * 512)
        def mA2(e, dg=dg, pDx=pDx, gsl8=gsl8):
            return e.matmul(out=pDx[:], lhsT=st["ones3"][:],
                            rhs=L["rhsA"][:].rearrange("r h t -> r (h t)")[:, gsl8],
                            start=True, stop=False)
        pr.do("tensor", mA2, reads=["rhsA", "o3"], writes=[kDx])
        hb = 0 if dg < 8 else 64
        lob = 0 if dg < 8 else 32
        def mB1(e, dg=dg, pDx=pDx, gslH=gslH, hb=hb):
            return e.matmul(out=pDx[:], lhsT=L["lhsB"][hb:hb + 64, csl4],
                            rhs=st["i64"][hb:hb + 64, gslH], start=False, stop=False)
        pr.do("tensor", mB1, reads=["lhsB", "i64"], writes=[kDx])
        def mC(e, dg=dg, pDx=pDx, gslH=gslH, lob=lob):
            return e.matmul(out=pDx[:], lhsT=L["ncsLO"][lob:lob + 32, csl4],
                            rhs=st["i64"][lob:lob + 32, gslH], start=False, stop=False)
        pr.do("tensor", mC, reads=["ncsLO", "i64"], writes=[kDx])
        def mD(e, dg=dg, pDx=pDx):
            return e.matmul(out=pDx[:], lhsT=st["emask"][:],
                            rhs=st["urep"][:], start=False, stop=True)
        pr.do("tensor", mD, reads=["em", "ur"], writes=[kDx])
        hof = (dg % 8) * 512
        def ex(e, pDx=pDx, hof=hof):
            return e.activation(out=Msl[:, hof:hof + 512], in_=pDx[:], func=AF.Exp)
        pr.do("scalar", ex, reads=[kDx], writes=[kM + "%d" % (dg % 8)])
        def mul(e, dg=dg, hof=hof):
            return e.tensor_tensor(
                out=Msl[:, hof:hof + 512].rearrange("p (h t) -> p h t", h=4),
                in0=Msl[:, hof:hof + 512].rearrange("p (h t) -> p h t", h=4),
                in1=Gs[:].unsqueeze(1).broadcast_to((128, 4, 128)),
                op=OP.mult)
        pr.do("vector", mul, reads=[kM + "%d" % (dg % 8), "Gs"],
              writes=[kM + "%d" % (dg % 8)])
        if dg % 8 == 7:
            hhalf = dg // 8
            for gy in range(4):
                g = hhalf * 4 + gy
                pY = pC if gy % 2 == 0 else pD
                kY = KC_ if gy % 2 == 0 else KD
                for jj in range(4):
                    j = g * 4 + jj
                    def md(e, j=j, jj=jj, pY=pY):
                        return e.matmul(out=pY[:, jj * 128:(jj + 1) * 128],
                                        lhsT=xfm[:, j, c4 * Q:(c4 + 1) * Q],
                                        rhs=st["Ddiag"][:, j, :],
                                        start=(jj == 0), stop=False)
                    pr.do("tensor", md, reads=[tagc + "xfm%d" % j, "Ddiag"],
                          writes=[kY])
                for hh in range(8):
                    h = g * 8 + hh
                    hloc = h - hhalf * 32
                    def mh(e, h=h, hh=hh, hloc=hloc, pY=pY):
                        return e.matmul(out=pY[:, hh * 64:(hh + 1) * 64],
                                        lhsT=Msl[:, hloc * Q:(hloc + 1) * Q],
                                        rhs=xdt_tok[:, h * 64:(h + 1) * 64],
                                        start=False, stop=(hh == 7))
                    pr.do("tensor", mh,
                          reads=[kM + "%d" % (hloc // 4), kxd], writes=[kY])
                def pbc2(e, g=g):
                    return e.tensor_copy(out=Pbg[:], in_=P_state[:, g * 512:(g + 1) * 512])
                pr.do("vector", pbc2, reads=["P_state"], writes=["Pbg"])
                def mo(e, g=g):
                    return e.matmul(out=pF[:], lhsT=L["C_keep"][:, csl],
                                    rhs=Pbg[:], start=True, stop=True)
                pr.do("tensor", mo, reads=["C_keep", "Pbg"], writes=[KF])
                pr.do("scalar", lambda e: e.copy(out=L["yo"][:], in_=pF[:]),
                      reads=[KF], writes=["yo"])
                def sc(e, g=g):
                    return e.tensor_tensor(
                        out=L["yos"][:].rearrange("p (h q) -> p h q", h=8),
                        in0=L["yo"][:].rearrange("p (h q) -> p h q", h=8),
                        in1=L["ecs_tok"][:, g * 8:(g + 1) * 8]
                        .unsqueeze(2).broadcast_to((128, 8, 64)), op=OP.mult)
                pr.do("gpsimd", sc, reads=["yo", "ecs_tok"], writes=["yos"])
                def fin(e, g=g, pY=pY):
                    return e.tensor_tensor(out=L["ysl2"][:],
                                           in0=pY[:], in1=L["yos"][:], op=OP.add)
                pr.do("vector", fin, reads=[kY, "yos"], writes=["ysl2"])
                def sp(e, g=g, c=c):
                    return e.dma_start(
                        out=ctx["ymain_sp"][c * Q:(c + 1) * Q,
                                            g * 512:(g + 1) * 512],
                        in_=L["ysl2"][:])
                pr.do("scalar", sp, reads=["ysl2"], dma=True)
def emit_exchange_and_tail(ctx, x_lat, P_state, fO, flogD, ologD,
                           C_keep, cs_keep, dpre_keep, rstd_res,
                           onesb1, onescol_bf, zcol):
    nc = ctx["nc"]; pr = ctx["pr"]; sb = ctx["sb"]; consts = ctx["consts"]

    Pinb = ctx["Pinb"]
    with ExitStack() as PH1:
        def psl(name, shape, dt=F32):
            return PH1.enter_context(nc.sbuf_tensor(name, list(shape), dt))
        def pp(name, shape, dt=F32):
            return PH1.enter_context(nc.psum_tensor(name, list(shape), dt))
        pA = pp("x1pA", [128, 512]); pB = pp("x1pB", [128, 512])
        KA, KB = "P$pA", "P$pB"

        sendb = psl("sendb", [128, 4104])
        recvb = psl("recvb", [128, 4104])
        P_in = psl("P_in", [128, DIN])
        Pmid = psl("Pmid", [128, DIN])
        lrow = psl("lrow", [1, 64])
        lrow4 = psl("lrow4", [1, DIN])
        elcol = psl("elcol", [64, 1])
        masks = psl("masks", [128, 4])
        nonce_s = psl("nonce_s", [1, 4])
        nonce128 = psl("nonce128", [128, 4])
        widx_s = psl("widx_s", [128, 1], U32)
        ridx_s = psl("ridx_s", [128, 1], U32)
        t2s = psl("t2s", [128, 512])

        pr.do("sync", lambda e: e.dma_start(out=masks[:], in_=ctx["rolemask"][:]),
              writes=["masks"], dma=True)
        pr.do("sync", lambda e: e.dma_start(out=nonce_s[:], in_=ctx["nonce"][:]),
              writes=["nonce_s"], dma=True)
        pr.do("sync", lambda e: e.dma_start(
            out=nonce128[:], in_=ctx["nonce"][:].broadcast_to((128, 4))),
            writes=["nonce128"], dma=True)
        pr.do("sync", lambda e: e.dma_start(out=widx_s[:], in_=ctx["widx"][:]),
              writes=["widx_s"], dma=True)
        pr.do("sync", lambda e: e.dma_start(out=ridx_s[:], in_=ctx["ridx"][:]),
              writes=["ridx_s"], dma=True)

        pr.do("vector", lambda e: e.memset(sendb[:], 0.0), writes=["sendb"])
        for g in range(8):
            gs = slice(g * 512, (g + 1) * 512)
            def s1(e, gs=gs):
                return e.tensor_tensor(out=sendb[:, gs], in0=fO[:, gs],
                                       in1=P_state[:, gs], op=OP.subtract)
            pr.do("vector", s1, reads=["fO", "P_state"], writes=["sendb"])
            def s2(e, gs=gs):
                return e.scalar_tensor_tensor(out=sendb[:, gs], in0=sendb[:, gs],
                                              scalar=masks[:, 0:1], in1=P_state[:, gs],
                                              op0=OP.mult, op1=OP.add)
            pr.do("vector", s2, reads=["sendb", "masks", "P_state"], writes=["sendb"])
        def sl1(e):
            return e.tensor_tensor(out=sendb[0:64, 4096:4097], in0=flogD[:],
                                   in1=ologD[:], op=OP.subtract)
        pr.do("vector", sl1, reads=["flogD", "ologD"], writes=["sendb"])
        def sl2(e):
            return e.scalar_tensor_tensor(out=sendb[0:64, 4096:4097],
                                          in0=sendb[0:64, 4096:4097],
                                          scalar=masks[0:64, 0:1], in1=ologD[:],
                                          op0=OP.mult, op1=OP.add)
        pr.do("vector", sl2, reads=["sendb", "masks", "ologD"], writes=["sendb"])

        import os as _os
        SIMEXCH = _os.environ.get("KK_SIMEXCH", "0") == "1"
        if SIMEXCH:
            pr.do("vector", lambda e: e.memset(recvb[:], 0.0), writes=["recvb"])

        def exch_round(r, src, dst, kdst):
            if SIMEXCH:
                pr.do("vector", lambda e, dst=dst: e.memset(dst[:], 0.0),
                      writes=[kdst])
                return
            def sc(e, r=r, src=src):
                return e.indirect_dma_start(
                    out=ctx["exch"],
                    out_offset=IndirectOffsetOnAxis(ap=widx_s[:], axis=0),
                    in_=src[:], in_offset=None, element_offset=r * 4104)
            pr.do("gpsimd", sc, reads=["sendb", "widx_s"], dma=True)
            def fl(e, r=r):
                return e.indirect_dma_start(
                    out=ctx["exfl"],
                    out_offset=IndirectOffsetOnAxis(ap=widx_s[:], axis=0),
                    in_=nonce128[:, 2 * r:2 * r + 2], in_offset=None,
                    element_offset=2 * r)
            pr.do("gpsimd", fl, reads=["nonce128", "widx_s"], dma=True)

            def poll(g, r=r):
                rf = g.alloc_register("rf_%d" % r)
                rf1 = g.alloc_register("rf1_%d" % r)
                rn = g.alloc_register("rn_%d" % r)
                cc = g.alloc_register("cc_%d" % r)
                c1 = g.alloc_register("c1_%d" % r)
                g.reg_load(rn, nonce_s[0:1, 2 * r:2 * r + 1].bitcast(I32))
                g.reg_mov(cc, 1)
                with g.While(cc):
                    g.nop(cycle_cnt=20000)
                    g.load(rf, ctx["exfl"][0:1, 2 * r:2 * r + 1].bitcast(I32))
                    g.load(rf1, ctx["exfl"][128:129, 2 * r:2 * r + 1].bitcast(I32))
                    g.reg_alu(cc, rf, rn, OP.bitwise_xor)
                    g.reg_alu(c1, rf1, rn, OP.bitwise_xor)
                    g.reg_alu(cc, cc, c1, OP.bitwise_or)
            import os as _os
            if _os.environ.get("KK_NOPOLL", "0") == "1":
                def poll(g):
                    g.nop(cycle_cnt=2000000)
            pr.raw("gpsimd", poll)
            pr.do("gpsimd", lambda e: e.nop(), writes=["polled%d" % r])
            def ga(e, r=r, dst=dst):
                return e.indirect_dma_start(
                    out=dst[:], out_offset=None, in_=ctx["exch"],
                    in_offset=IndirectOffsetOnAxis(ap=ridx_s[:], axis=0),
                    element_offset=r * 4104)
            pr.do("gpsimd", ga, reads=["polled%d" % r, "ridx_s"], writes=[kdst], dma=True)

        exch_round(0, sendb, recvb, "recvb")

        pr.do("scalar", lambda e: e.activation(out=elcol[:], in_=recvb[0:64, 4096:4097],
                                               func=AF.Exp), reads=["recvb"], writes=["elcol"])
        def elt(e):
            return e.transpose(out=pA[0:1, 0:64], in_=elcol[:],
                               identity=consts["eye_f32"][0:64, 0:64])
        pr.do("tensor", elt, reads=["elcol", "eye_f32"], writes=[KA])
        pr.do("vector", lambda e: e.tensor_copy(out=lrow[:], in_=pA[0:1, 0:64]),
              reads=[KA], writes=["lrow"])
        def eld(e):
            return e.tensor_copy(
                out=lrow4[:].rearrange("o (h q) -> o h q", h=64),
                in_=lrow[:].unsqueeze(2).broadcast_to((1, 64, 64)))
        pr.do("gpsimd", eld, reads=["lrow"], writes=["lrow4"])
        for g in range(8):
            gs = slice(g * 512, (g + 1) * 512)
            pX = pA if g % 2 == 0 else pB
            kX = KA if g % 2 == 0 else KB
            def bm(e, g=g, pX=pX):
                return e.matmul(out=pX[:], lhsT=consts["ones1_f32"][:],
                                rhs=lrow4[:, g * 512:(g + 1) * 512],
                                start=True, stop=True)
            pr.do("tensor", bm, reads=["ones1", "lrow4"], writes=[kX])
            def pm(e, gs=gs, pX=pX):
                return e.tensor_tensor(out=Pmid[:, gs], in0=fO[:, gs], in1=pX[:],
                                       op=OP.mult)
            pr.do("vector", pm, reads=["fO", kX], writes=["Pmid"])
            def pm2(e, gs=gs):
                return e.tensor_tensor(out=Pmid[:, gs], in0=Pmid[:, gs],
                                       in1=recvb[:, gs], op=OP.add)
            pr.do("vector", pm2, reads=["Pmid", "recvb"], writes=["Pmid"])

        pr.do("scalar", lambda e: e.activation(out=elcol[:], in_=ologD[:],
                                               func=AF.Exp), reads=["ologD"], writes=["elcol"])
        pr.do("tensor", elt, reads=["elcol", "eye_f32"], writes=[KA])
        pr.do("vector", lambda e: e.tensor_copy(out=lrow[:], in_=pA[0:1, 0:64]),
              reads=[KA], writes=["lrow"])
        pr.do("gpsimd", eld, reads=["lrow"], writes=["lrow4"])
        for g in range(8):
            gs = slice(g * 512, (g + 1) * 512)
            pX = pA if g % 2 == 0 else pB
            kX = KA if g % 2 == 0 else KB
            def bm(e, g=g, pX=pX):
                return e.matmul(out=pX[:], lhsT=consts["ones1_f32"][:],
                                rhs=lrow4[:, g * 512:(g + 1) * 512],
                                start=True, stop=True)
            pr.do("tensor", bm, reads=["ones1", "lrow4"], writes=[kX])
            def sb1(e, gs=gs, pX=pX):
                return e.tensor_tensor(out=sendb[:, gs], in0=Pmid[:, gs], in1=pX[:],
                                       op=OP.mult)
            pr.do("vector", sb1, reads=["Pmid", kX], writes=["sendb"])
            def sb2(e, gs=gs):
                return e.tensor_tensor(out=sendb[:, gs], in0=sendb[:, gs],
                                       in1=P_state[:, gs], op=OP.add)
            pr.do("vector", sb2, reads=["sendb", "P_state"], writes=["sendb"])

        exch_round(1, sendb, recvb, "recvb2")

        def ga1(e):
            return e.indirect_dma_start(
                out=Pmid[:] if False else recvb[:], out_offset=None, in_=ctx["exch"],
                in_offset=IndirectOffsetOnAxis(ap=ridx_s[:], axis=0),
                element_offset=0)
        recvb2 = psl("recvb2", [128, 4104])
        def ga2(e):
            return e.indirect_dma_start(
                out=recvb2[:], out_offset=None, in_=ctx["exch"],
                in_offset=IndirectOffsetOnAxis(ap=ridx_s[:], axis=0),
                element_offset=0 * 4104)
        if SIMEXCH:
            pr.do("vector", lambda e: e.memset(recvb2[:], 0.0), writes=["recvbA"])
        else:
            pr.do("gpsimd", ga2, reads=["polled0", "ridx_s"], writes=["recvbA"], dma=True)
        for g in range(8):
            gs = slice(g * 512, (g + 1) * 512)
            def f1(e, gs=gs):
                return e.tensor_scalar(out=P_in[:, gs], in0=recvb2[:, gs],
                                       scalar1=masks[:, 1:2], scalar2=None,
                                       op0=OP.mult, op1=OP.bypass)
            pr.do("vector", f1, reads=["recvbA", "masks"], writes=["P_in"])
            def f2(e, gs=gs):
                return e.scalar_tensor_tensor(out=P_in[:, gs], in0=Pmid[:, gs],
                                              scalar=masks[:, 2:3], in1=P_in[:, gs],
                                              op0=OP.mult, op1=OP.add)
            pr.do("vector", f2, reads=["Pmid", "masks", "P_in"], writes=["P_in"])
            def f3(e, gs=gs):
                return e.scalar_tensor_tensor(out=P_in[:, gs], in0=recvb[:, gs],
                                              scalar=masks[:, 3:4], in1=P_in[:, gs],
                                              op0=OP.mult, op1=OP.add)
            pr.do("vector", f3, reads=["recvb2" if False else "recvb", "masks", "P_in"],
                  writes=["P_in"])
        pr.do("vector", lambda e: e.tensor_copy(out=Pinb[:], in_=P_in[:]),
              reads=["P_in"], writes=["Pinb"])

        pr.barrier()
        if ctx["stop_after"] == "pexch":
            od = ctx["dbg"]("P_in", [128, DIN])
            if od is not None:
                pr.do("sync", lambda e, od=od: e.dma_start(out=od[:], in_=P_in[:]),
                      reads=["P_in"], dma=True)
            pr.barrier()
    if ctx["stop_after"] == "pexch":
        ctx["att_es"].close()
        ctx["keep_es"].close()
        _finish_zero_out(ctx)
        return
    with ExitStack() as PH1X:
        def psl(name, shape, dt=F32):
            return PH1X.enter_context(nc.sbuf_tensor(name, list(shape), dt))
        def pp(name, shape, dt=F32):
            return PH1X.enter_context(nc.psum_tensor(name, list(shape), dt))

    ctx["att_es"].close()
    with ExitStack() as PH:
        def psl(name, shape, dt=F32):
            return PH.enter_context(nc.sbuf_tensor(name, list(shape), dt))
        def pp(name, shape, dt=F32):
            return PH.enter_context(nc.psum_tensor(name, list(shape), dt))
        hgn = psl("hgn", [128, 32, T], BF16)
        pA = pp("xpA", [128, 512]); pB = pp("xpB", [128, 512])
        pC = pp("xpC", [128, 512]); pD = pp("xpD", [128, 512])
        ptp = [pp("xptp%d" % i, [128, 1024], BF16) for i in range(2)]
        pvar = pp("xpvar", [128, 1])
        KA, KB, KC_, KD = "P$xpA", "P$xpB", "P$xpC", "P$xpD"
        KT = ["P$xptp0", "P$xptp1"]
        CORR = ExitStack()
        def pslc(name, shape, dt=F32):
            return CORR.enter_context(nc.sbuf_tensor(name, list(shape), dt))
        ecs2 = pslc("ecs2", [64, Q], BF16)
        ecs2t = pslc("ecs2t", [128, 64], BF16)
        yo2 = pslc("yo2", [128, 512], BF16)
        yos2 = pslc("yos2", [128, 512], BF16)
        ysl = [pslc("ysl0", [128, DIN], BF16)]
        ysl.append(ysl[0])
        ymr = [pslc("ymr0", [128, DIN], BF16)]
        ymr.append(ymr[0])
        sgr = pslc("sgr", [128, 32, Q], BF16)
        sqv = pslc("sqv", [128, 32, Q], BF16)
        for c in range(NCH):
            csl = slice(c * Q, (c + 1) * Q)
            cb = c % 2
            def e2(e, c=c, csl=csl):
                return e.activation(out=ecs2[:], in_=cs_keep[:, csl], func=AF.Exp,
                                    bias=dpre_keep[:, c:c + 1])
            pr.do("scalar", e2, reads=["cs_keep", "dpre"], writes=["ecs2"])
            def e2t(e):
                return e.transpose(out=ptp[0][0:128, 0:64], in_=ecs2[:],
                                   identity=consts["eye_bf"][0:64, 0:64])
            pr.do("tensor", e2t, reads=["ecs2", "eye_bf"], writes=[KT[0]])
            pr.do("vector", lambda e: e.tensor_copy(out=ecs2t[:], in_=ptp[0][0:128, 0:64]),
                  reads=[KT[0]], writes=["ecs2t"])
            def ldy(e, c=c, cb=cb):
                return e.dma_start(out=ymr[cb][:], in_=ctx["ymain_sp"][c * Q:(c + 1) * Q, :])
            pr.do("sync", ldy, writes=["ymr0"], dma=True)
            for g in range(8):
                gs = slice(g * 512, (g + 1) * 512)
                pX = pA if g % 2 == 0 else pB
                kX = KA if g % 2 == 0 else KB
                def cm(e, csl=csl, gs=gs, pX=pX):
                    return e.matmul(out=pX[:], lhsT=C_keep[:, csl], rhs=Pinb[:, gs],
                                    start=True, stop=True)
                pr.do("tensor", cm, reads=["C_keep", "Pinb"], writes=[kX])
                pr.do("scalar", lambda e, pX=pX: e.copy(out=yo2[:], in_=pX[:]),
                      reads=[kX], writes=["yo2"])
                def sc2(e, g=g):
                    return e.tensor_tensor(
                        out=yos2[:].rearrange("p (h q) -> p h q", h=8),
                        in0=yo2[:].rearrange("p (h q) -> p h q", h=8),
                        in1=ecs2t[:, g * 8:(g + 1) * 8].unsqueeze(2)
                        .broadcast_to((128, 8, 64)), op=OP.mult)
                pr.do("gpsimd", sc2, reads=["yo2", "ecs2t"], writes=["yos2"])
                def fin(e, gs=gs, cb=cb):
                    return e.tensor_tensor(out=ysl[cb][:, gs], in0=ymr[cb][:, gs],
                                           in1=yos2[:], op=OP.add)
                pr.do("vector", fin, reads=["ymr0", "yos2"], writes=["ysl0"])
            def ldsg(e, csl=csl):
                return e.dma_start(
                    out=sgr[:], in_=ctx["sgate_sp"][:, csl]
                    .rearrange("(j p) t -> p j t", p=128))
            pr.do("sync", ldsg, writes=["sgr"], dma=True)
            for half in range(4):
                for j8 in range(8):
                    j = half * 8 + j8
                    def tp(e, j=j, j8=j8, cb=cb):
                        return e.transpose(out=ptp[1][:, j8 * 128:(j8 + 1) * 128],
                                           in_=ysl[cb][:, j * 128:(j + 1) * 128]
                                           .rearrange("p q -> p q"),
                                           identity=consts["eye_bf"][:])
                    pr.do("tensor", tp, reads=["ysl0", "eye_bf"], writes=[KT[1]])
                def hg(e, half=half, csl=csl):
                    return e.tensor_tensor(
                        out=hgn[:, half * 8:(half + 1) * 8, csl],
                        in0=ptp[1][:].rearrange("p (j q) -> p j q", j=8),
                        in1=sgr[:, half * 8:(half + 1) * 8, :], op=OP.mult)
                pr.do("vector", hg, reads=[KT[1], "sgr"], writes=["hgn"])
            def sq(e, csl=csl):
                return e.activation(out=sqv[:], in_=hgn[:, :, csl], func=AF.Square)
            pr.do("scalar", sq, reads=["hgn"], writes=["sqv"])
            for j in range(32):
                def vm(e, j=j):
                    return e.matmul(out=pvar[:], lhsT=sqv[:, j, :],
                                    rhs=onescol_bf[:], start=(j == 0), stop=(j == 31))
                pr.do("tensor", vm, reads=["sqv", "onescol_bf"], writes=["P$xpvar"])
            def vr(e, c=c):
                return e.tensor_scalar(out=rstd_res[:, c:c + 1], in0=pvar[:],
                                       scalar1=1.0 / DIN, scalar2=None,
                                       op0=OP.mult, op1=OP.bypass)
            pr.do("vector", vr, reads=["P$xpvar"], writes=["rstd1"])
            def vs(e, c=c):
                return e.activation(out=rstd_res[:, c:c + 1], in_=rstd_res[:, c:c + 1],
                                    func=AF.Sqrt, bias=consts["epscol"][:])
            pr.do("scalar", vs, reads=["rstd1", "epscol"], writes=["rstd2"])
            def vc(e, c=c):
                return e.reciprocal(out=rstd_res[:, c:c + 1], in_=rstd_res[:, c:c + 1])
            pr.do("vector", vc, reads=["rstd2"], writes=["rstd_res"])

        pr.barrier()
        if ctx["stop_after"] == "pcorr":
            _finish_zero_out(ctx)
            return

        pr.barrier()
        CORR.close()
        wos = psl("wos", [128, 32, 512], BF16)
        nws = psl("nws", [128, 32])
        osl = [psl("xosl%d" % i, [128, 512]) for i in range(2)]
        pr.do("sync", lambda e: e.dma_start(out=nws[:], in_=ctx["norm_w"][:]),
              writes=["nws"], dma=True)
        for dg in range(4):
            def ldw(e, dg=dg):
                return e.dma_start(
                    out=wos[:], in_=ctx["w_outT"][:, dg * 512:(dg + 1) * 512]
                    .rearrange("(j p) c -> p j c", p=128))
            pr.do("gpsimd", ldw, writes=["wos"], dma=True)
            for j in range(32):
                def nsc(e, j=j):
                    return e.tensor_scalar(out=wos[:, j, :], in0=wos[:, j, :],
                                           scalar1=nws[:, j:j + 1], scalar2=None,
                                           op0=OP.mult, op1=OP.bypass)
                pr.do("vector", nsc, reads=["wos", "nws"], writes=["wos"])
            for t in range(16):
                pX = pC if t % 2 == 0 else pD
                kX = KC_ if t % 2 == 0 else KD
                for j in range(32):
                    def mm(e, j=j, t=t, pX=pX):
                        return e.matmul(out=pX[:], lhsT=hgn[:, j, t * 128:(t + 1) * 128],
                                        rhs=wos[:, j, :], start=(j == 0), stop=(j == 31))
                    pr.do("tensor", mm, reads=["hgn", "wos"], writes=[kX])
                b2 = t % 2
                def ev(e, t=t, pX=pX, b2=b2):
                    return e.tensor_scalar(out=osl[b2][:], in0=pX[:],
                                           scalar1=rstd_res[:, t // 1][:, t:t + 1]
                                           if False else rstd_res[:, t:t + 1],
                                           scalar2=None, op0=OP.mult, op1=OP.bypass)
                pr.do("vector", ev, reads=[kX, "rstd_res"], writes=["xosl%d" % b2])
                def ac(e, t=t, dg=dg, b2=b2):
                    return e.dma_start(
                        out=ctx["partial"][t * 128:(t + 1) * 128,
                                           dg * 512:(dg + 1) * 512],
                        in_=osl[b2][:], accum_op=OP.add)
                pr.do("gpsimd", ac, reads=["xosl%d" % b2], dma=True)
        pr.barrier()

    pr.barrier()
    ctx["keep_es"].close()
    if ctx["stop_after"] == "pattn":
        _finish_zero_out(ctx)
        return
    emit_experts_and_finale(ctx, x_lat)


def emit_experts_and_finale(ctx, x_lat):
    nc = ctx["nc"]; pr = ctx["pr"]; consts = ctx["consts"]; cwT = ctx["cwT"]

    cwrows = nc.dram_tensor("cwrows", [E, T], BF16, kind="Internal").ap()
    for i in range(4):
        def st(e, i=i):
            return e.dma_start(
                out=cwrows[i * 8:(i + 1) * 8, :],
                in_=cwT[i * 8:(i + 1) * 8, :])
        pr.do("scalar", st, reads=["cwT"], writes=["cwrows"], dma=True)

    with ExitStack() as PH:
        def psl(name, shape, dt=F32):
            return PH.enter_context(nc.sbuf_tensor(name, list(shape), dt))
        def pp(name, shape, dt=F32):
            return PH.enter_context(nc.psum_tensor(name, list(shape), dt))
        ph = [pp("eph%d" % i, [128, 512]) for i in range(2)]
        pz = pp("epz", [128, 2048])
        pbc = pp("epbc", [128, 512])
        KH = ["P$eph0", "P$eph1"]
        KZ = "P$epz"
        KBC = "P$epbc"

        x_lat = psl("x_lat9", [128, 8, T], BF16)
        for lt in range(8):
            def lx(e, lt=lt):
                return e.dma_start(out=x_lat[:, lt, :], in_=ctx["xlat_sp"][lt])
            pr.do("sync", lx, writes=["x_lat"], dma=True)
        w1b = [psl("w1b%d" % i, [128, 8, INTER], BF16) for i in range(2)]
        w2b = [psl("w2b%d" % i, [128, 8, LAT], BF16) for i in range(2)]
        hp = [psl("hp%d" % i, [128, 8, T], BF16) for i in range(2)]
        rl = psl("erl", [128, 512], BF16)
        bcw = [psl("ebcw0", [128, T], BF16)]
        bcw.append(bcw[0])
        zsl = psl("ezsl", [128, 2048])

        cwr_s = psl("cwr_s", [1, T], BF16)

        def load_w1(e_i, wb):
            for j4 in range(2):
                def lw1(e, e_i=e_i, wb=wb, j4=j4):
                    return e.dma_start(
                        out=w1b[wb][:, j4 * 4:(j4 + 1) * 4, :],
                        in_=ctx["w1T"][e_i].rearrange("(j p) i -> p j i", p=128)
                        [:, j4 * 4:(j4 + 1) * 4, :])
                pr.do("gpsimd", lw1, writes=["w1b%d" % wb], dma=True)

        def hprime(e_i, wb):
            def lcw(e, e_i=e_i):
                return e.dma_start(out=cwr_s[:],
                                   in_=cwrows[e_i:e_i + 1, :])
            pr.do("sync", lcw, reads=["cwrows"], writes=["cwr_s"], dma=True)
            for j4 in range(2):
                def lw2(e, e_i=e_i, wb=wb, j4=j4):
                    return e.dma_start(
                        out=w2b[wb][:, j4 * 4:(j4 + 1) * 4, :],
                        in_=ctx["w2T"][e_i].rearrange("(j p) l -> p j l", p=128)
                        [:, j4 * 4:(j4 + 1) * 4, :])
                pr.do("gpsimd", lw2, writes=["w2b%d" % wb], dma=True)
            for tg in range(4):
                def bm(e, e_i=e_i, tg=tg):
                    return e.matmul(out=pbc[:], lhsT=consts["ones1_bf"][:],
                                    rhs=cwr_s[:, tg * TG:(tg + 1) * TG],
                                    start=True, stop=True)
                pr.do("tensor", bm, reads=["ones1bf", "cwr_s"], writes=[KBC])
                def bev(e, tg=tg, wb=wb):
                    return e.copy(out=bcw[wb][:, tg * TG:(tg + 1) * TG], in_=pbc[:])
                pr.do("scalar", bev, reads=[KBC], writes=["ebcw0"])
            hb = hp[wb]
            for it in range(8):
                for tg in range(4):
                    px = ph[(it * 4 + tg) % 2]; kx = KH[(it * 4 + tg) % 2]
                    for k in range(8):
                        def mm(e, k=k, it=it, tg=tg, px=px, wb=wb):
                            return e.matmul(
                                out=px[:], lhsT=w1b[wb][:, k, it * 128:(it + 1) * 128],
                                rhs=x_lat[:, k, tg * TG:(tg + 1) * TG],
                                start=(k == 0), stop=(k == 7))
                        pr.do("tensor", mm, reads=["w1b%d" % wb, "x_lat"], writes=[kx])
                    def rel(e, px=px):
                        return e.activation(out=rl[:], in_=px[:], func=AF.Relu)
                    pr.do("scalar", rel, reads=[kx], writes=["erl"])
                    def t1(e, it=it, tg=tg, wb=wb):
                        return e.tensor_tensor(
                            out=hb[:, it, tg * TG:(tg + 1) * TG], in0=rl[:],
                            in1=bcw[wb][:, tg * TG:(tg + 1) * TG], op=OP.mult)
                    pr.do("vector", t1, reads=["erl", "ebcw0"], writes=["ehp%d" % wb])
                    def t2(e, it=it, tg=tg, wb=wb):
                        return e.tensor_tensor(
                            out=hb[:, it, tg * TG:(tg + 1) * TG],
                            in0=hb[:, it, tg * TG:(tg + 1) * TG],
                            in1=rl[:], op=OP.mult)
                    pr.do("vector", t2, reads=["ehp%d" % wb, "erl"], writes=["ehp%d" % wb])

        load_w1(0, 0)
        load_w1(1, 1)
        for ep in range(E // 2):
            hprime(2 * ep, 0)
            hprime(2 * ep + 1, 1)
            if ep + 1 < E // 2:
                load_w1(2 * ep + 2, 0)
                load_w1(2 * ep + 3, 1)
            for lt in range(8):
                for e2 in range(2):
                    for tg in range(4):
                        for k in range(8):
                            def mm(e, k=k, lt=lt, tg=tg, e2=e2):
                                return e.matmul(
                                    out=pz[:, tg * TG:(tg + 1) * TG],
                                    lhsT=w2b[e2][:, k, lt * 128:(lt + 1) * 128],
                                    rhs=hp[e2][:, k, tg * TG:(tg + 1) * TG],
                                    start=(e2 == 0 and k == 0),
                                    stop=(e2 == 1 and k == 7))
                            pr.do("tensor", mm, reads=["w2b%d" % e2, "ehp%d" % e2],
                                  writes=[KZ])
                def ev(e, lt=lt):
                    return e.tensor_copy(out=zsl[:], in_=pz[:])
                pr.do("vector", ev, reads=[KZ], writes=["ezsl"])
                def ac(e, lt=lt, ep=ep):
                    return e.dma_start(out=ctx["ylat_sp"][lt], in_=zsl[:],
                                       accum_op=(OP.add if ep > 0 else OP.bypass))
                pr.do("gpsimd", ac, reads=["ezsl"], dma=True)
        pr.barrier()
    pr.barrier()
    if ctx["stop_after"] == "pexp":
        _finish_zero_out(ctx)
        return

    with ExitStack() as PH:
        def psl(name, shape, dt=F32):
            return PH.enter_context(nc.sbuf_tensor(name, list(shape), dt))
        def pp(name, shape, dt=F32):
            return PH.enter_context(nc.psum_tensor(name, list(shape), dt))
        ylb = psl("ylb", [128, 8, T], BF16)
        fc2w = psl("fc2w", [128, 8, D], BF16)
        prd = [psl("prd%d" % i, [128, 512]) for i in range(2)]
        out_s = [psl("outs%d" % i, [128, 512]) for i in range(2)]
        pf = [pp("fpf%d" % i, [128, 512]) for i in range(2)]
        KF2 = ["P$fpf0", "P$fpf1"]
        for lt in range(8):
            def ly(e, lt=lt):
                return e.dma_start(out=ylb[:, lt, :], in_=ctx["ylat_sp"][lt])
            pr.do("gpsimd", ly, writes=["ylb"], dma=True)
        for j4 in range(2):
            def lw(e, j4=j4):
                return e.dma_start(
                    out=fc2w[:, j4 * 4:(j4 + 1) * 4, :],
                    in_=ctx["fc2T"].rearrange("(j p) d -> p j d", p=128)
                    [:, j4 * 4:(j4 + 1) * 4, :])
            pr.do("gpsimd", lw, writes=["fc2w"], dma=True)
        for t in range(16):
            for dg in range(4):
                ib = (t * 4 + dg) % 2
                px = pf[ib]; kx = KF2[ib]
                for k in range(8):
                    def mm(e, k=k, t=t, dg=dg, px=px):
                        return e.matmul(out=px[:], lhsT=ylb[:, k, t * 128:(t + 1) * 128],
                                        rhs=fc2w[:, k, dg * 512:(dg + 1) * 512],
                                        start=(k == 0), stop=(k == 7))
                    pr.do("tensor", mm, reads=["ylb", "fc2w"], writes=[kx])
                def lp(e, t=t, dg=dg, ib=ib):
                    return e.dma_start(
                        out=prd[ib][:],
                        in_=ctx["partial"][t * 128:(t + 1) * 128,
                                           dg * 512:(dg + 1) * 512])
                pr.do("sync", lp, writes=["prd%d" % ib], dma=True)
                def fin(e, ib=ib, px=px):
                    return e.tensor_tensor(out=out_s[ib][:], in0=px[:],
                                           in1=prd[ib][:], op=OP.add)
                pr.do("vector", fin, reads=[kx, "prd%d" % ib], writes=["outs%d" % ib])
                def sto(e, t=t, dg=dg, ib=ib):
                    return e.dma_start(
                        out=ctx["y_out"][t * 128:(t + 1) * 128,
                                         dg * 512:(dg + 1) * 512],
                        in_=out_s[ib][:])
                pr.do("scalar", sto, reads=["outs%d" % ib], dma=True)
